# version 03
# baseline (speedup 1.0000x reference)
import numpy as np
import ml_dtypes
from contextlib import ExitStack
import concourse.bass as bass
import concourse.mybir as mybir
from concourse.bass_utils import run_bass_kernel_spmd

F32 = mybir.dt.float32
BF16 = mybir.dt.bfloat16
U8 = mybir.dt.uint8
AF = mybir.ActivationFunctionType
ALU = mybir.AluOpType
BF = ml_dtypes.bfloat16

DM = 1024
DI = 1024
NT = 4096
TC = 512
NCH = NT // TC
S = 16
QS = 4
NQ = S // QS
KT = DM // 128
IT = DI // 128
HTOK = TC // 2
LNEPS = 1e-5
GROUPS = [[0, 1], [2, 3], [4, 5], [6, 7]]

_NC_CACHE = {}


class Plan:
    ENGS = ("sync", "gpsimd", "scalar", "vector", "tensor")

    def __init__(self):
        self.items = {e: [] for e in self.ENGS}
        self.sems = {}
        self.waited = {e: {} for e in self.ENGS}

    def add_sem(self, name, handle):
        self.sems[name] = [handle, 0]

    def ensure(self, name, nc, stack):
        if name not in self.sems:
            self.add_sem(name, stack.enter_context(nc.semaphore(name)))

    def step(self, eng, fn, waits=(), sem=None, k=1):
        ws = []
        for w in waits:
            if w is None:
                continue
            sn, val = w
            if val <= 0:
                continue
            if val > self.waited[eng].get(sn, 0):
                self.waited[eng][sn] = val
                ws.append((self.sems[sn][0], val))
        ev = None
        if sem is not None:
            self.sems[sem][1] += k
            ev = (sem, self.sems[sem][1])
        self.items[eng].append((fn, tuple(ws), None if sem is None else self.sems[sem][0], k))
        return ev

    def run(self, eng_name, eng):
        for fn, ws, semh, k in self.items[eng_name]:
            for (h, v) in ws:
                eng.wait_ge(h, v)
            ins = fn(eng)
            if semh is not None:
                ins.then_inc(semh, k)


def build_nc():
    nc = bass.Bass("TRN2", target_bir_lowering=False, detect_race_conditions=False)

    def inp(name, shape, dt):
        return nc.dram_tensor(name, list(shape), dt, kind="ExternalInput").ap()

    xT = inp("xT", (DM, NT), F32)
    wbr = inp("wbr", (DM, DI), F32)
    wz = inp("wz", (DM, DI), F32)
    wx = inp("wx", (DI, 33), F32)
    convw = inp("convw", (DI, 4), F32)
    convb = inp("convb", (DI, 1), F32)
    alog = inp("alog", (DI, S), F32)
    dpar = inp("dpar", (DI, 1), F32)
    dtb = inp("dtb", (DI, 1), F32)
    wout = inp("wout", (DI, DM), F32)
    k0t = inp("k0t", (DM, 4096), U8)
    k0b = inp("k0b", (DM, 4096), U8)
    k1u = inp("k1u", (4096, 4096), U8)
    k2u = inp("k2u", (4096, DM), U8)
    s0i = inp("s0i", (1, 1), F32)
    s1i = inp("s1i", (1, 1), F32)
    s2i = inp("s2i", (1, 1), F32)
    m_e = inp("m_e", (128, 1), F32)
    minv_e = inp("minv_e", (128, 1), F32)
    m_o = inp("m_o", (128, 1), F32)
    minv_o = inp("minv_o", (128, 1), F32)
    ones_bf = inp("ones_bf", (128, 1), BF16)
    neghalf = inp("neghalf", (1, 128), BF16)

    outT = nc.dram_tensor("outT", [DM, NT // 2], F32, kind="ExternalOutput").ap()

    def ihb(name, shape, dt):
        return nc.dram_tensor(name, list(shape), dt, kind="Internal").ap()

    w0e = ihb("w0e", (KT, 128, 4096), BF16)
    k1b = ihb("k1b", (4096, 4096), BF16)
    k2b = ihb("k2b", (4096, DM), BF16)
    wbr_b = ihb("wbr_b", (DM, DI), BF16)
    wz_b = ihb("wz_b", (DM, DI), BF16)
    wout_b = ihb("wout_b", (DI, DM), BF16)
    ssm_i = ihb("ssm_i", (NCH, 33, TC), BF16)
    ssm_o = ihb("ssm_o", (NCH, 33, TC), BF16)
    wo_i = ihb("wo_i", (NCH, DM, TC), BF16)
    wo_o = ihb("wo_o", (NCH, DM, TC), BF16)
    lnr = ihb("lnr", (4 * NCH, 2, TC), F32)

    st = ExitStack()
    sb = lambda n, shp, dt: st.enter_context(nc.sbuf_tensor(n, list(shp), dt))
    ps = lambda n: st.enter_context(nc.psum_tensor(n, [128, 512], F32))

    with st:
        wwin = sb("wwin", (128, KT * DI), BF16)
        wx_s = sb("wx_s", (128, IT * 33), BF16)
        acol = sb("acol", (128, IT * S), F32)
        cwcol = sb("cwcol", (128, IT * 4), F32)
        cbcol = sb("cbcol", (128, IT), F32)
        dtbcol = sb("dtbcol", (128, IT), F32)
        dcol = sb("dcol", (128, IT), F32)
        scols = sb("scols", (128, 4), F32)
        onec = sb("onec", (128, 1), BF16)
        nhrow = sb("nhrow", (1, 128), BF16)
        mcols = sb("mcols", (128, 4), F32)
        epsc = sb("epsc", (128, 1), F32)
        onef = sb("onef", (128, 1), F32)

        ln1x = sb("ln1x", (128, KT * TC), BF16)
        xcp = sb("xcp", (128, IT * TC), BF16)
        szp = sb("szp", (128, IT * TC), BF16)
        xbrg = sb("xbrg", (128, IT * (TC + 8)), BF16)
        dtp = sb("dtp", (128, IT * TC), BF16)
        wpl = sb("wpl", (128, IT * TC), BF16)
        yp = sb("yp", (128, IT * TC), BF16)
        x1b = sb("x1b", (128, KT * TC), BF16)
        pool_a = sb("pool_a", (128, KT * TC), BF16)
        pool_b = sb("pool_b", (128, KT * TC), BF16)
        pool_c = sb("pool_c", (128, KT * TC), BF16)

        u_q = sb("u_q", (128, QS * TC), BF16)
        dA_q = sb("dA_q", (128, 2 * QS * TC), BF16)
        h_q = sb("h_q", (128, 2 * QS * TC), BF16)
        hcar = sb("hcar", (128, IT * S), F32)
        bqt = sb("bqt", (128, 2 * QS * TC), BF16)
        cqt = sb("cqt", (128, 2 * QS * TC), BF16)

        rows = sb("rows", (1, 3 * TC), F32)
        abc_a = sb("abc_a", (128, TC), BF16)
        abc_b = sb("abc_b", (128, TC), BF16)
        ssm_s = sb("ssm_s", (33, TC), BF16)
        cbc = sb("cbc", (128, TC), BF16)

        f1 = sb("f1", (128, 32 * HTOK), BF16)
        f2h = sb("f2h", (128, 16 * HTOK), BF16)
        ln3 = sb("ln3", (128, KT * TC), BF16)
        f3 = sb("f3", (128, KT * HTOK), BF16)
        kwin = sb("kwin", (128, 2 * 2048), BF16)
        rsf = sb("rsf", (1, HTOK), F32)
        rshl = sb("rshl", (1, 4 * HTOK), BF16)
        x2t = sb("x2t", (128, 2 * HTOK), F32)

        p0, p1, p2, p3, p4, p5, p6, p7 = (ps(f"p{i}") for i in range(8))

        P = Plan()
        _orig_step = P.step
        def _step(eng, fn, waits=(), sem=None, k=1):
            if sem is not None:
                P.ensure(sem, nc, st)
            return _orig_step(eng, fn, waits, sem, k)
        P.step = _step
        for sname in ("sV", "sS", "sT"):
            P.ensure(sname, nc, st)

        def blk(t, i, w):
            return t[:, i * w:(i + 1) * w]

        ev_w = {}
        P.step("gpsimd", lambda e: e.dma_start(out=wbr_b, in_=wbr), sem="sWI", k=16)
        P.step("gpsimd", lambda e: e.dma_start(out=wz_b, in_=wz), sem="sWI", k=16)
        P.step("gpsimd", lambda e: e.dma_start(out=wout_b, in_=wout), sem="sWI", k=16)
        evwi = P.step("gpsimd", lambda e: e.dma_start(
            out=wx_s[:].rearrange("p (kt j) -> p kt j", kt=IT),
            in_=wx.rearrange("(kt p) j -> p kt j", p=128)), sem="sWI", k=16)
        ev_w["wbr"] = ev_w["wz"] = ev_w["wout"] = ev_w["wx"] = evwi

        def ld_cols(dst, src, n):
            return lambda e: e.dma_start(
                out=dst.rearrange("p (kt j) -> p kt j", kt=IT),
                in_=src.rearrange("(kt p) j -> p kt j", p=128),
                allow_slow_non_contiguous=True)

        P.step("sync", ld_cols(acol[:], alog, S), sem="sSM", k=16)
        P.step("sync", ld_cols(cwcol[:], convw, 4), sem="sSM", k=16)
        P.step("sync", ld_cols(cbcol[:], convb, 1), sem="sSM", k=16)
        P.step("sync", ld_cols(dtbcol[:], dtb, 1), sem="sSM", k=16)
        P.step("sync", ld_cols(dcol[:], dpar, 1), sem="sSM", k=16)
        P.step("sync", lambda e: e.dma_start(out=onec[:], in_=ones_bf), sem="sSM", k=16)
        P.step("sync", lambda e: e.dma_start(out=nhrow[:], in_=neghalf), sem="sSM", k=16)
        P.step("sync", lambda e: e.dma_start(out=mcols[:, 0:1], in_=m_e), sem="sSM", k=16)
        P.step("sync", lambda e: e.dma_start(out=mcols[:, 1:2], in_=minv_e), sem="sSM", k=16)
        P.step("sync", lambda e: e.dma_start(out=mcols[:, 2:3], in_=m_o), sem="sSM", k=16)
        ev_small = P.step("sync", lambda e: e.dma_start(out=mcols[:, 3:4], in_=minv_o), sem="sSM", k=16)
        ev_alog = ev_small

        P.step("gpsimd", lambda e: e.dma_start(
            out=scols[:, 0:1], in_=s0i[0:1, :].squeeze(0).partition_broadcast(128)), sem="sSC", k=16)
        P.step("gpsimd", lambda e: e.dma_start(
            out=scols[:, 1:2], in_=s1i[0:1, :].squeeze(0).partition_broadcast(128)), sem="sSC", k=16)
        ev_s2x = P.step("gpsimd", lambda e: e.dma_start(
            out=scols[:, 2:3], in_=s2i[0:1, :].squeeze(0).partition_broadcast(128)), sem="sSC", k=16)
        ev_s0 = ev_s1 = ev_s2 = ev_s2x
        P.step("vector", lambda e: e.tensor_scalar_mul(scols[:, 0:1], scols[:, 0:1], 2.0),
               waits=[ev_s0], sem="sV")
        P.step("vector", lambda e: e.tensor_scalar_mul(scols[:, 1:2], scols[:, 1:2], 4.0),
               waits=[ev_s1], sem="sV")
        ev_scl = P.step("vector", lambda e: e.tensor_scalar_mul(scols[:, 2:3], scols[:, 2:3], 4.0),
                        waits=[ev_s2], sem="sV")
        ev_ax = P.step("scalar", lambda e: e.activation(out=acol[:], in_=acol[:], func=AF.Exp),
                       waits=[ev_alog], sem="sS")
        ev_acol = P.step("vector", lambda e: e.tensor_scalar_mul(acol[:], acol[:], -1.0),
                         waits=[ev_ax], sem="sV")
        for i in range(IT):
            off = i * (TC + 8)
            P.step("vector", (lambda off=off: (lambda e: e.memset(xbrg[:, off:off + 8], 0.0)))(), sem="sV")
        P.step("vector", lambda e: e.memset(x1b[:], 0.0), sem="sV")
        ev_zero = P.step("vector", lambda e: e.memset(hcar[:], 0.0), sem="sV")
        ev_eps = P.step("vector", lambda e: e.memset(epsc[:], LNEPS), sem="sV")
        ev_onef = P.step("vector", lambda e: e.memset(onef[:], 1.0), sem="sV")

        ev_k1b = ev_k2b = None
        for q in range(4):
            ev_k1b = P.step("gpsimd", (lambda q=q: (lambda e: e.dma_start(
                out=k1b[q * 1024:(q + 1) * 1024, :], in_=k1u[q * 1024:(q + 1) * 1024, :])))(),
                sem="sK1", k=16)
        for q in range(2):
            ev_k2b = P.step("gpsimd", (lambda q=q: (lambda e: e.dma_start(
                out=k2b[q * 2048:(q + 1) * 2048, :], in_=k2u[q * 2048:(q + 1) * 2048, :])))(),
                sem="sK2", k=16)
        f1u8 = f1[:].bitcast(U8)
        prev_sub = None
        prev_store = None
        ev_stage_done = None
        for kt in range(KT):
            wl = [prev_sub] if prev_sub else []
            eva = P.step("sync", (lambda kt=kt: (lambda e: e.dma_start(
                out=f1u8[:, 0:4096], in_=k0t[kt * 128:(kt + 1) * 128, :])))(),
                waits=wl, sem="sK0A", k=16)
            evb = P.step("sync", (lambda kt=kt: (lambda e: e.dma_start(
                out=f1u8[:, 4096:8192], in_=k0b[kt * 128:(kt + 1) * 128, :])))(),
                waits=wl, sem="sK0B", k=16)
            evs = P.step("vector", lambda e: e.tensor_tensor(
                f1[:, 4096:8192], f1u8[:, 0:4096], f1u8[:, 4096:8192], ALU.subtract),
                waits=[eva, evb] + ([prev_store] if prev_store else []), sem="sV")
            evd = P.step("gpsimd", (lambda kt=kt: (lambda e: e.dma_start(
                out=w0e[kt], in_=f1[:, 4096:8192])))(),
                waits=[evs], sem="sSTG", k=16)
            prev_sub, prev_store = evs, evd
            ev_stage_done = evd
        ev_f1_free = ev_stage_done

        t1_evac_guard = {0: None, 1: None}
        wwin_guard = None
        t3_guard = {0: None, 1: None}
        p2_guard = None
        p34_guard = None
        ln_dma_guard = None
        abc_guard = None
        prev_halo = [None] * IT
        halo_src_guard = [None] * IT
        prev_g = [None] * IT
        prev_carry = {}
        prev_scan = {}
        hq_user = {}
        dA_user = {}
        uq_user = None
        bq_user = {}
        cq_user = {}
        ev_dt = [None] * IT
        ev_wpl = [None] * IT
        ev_ssm_out_prev = None
        t3_uq_guard = {0: None, 1: None, 2: None, 3: None}
        ev_dt_prev_last = None
        chunk_done = None
        ev_ffn_slotbuf = None
        f2h_guard = None
        x2t_guard = None
        rsf_guard = None
        p7_guard = None
        kwin_user = {0: None, 1: None}
        ev_stats_ready = None

        def mm_stats(dst, getsrc, kt, first, last, N=TC):
            return lambda e: e.matmul(dst[0:1, 0:N], onec[:, 0:1], getsrc(),
                                      start=first, stop=last)

        def ln_rows(idx, w):
            nonlocal ln_dma_guard, p34_guard, abc_guard
            r_mu = rows[0:1, 0:w]
            r_v = rows[0:1, TC:TC + w]
            r_a = rows[0:1, 2 * TC:2 * TC + w]
            r_b = r_v
            inv = 1.0 / DM
            wl = [("sT", P.sems["sT"][1])]
            if ln_dma_guard:
                wl.append(ln_dma_guard)
            e1 = P.step("vector", lambda e: e.tensor_scalar_mul(r_mu, p3[0:1, 0:w], inv),
                        waits=wl, sem="sV")
            e2 = P.step("vector", lambda e: e.tensor_scalar_mul(r_v, p4[0:1, 0:w], inv),
                        sem="sV")
            P.step("vector", lambda e: e.tensor_tensor(r_a, r_mu, r_mu, ALU.mult), sem="sV")
            ev_v = P.step("vector", lambda e: e.tensor_tensor(r_v, r_v, r_a, ALU.subtract), sem="sV")
            P.step("scalar", lambda e: e.activation(out=r_v, in_=r_v, func=AF.Ln, bias=epsc[0:1, 0:1]),
                   waits=[ev_v, ev_eps], sem="sS")
            es = P.step("scalar", lambda e: e.activation(out=r_a, in_=r_v, func=AF.Exp, scale=-0.5),
                        sem="sS")
            ea = es
            eb = P.step("vector", lambda e: e.tensor_tensor(r_b, r_mu, r_a, ALU.mult),
                        waits=[es], sem="sV")
            ed0 = P.step("sync", lambda e: e.dma_start(out=lnr[idx, 0:1, 0:w], in_=r_b),
                         waits=[ea, eb], sem="sRWB", k=16)
            ed1 = P.step("sync", lambda e: e.dma_start(out=lnr[idx, 1:2, 0:w], in_=r_a),
                         waits=[ea], sem="sRWA", k=16)
            gl = [abc_guard] if abc_guard else []
            eba = P.step("gpsimd", lambda e: e.dma_start(
                out=abc_a[:, 0:w], in_=lnr[idx, 1:2, 0:w].squeeze(0).partition_broadcast(128)),
                waits=[ed1] + gl, sem="sABA", k=16)
            ebb = P.step("gpsimd", lambda e: e.dma_start(
                out=abc_b[:, 0:w], in_=lnr[idx, 0:1, 0:w].squeeze(0).partition_broadcast(128)),
                waits=[ed0] + gl, sem="sABB", k=16)
            ln_dma_guard = ed0
            p34_guard = e2 if e2 else e1
            return eba, ebb

        ev_out_final = None

        for c in range(NCH):
            kslot = c // 2
            xsrc = xT[:, c * TC:(c + 1) * TC].rearrange("(kt p) t -> p kt t", p=128)

            ev_xl = P.step("gpsimd", (lambda xsrc=xsrc: (lambda e: e.dma_start(
                out=pool_a[:].rearrange("p (kt t) -> p kt t", kt=KT), in_=xsrc)))(),
                waits=[chunk_done], sem="sXL", k=16)
            ev_sq = []
            for kt in range(KT):
                ev_sq.append(P.step("scalar", (lambda kt=kt: (lambda e: e.activation(
                    out=blk(pool_b, kt, TC), in_=blk(pool_a, kt, TC), func=AF.Square)))(),
                    waits=[ev_xl], sem="sS"))
            evA = evB = None
            for kt in range(KT):
                evA = P.step("tensor", mm_stats(p3, (lambda kt=kt: lambda: blk(pool_a, kt, TC))(), kt, kt == 0, kt == KT - 1),
                             waits=([ev_xl, ev_small, p34_guard] if kt == 0 else []),
                             sem="sT" if kt == KT - 1 else None)
            for kt in range(KT):
                evB = P.step("tensor", mm_stats(p4, (lambda kt=kt: lambda: blk(pool_b, kt, TC))(), kt, kt == 0, kt == KT - 1),
                             waits=[ev_sq[kt]], sem="sT" if kt == KT - 1 else None)
            eba, ebb = ln_rows(c, TC)
            ev_ln1 = None
            for kt in range(KT):
                P.step("vector", (lambda kt=kt: (lambda e: e.tensor_tensor(
                    blk(ln1x, kt, TC), blk(pool_a, kt, TC), abc_a[:, 0:TC], ALU.mult)))(),
                    waits=[eba], sem="sV")
                ev_ln1 = P.step("vector", (lambda kt=kt: (lambda e: e.tensor_tensor(
                    blk(ln1x, kt, TC), blk(ln1x, kt, TC), abc_b[:, 0:TC], ALU.subtract)))(),
                    waits=[ebb], sem="sV")
            abc_guard = ev_ln1

            ev_t1 = []
            for part in range(2):
                src_hbm = wz_b if part else wbr_b
                evl = P.step("sync", (lambda src_hbm=src_hbm: (lambda e: e.dma_start(
                    out=wwin[:].rearrange("p (kt j) -> p kt j", kt=KT),
                    in_=src_hbm.rearrange("(kt p) j -> p kt j", p=128))))(),
                    waits=[ev_w["wz" if part else "wbr"], wwin_guard], sem="sWW", k=16)
                for j in range(IT):
                    jj = part * IT + j
                    bank = [p0, p1][jj % 2]
                    w0l = [ev_ln1, evl, t1_evac_guard[jj % 2]]
                    evm = None
                    for kt in range(KT):
                        evm = P.step("tensor", (lambda bank=bank, kt=kt, j=j: (lambda e: e.matmul(
                            bank[:, 0:TC], wwin[:, kt * DI + j * 128:kt * DI + (j + 1) * 128],
                            blk(ln1x, kt, TC), start=(kt == 0), stop=(kt == KT - 1))))(),
                            waits=(w0l if kt == 0 else []), sem="sT" if kt == KT - 1 else None)
                    if part:
                        eve = P.step("scalar", (lambda bank=bank, j=j: (lambda e: e.activation(
                            out=blk(szp, j, TC), in_=bank[:, 0:TC], func=AF.Silu)))(),
                            waits=[evm], sem="sS")
                    else:
                        off = j * (TC + 8) + 3
                        eve = P.step("scalar", (lambda bank=bank, off=off: (lambda e: e.copy(
                            out=xbrg[:, off:off + TC], in_=bank[:, 0:TC])))(),
                            waits=[evm, prev_halo[j], prev_g[j]], sem="sS")
                    t1_evac_guard[jj % 2] = eve
                    ev_t1.append(eve)
                    wwin_guard = evm

            ev_conv = []
            for i in range(IT):
                off = i * (TC + 8)
                e0 = P.step("vector", (lambda i=i, off=off: (lambda e: e.tensor_scalar_mul(
                    blk(xcp, i, TC), xbrg[:, off:off + TC], cwcol[:, i * 4:i * 4 + 1])))(),
                    waits=[ev_t1[i], (prev_halo[i] if c > 0 else ev_zero)], sem="sV")
                for tau in range(1, 4):
                    e0 = P.step("vector", (lambda i=i, off=off, tau=tau: (lambda e: e.scalar_tensor_tensor(
                        blk(xcp, i, TC), xbrg[:, off + tau:off + tau + TC],
                        cwcol[:, i * 4 + tau:i * 4 + tau + 1],
                        blk(xcp, i, TC), ALU.mult, ALU.add)))(), sem="sV")
                halo_src_guard[i] = e0
                evs = P.step("scalar", (lambda i=i: (lambda e: e.activation(
                    out=blk(xcp, i, TC), in_=blk(xcp, i, TC), func=AF.Silu,
                    bias=cbcol[:, i:i + 1])))(), waits=[e0], sem="sS")
                ev_conv.append(evs)
                evh = P.step("scalar", (lambda off=off: (lambda e: e.copy(
                    out=xbrg[:, off:off + 3], in_=xbrg[:, off + TC:off + TC + 3])))(),
                    waits=[e0], sem="sS")
                prev_halo[i] = evh

            evm = None
            for kt in range(IT):
                wl = [ev_conv[kt], ev_w["wx"]]
                if kt == 0 and p2_guard:
                    wl.append(p2_guard)
                evm = P.step("tensor", (lambda kt=kt: (lambda e: e.matmul(
                    p2[0:33, 0:TC], wx_s[:, kt * 33:(kt + 1) * 33], blk(xcp, kt, TC),
                    start=(kt == 0), stop=(kt == IT - 1))))(),
                    waits=wl, sem="sT" if kt == IT - 1 else None)
            ev_sevac = P.step("vector", lambda e: e.tensor_copy(ssm_s[:, 0:TC], p2[0:33, 0:TC]),
                              waits=[evm, ev_ssm_out_prev], sem="sV")
            p2_guard = ev_sevac
            ev_sout = P.step("gpsimd", (lambda c=c: (lambda e: e.dma_start(
                out=ssm_i[c], in_=ssm_s[:, 0:TC])))(), waits=[ev_sevac], sem="sSSM", k=16)
            ev_ssm_out_prev = ev_sout
            ev_ar1 = P.step("gpsimd", (lambda c=c: (lambda e: e.collective_compute(
                "AllReduce", ALU.add, replica_groups=GROUPS,
                ins=[ssm_i[c]], outs=[ssm_o[c]])))(), waits=[ev_sout], sem="sCCS")
            ev_cbc = P.step("gpsimd", (lambda c=c: (lambda e: e.dma_start(
                out=cbc[:, 0:TC], in_=ssm_o[c][32:33, :].squeeze(0).partition_broadcast(128))))(),
                waits=[ev_ar1, ev_dt_prev_last], sem="sCB", k=16)

            for i in range(IT):
                P.step("scalar", (lambda i=i: (lambda e: e.activation(
                    out=blk(dtp, i, TC), in_=cbc[:, 0:TC], func=AF.Identity,
                    bias=dtbcol[:, i:i + 1])))(),
                    waits=[ev_cbc, ev_wpl[i], prev_g[i]], sem="sS")
                P.step("scalar", (lambda i=i: (lambda e: e.activation(
                    out=blk(yp, i, TC), in_=blk(dtp, i, TC), func=AF.Abs)))(), sem="sS")
                P.step("scalar", (lambda i=i: (lambda e: e.activation(
                    out=blk(yp, i, TC), in_=blk(yp, i, TC), func=AF.Exp, scale=-1.0)))(), sem="sS")
                P.step("scalar", (lambda i=i: (lambda e: e.activation(
                    out=blk(yp, i, TC), in_=blk(yp, i, TC), func=AF.Ln,
                    bias=onef[:, 0:1])))(), waits=[ev_onef], sem="sS")
                ev_relu = P.step("scalar", (lambda i=i: (lambda e: e.activation(
                    out=blk(dtp, i, TC), in_=blk(dtp, i, TC), func=AF.Relu)))(), sem="sS")
                ev_dt[i] = P.step("vector", (lambda i=i: (lambda e: e.tensor_tensor(
                    blk(dtp, i, TC), blk(dtp, i, TC), blk(yp, i, TC), ALU.add)))(),
                    waits=[ev_relu], sem="sV")
                ev_wpl[i] = P.step("vector", (lambda i=i: (lambda e: e.tensor_tensor(
                    blk(wpl, i, TC), blk(dtp, i, TC), blk(xcp, i, TC), ALU.mult)))(),
                    waits=[ev_conv[i]], sem="sV")
            ev_dt_prev_last = ev_dt[IT - 1]

            ev_y = [None] * IT
            for qg in range(NQ):
                qpar = qg % 2
                evb = P.step("gpsimd", (lambda c=c, qg=qg, qpar=qpar: (lambda e: e.dma_start(
                    out=bqt[:, qpar * QS * TC:(qpar + 1) * QS * TC].rearrange("p (s t) -> p s t", s=QS),
                    in_=ssm_o[c][qg * QS:(qg + 1) * QS, :].partition_broadcast(128))))(),
                    waits=[ev_ar1, bq_user.get(qpar)], sem="sBQ%d" % qpar, k=16)
                evcq = P.step("gpsimd", (lambda c=c, qg=qg, qpar=qpar: (lambda e: e.dma_start(
                    out=cqt[:, qpar * QS * TC:(qpar + 1) * QS * TC].rearrange("p (s t) -> p s t", s=QS),
                    in_=ssm_o[c][S + qg * QS:S + (qg + 1) * QS, :].partition_broadcast(128))))(),
                    waits=[ev_ar1, cq_user.get(qpar)], sem="sCQ%d" % qpar, k=16)
                last_u = None
                for i in range(IT):
                    it_par = (qg * IT + i) % 2
                    hbase = it_par * QS * TC
                    ev_exps = []
                    for s4 in range(QS):
                        sidx = qg * QS + s4
                        ev_exps.append(P.step("scalar",
                            (lambda i=i, s4=s4, sidx=sidx, it_par=it_par: (lambda e: e.activation(
                                out=dA_q[:, it_par * QS * TC + s4 * TC:it_par * QS * TC + (s4 + 1) * TC],
                                in_=blk(dtp, i, TC), func=AF.Exp,
                                scale=acol[:, i * S + sidx:i * S + sidx + 1])))(),
                            waits=[ev_dt[i], ev_acol, dA_user.get(it_par)], sem="sS"))
                    wrep = blk(wpl, i, TC).unsqueeze(1).broadcast_to((128, QS, TC))
                    evu = P.step("vector", (lambda qpar=qpar, wrep=wrep: (lambda e: e.tensor_tensor(
                        u_q[:].rearrange("p (s t) -> p s t", s=QS), wrep,
                        bqt[:, qpar * QS * TC:(qpar + 1) * QS * TC].rearrange("p (s t) -> p s t", s=QS),
                        ALU.mult)))(),
                        waits=[ev_wpl[i], evb, uq_user, t3_uq_guard[0], t3_uq_guard[1], t3_uq_guard[2], t3_uq_guard[3]], sem="sV")
                    ev_sc = None
                    for s4 in range(QS):
                        sidx = qg * QS + s4
                        ini = 0.0 if c == 0 else hcar[:, i * S + sidx:i * S + sidx + 1]
                        wl = [evu, ev_exps[s4], hq_user.get(it_par), prev_carry.get(("hp", it_par))]
                        if c > 0:
                            wl.append(prev_carry.get((i, qg)))
                        ev_sc = P.step("vector",
                            (lambda s4=s4, hbase=hbase, ini=ini: (lambda e: e.tensor_tensor_scan(
                                h_q[:, hbase + s4 * TC:hbase + (s4 + 1) * TC],
                                dA_q[:, hbase + s4 * TC:hbase + (s4 + 1) * TC],
                                u_q[:, s4 * TC:(s4 + 1) * TC], ini, ALU.mult, ALU.add)))(),
                            waits=wl, sem="sV")
                    prev_scan[(i, qg)] = ev_sc
                    dA_user[it_par] = ev_sc
                    evcar = P.step("scalar", (lambda i=i, qg=qg, hbase=hbase: (lambda e: e.copy(
                        out=hcar[:, i * S + qg * QS:i * S + (qg + 1) * QS],
                        in_=h_q[:, hbase:hbase + QS * TC].rearrange("p (s t) -> p s t", s=QS)[:, :, TC - 1])))(),
                        waits=[ev_sc], sem="sS")
                    prev_carry[(i, qg)] = evcar
                    prev_carry[("hp", it_par)] = evcar
                    evm1 = P.step("vector", (lambda qpar=qpar, hbase=hbase: (lambda e: e.tensor_tensor(
                        u_q[:], h_q[:, hbase:hbase + QS * TC],
                        cqt[:, qpar * QS * TC:(qpar + 1) * QS * TC], ALU.mult)))(),
                        waits=[evcq, ev_sc], sem="sV")
                    hq_user[it_par] = evm1
                    P.step("vector", lambda e: e.tensor_tensor(
                        u_q[:, 0:2 * TC], u_q[:, 0:2 * TC], u_q[:, 2 * TC:4 * TC], ALU.add), sem="sV")
                    if qg == 0:
                        evy = P.step("vector", (lambda i=i: (lambda e: e.tensor_tensor(
                            blk(yp, i, TC), u_q[:, 0:TC], u_q[:, TC:2 * TC], ALU.add)))(), sem="sV")
                    else:
                        P.step("vector", lambda e: e.tensor_tensor(
                            u_q[:, 0:TC], u_q[:, 0:TC], u_q[:, TC:2 * TC], ALU.add), sem="sV")
                        evy = P.step("vector", (lambda i=i: (lambda e: e.tensor_tensor(
                            blk(yp, i, TC), blk(yp, i, TC), u_q[:, 0:TC], ALU.add)))(), sem="sV")
                    ev_y[i] = evy
                    uq_user = evy
                    last_u = evu
                bq_user[qpar] = last_u
                cq_user[qpar] = evm1

            ev_g_new = []
            for i in range(IT):
                off = i * (TC + 8) + 3
                P.step("vector", (lambda i=i: (lambda e: e.scalar_tensor_tensor(
                    blk(pool_c, i, TC), blk(xcp, i, TC), dcol[:, i:i + 1],
                    blk(yp, i, TC), ALU.mult, ALU.add)))(),
                    waits=[ev_y[i], ev_small], sem="sV")
                e2 = P.step("vector", (lambda i=i, off=off: (lambda e: e.tensor_tensor(
                    xbrg[:, off:off + TC], blk(pool_c, i, TC), blk(szp, i, TC), ALU.mult)))(),
                    waits=[ev_t1[IT + i], prev_halo[i]], sem="sV")
                ev_g_new.append(e2)
                prev_g[i] = e2

            evlw = P.step("sync", lambda e: e.dma_start(
                out=wwin[:].rearrange("p (kt j) -> p kt j", kt=IT),
                in_=wout_b.rearrange("(kt p) j -> p kt j", p=128)),
                waits=[ev_w["wout"], wwin_guard], sem="sWW", k=16)
            ev_t3d = []
            for j in range(IT):
                bank = [p5, p6][j % 2]
                evm = None
                for kt in range(IT):
                    off = kt * (TC + 8) + 3
                    wl = [ev_g_new[kt]]
                    if kt == 0:
                        wl += [evlw, t3_guard[j % 2]]
                    evm = P.step("tensor", (lambda bank=bank, kt=kt, j=j, off=off: (lambda e: e.matmul(
                        bank[:, 0:TC], wwin[:, kt * DM + j * 128:kt * DM + (j + 1) * 128],
                        xbrg[:, off:off + TC], start=(kt == 0), stop=(kt == IT - 1))))(),
                        waits=wl, sem="sT" if kt == IT - 1 else None)
                wwin_guard = evm
                uqb = u_q[:, (j % 4) * TC:((j % 4) + 1) * TC]
                eve3 = P.step("scalar", (lambda bank=bank, uqb=uqb: (lambda e: e.copy(
                    out=uqb, in_=bank[:, 0:TC])))(),
                    waits=[evm, t3_uq_guard[j % 4]], sem="sS")
                evd = P.step("sync", (lambda c=c, j=j, uqb=uqb: (lambda e: e.dma_start(
                    out=wo_i[c][j * 128:(j + 1) * 128, :], in_=uqb)))(),
                    waits=[eve3], sem="sT3%d" % (j % 4), k=16)
                t3_guard[j % 2] = eve3
                t3_uq_guard[j % 4] = evd
                ev_t3d.append(evd)
            ev_ar2 = P.step("gpsimd", (lambda c=c: (lambda e: e.collective_compute(
                "AllReduce", ALU.add, replica_groups=GROUPS,
                ins=[wo_i[c]], outs=[wo_o[c]])))(), waits=ev_t3d, sem="sCCW")

            ev_xm = P.step("gpsimd", (lambda c=c: (lambda e: e.dma_start(
                out=pool_a[:].rearrange("p (kt t) -> p kt t", kt=KT),
                in_=wo_o[c].rearrange("(kt p) t -> p kt t", p=128))))(),
                waits=[ev_ar2, ev_ln1], sem="sXM", k=16)
            ev_sq2 = []
            for kt in range(KT):
                ev_sq2.append(P.step("scalar", (lambda kt=kt: (lambda e: e.activation(
                    out=blk(pool_b, kt, TC), in_=blk(pool_a, kt, TC), func=AF.Square)))(),
                    waits=[ev_xm], sem="sS"))
            evA = evB = None
            for kt in range(KT):
                evA = P.step("tensor", mm_stats(p3, (lambda kt=kt: lambda: blk(pool_a, kt, TC))(), kt, kt == 0, kt == KT - 1),
                             waits=([ev_xm, p34_guard] if kt == 0 else []),
                             sem="sT" if kt == KT - 1 else None)
            for kt in range(KT):
                evB = P.step("tensor", mm_stats(p4, (lambda kt=kt: lambda: blk(pool_b, kt, TC))(), kt, kt == 0, kt == KT - 1),
                             waits=[ev_sq2[kt]], sem="sT" if kt == KT - 1 else None)
            eba2, ebb2 = ln_rows(NCH + c, TC)
            ev_xr = P.step("gpsimd", (lambda xsrc=xsrc: (lambda e: e.dma_start(
                out=pool_c[:].rearrange("p (kt t) -> p kt t", kt=KT), in_=xsrc)))(),
                waits=[ev_g_new[IT - 1]], sem="sXR", k=16)
            ev_x1c = []
            for kt in range(KT):
                P.step("vector", (lambda kt=kt: (lambda e: e.tensor_tensor(
                    blk(pool_b, kt, TC), blk(pool_a, kt, TC), abc_a[:, 0:TC], ALU.mult)))(),
                    waits=[eba2], sem="sV")
                P.step("vector", (lambda kt=kt: (lambda e: e.tensor_tensor(
                    blk(pool_b, kt, TC), blk(pool_b, kt, TC), abc_b[:, 0:TC], ALU.subtract)))(),
                    waits=[ebb2], sem="sV")
                ev_x1c.append(P.step("vector", (lambda kt=kt: (lambda e: e.tensor_tensor(
                    blk(pool_b, kt, TC), blk(pool_b, kt, TC), blk(pool_c, kt, TC), ALU.add)))(),
                    waits=[ev_xr], sem="sV"))
            abc_guard = ev_x1c[-1]
            mc = 0 if c % 2 == 0 else 2
            ev_blend = None
            for kt in range(KT):
                P.step("vector", (lambda kt=kt, mc=mc: (lambda e: e.tensor_scalar_mul(
                    blk(pool_c, kt, TC), blk(x1b, kt, TC),
                    mcols[:, mc + 1:mc + 2])))(),
                    waits=[ev_x1c[kt], ev_ffn_slotbuf], sem="sV")
                ev_blend = P.step("vector", (lambda kt=kt, mc=mc: (lambda e: e.scalar_tensor_tensor(
                    blk(x1b, kt, TC),
                    blk(pool_b, kt, TC), mcols[:, mc:mc + 1],
                    blk(pool_c, kt, TC), ALU.mult, ALU.add)))(), sem="sV")
            chunk_done = ev_blend

            if c % 2 == 1:
                k = kslot
                x1s = x1b
                ev_q3 = []
                for kt in range(KT):
                    ev_q3.append(P.step("scalar", (lambda kt=kt: (lambda e: e.activation(
                        out=blk(f2h, kt, TC), in_=blk(x1s, kt, TC), func=AF.Square)))(),
                        waits=[ev_blend, f2h_guard], sem="sS"))
                evA = evB = None
                for kt in range(KT):
                    evA = P.step("tensor", mm_stats(p3, (lambda kt=kt: lambda: blk(x1s, kt, TC))(), kt, kt == 0, kt == KT - 1),
                                 waits=([ev_blend, p34_guard] if kt == 0 else []),
                                 sem="sT" if kt == KT - 1 else None)
                for kt in range(KT):
                    evB = P.step("tensor", mm_stats(p4, (lambda kt=kt: lambda: blk(f2h, kt, TC))(), kt, kt == 0, kt == KT - 1),
                                 waits=[ev_q3[kt]], sem="sT" if kt == KT - 1 else None)
                eba3, ebb3 = ln_rows(2 * NCH + k, TC)
                ev_ln3 = None
                for kt in range(KT):
                    P.step("vector", (lambda kt=kt: (lambda e: e.tensor_tensor(
                        blk(ln3, kt, TC), blk(x1s, kt, TC), abc_a[:, 0:TC], ALU.mult)))(),
                        waits=[eba3], sem="sV")
                    ev_ln3 = P.step("vector", (lambda kt=kt: (lambda e: e.tensor_tensor(
                        blk(ln3, kt, TC), blk(ln3, kt, TC), abc_b[:, 0:TC], ALU.subtract)))(),
                        waits=[ebb3], sem="sV")
                abc_guard = ev_ln3
                f2h_stats_done = evB

                ev_ffn_last = None
                bankg = {0: t1_evac_guard[0], 1: t1_evac_guard[1],
                         2: t3_guard[0], 3: p7_guard}
                bank_of = lambda mt: [p0, p1, p5, p7][mt]
                coff_of = lambda mt: 0
                for h in range(2):
                    hc0 = h * HTOK
                    ev_f1 = [None] * 32
                    for mg in range(8):
                        evl = [None, None]
                        for kh in range(2):
                            wpar = (mg * 2 + kh) % 2
                            kw = kwin[:, wpar * 2048:(wpar + 1) * 2048]
                            evl[kh] = P.step("sync", (lambda mg=mg, kh=kh, kw=kw: (lambda e: e.dma_start(
                                out=kw.rearrange("p (kt j) -> p kt j", kt=4),
                                in_=w0e[kh * 4:(kh + 1) * 4, :, mg * 512:(mg + 1) * 512].rearrange("kt p j -> p kt j"))))(),
                                waits=[ev_stage_done, kwin_user[wpar]], sem="sKW%d" % wpar, k=16)
                        for kh in range(2):
                            wpar = (mg * 2 + kh) % 2
                            kw = kwin[:, wpar * 2048:(wpar + 1) * 2048]
                            for mt in range(4):
                                m = mg * 4 + mt
                                bank = bank_of(mt)
                                for kt in range(4):
                                    wl = []
                                    if kt == 0:
                                        wl = [evl[kh], ev_ln3, ev_f1_free]
                                        if kh == 0:
                                            wl.append(bankg[mt])
                                    evm = P.step("tensor", (lambda bank=bank, kw=kw, kt=kt, kh=kh, mt=mt, hc0=hc0: (lambda e: e.matmul(
                                        bank[:, 0:HTOK],
                                        kw[:, kt * 512 + mt * 128:kt * 512 + (mt + 1) * 128],
                                        ln3[:, (kh * 4 + kt) * TC + hc0:(kh * 4 + kt) * TC + hc0 + HTOK],
                                        start=(kh == 0 and kt == 0), stop=(kh == 1 and kt == 3))))(),
                                        waits=wl, sem="sT" if kt == 3 else None)
                                if kh == 1:
                                    eve = P.step("scalar", (lambda bank=bank, m=m: (lambda e: e.activation(
                                        out=blk(f1, m, HTOK), in_=bank[:, 0:HTOK], func=AF.Relu,
                                        scale=scols[:, 0:1])))(),
                                        waits=[evm, ev_scl], sem="sS")
                                    ev_f1[m] = eve
                                    bankg[mt] = eve
                            kwin_user[wpar] = evm
                    evr = None
                    for m in range(32):
                        wl = [ev_f1[m]]
                        if m == 0:
                            wl.append(t3_guard[1])
                        evr = P.step("tensor", (lambda m=m: (lambda e: e.matmul(
                            p6[0:1, 0:HTOK], onec[:, 0:1], blk(f1, m, HTOK),
                            start=(m == 0), stop=(m == 31))))(),
                            waits=wl, sem="sT" if m == 31 else None)
                    ev_rsf = P.step("scalar", lambda e: e.copy(
                        out=rsf[0:1, 0:HTOK], in_=p6[0:1, 0:HTOK]), waits=[evr, rsf_guard], sem="sS")
                    t3_guard[1] = ev_rsf
                    P.step("vector", lambda e: e.tensor_copy(rshl[0:1, 0:HTOK], rsf[0:1, 0:HTOK]),
                           waits=[ev_rsf], sem="sV")
                    ev_rs1 = P.step("vector", lambda e: e.tensor_tensor(
                        rshl[0:1, HTOK:2 * HTOK], rsf[0:1, 0:HTOK], rshl[0:1, 0:HTOK], ALU.subtract), sem="sV")
                    rsf_guard = ev_rs1

                    ev_f3 = [None] * KT
                    f2_guard_prev = [f2h_stats_done] * 16
                    for mh in range(2):
                        ev_f2 = [None] * 16
                        for mgl in range(4):
                            mg = mh * 4 + mgl
                            evm_fin = [None] * 4
                            for kq in range(8):
                                wpar = kq % 2
                                kw = kwin[:, wpar * 2048:(wpar + 1) * 2048]
                                evl = P.step("sync", (lambda kq=kq, mg=mg, kw=kw: (lambda e: e.dma_start(
                                    out=kw.rearrange("p (kt j) -> p kt j", kt=4),
                                    in_=k1b[kq * 512:(kq + 1) * 512, mg * 512:(mg + 1) * 512]
                                    .rearrange("(kt p) j -> p kt j", p=128))))(),
                                    waits=[ev_k1b, kwin_user[wpar]], sem="sKW%d" % wpar, k=16)
                                for mt in range(4):
                                    ml = mgl * 4 + mt
                                    bank, coff = bank_of(mt), coff_of(mt)
                                    if kq == 0:
                                        P.step("tensor", (lambda bank=bank, coff=coff: (lambda e: e.matmul(
                                            bank[:, coff:coff + HTOK], nhrow[0:1, :],
                                            rshl[0:1, 0:HTOK], start=True, stop=False)))(),
                                            waits=[ev_rs1, bankg[mt]])
                                        P.step("tensor", (lambda bank=bank, coff=coff: (lambda e: e.matmul(
                                            bank[:, coff:coff + HTOK], nhrow[0:1, :],
                                            rshl[0:1, HTOK:2 * HTOK], start=False, stop=False)))())
                                    evm = None
                                    for kt8 in range(4):
                                        evm = P.step("tensor", (lambda bank=bank, kw=kw, kt8=kt8, mt=mt, coff=coff, kq=kq: (lambda e: e.matmul(
                                            bank[:, coff:coff + HTOK],
                                            kw[:, kt8 * 512 + mt * 128:kt8 * 512 + (mt + 1) * 128],
                                            blk(f1, kq * 4 + kt8, HTOK),
                                            start=False, stop=(kq == 7 and kt8 == 3))))(),
                                            waits=([evl] if kt8 == 0 else []),
                                            sem="sT" if kt8 == 3 else None)
                                    evm_fin[mt] = evm
                                kwin_user[wpar] = evm_fin[3]
                            for mt in range(4):
                                ml = mgl * 4 + mt
                                ev_f2[ml] = P.step("scalar", (lambda bank=bank_of(mt), ml=ml, coff=coff_of(mt): (lambda e: e.activation(
                                    out=blk(f2h, ml, HTOK), in_=bank[:, coff:coff + HTOK], func=AF.Relu,
                                    scale=scols[:, 1:2])))(),
                                    waits=[evm_fin[mt], f2_guard_prev[ml]], sem="sS")
                                bankg[mt] = ev_f2[ml]
                        evr = None
                        for ml in range(16):
                            wl = [ev_f2[ml]]
                            if ml == 0:
                                wl.append(t3_guard[1])
                            evr = P.step("tensor", (lambda ml=ml: (lambda e: e.matmul(
                                p6[0:1, HTOK:2 * HTOK], onec[:, 0:1], blk(f2h, ml, HTOK),
                                start=(ml == 0), stop=(ml == 15))))(),
                                waits=wl, sem="sT" if ml == 15 else None)
                        ev_rsf2 = P.step("scalar", lambda e: e.copy(
                            out=rsf[0:1, 0:HTOK], in_=p6[0:1, HTOK:2 * HTOK]),
                            waits=[evr, rsf_guard], sem="sS")
                        t3_guard[1] = ev_rsf2
                        P.step("vector", lambda e: e.tensor_copy(
                            rshl[0:1, 2 * HTOK:3 * HTOK], rsf[0:1, 0:HTOK]),
                            waits=[ev_rsf2], sem="sV")
                        ev_rs2 = P.step("vector", lambda e: e.tensor_tensor(
                            rshl[0:1, 3 * HTOK:4 * HTOK], rsf[0:1, 0:HTOK], rshl[0:1, 2 * HTOK:3 * HTOK], ALU.subtract), sem="sV")
                        rsf_guard = ev_rs2
                        for jg in range(2):
                            evm_fin = [None] * 4
                            for kq in range(4):
                                wpar = kq % 2
                                kw = kwin[:, wpar * 2048:(wpar + 1) * 2048]
                                krow = mh * 2048 + kq * 512
                                evl = P.step("sync", (lambda krow=krow, jg=jg, kw=kw: (lambda e: e.dma_start(
                                    out=kw.rearrange("p (kt j) -> p kt j", kt=4),
                                    in_=k2b[krow:krow + 512, jg * 512:(jg + 1) * 512]
                                    .rearrange("(kt p) j -> p kt j", p=128))))(),
                                    waits=[ev_k2b, kwin_user[wpar]], sem="sKW%d" % wpar, k=16)
                                for jt in range(4):
                                    j = jg * 4 + jt
                                    bank, coff = bank_of(jt), coff_of(jt)
                                    if kq == 0:
                                        P.step("tensor", (lambda bank=bank, coff=coff: (lambda e: e.matmul(
                                            bank[:, coff:coff + HTOK], nhrow[0:1, :],
                                            rshl[0:1, 2 * HTOK:3 * HTOK], start=True, stop=False)))(),
                                            waits=[ev_rs2, bankg[jt]])
                                        rsk1 = P.step("tensor", (lambda bank=bank, coff=coff: (lambda e: e.matmul(
                                            bank[:, coff:coff + HTOK], nhrow[0:1, :],
                                            rshl[0:1, 3 * HTOK:4 * HTOK], start=False, stop=False)))(),
                                            sem="sT")
                                        rsf_guard = rsk1
                                    evm = None
                                    for kt8 in range(4):
                                        evm = P.step("tensor", (lambda bank=bank, kw=kw, kt8=kt8, jt=jt, coff=coff, kq=kq, mh=mh: (lambda e: e.matmul(
                                            bank[:, coff:coff + HTOK],
                                            kw[:, kt8 * 512 + jt * 128:kt8 * 512 + (jt + 1) * 128],
                                            blk(f2h, kq * 4 + kt8, HTOK),
                                            start=False, stop=(kq == 3 and kt8 == 3))))(),
                                            waits=([evl] if kt8 == 0 else []),
                                            sem="sT" if kt8 == 3 else None)
                                    evm_fin[jt] = evm
                                kwin_user[wpar] = evm_fin[3]
                            for jt in range(4):
                                j = jg * 4 + jt
                                if mh == 0:
                                    ev_f3[j] = P.step("scalar", (lambda bank=bank_of(jt), j=j, coff=coff_of(jt): (lambda e: e.activation(
                                        out=blk(f3, j, HTOK), in_=bank[:, coff:coff + HTOK],
                                        func=AF.Identity, scale=scols[:, 2:3])))(),
                                        waits=[evm_fin[jt]], sem="sS")
                                else:
                                    ev_f3[j] = P.step("vector", (lambda bank=bank_of(jt), j=j, coff=coff_of(jt): (lambda e: e.scalar_tensor_tensor(
                                        blk(f3, j, HTOK), bank[:, coff:coff + HTOK],
                                        scols[:, 2:3], blk(f3, j, HTOK), ALU.mult, ALU.add)))(),
                                        waits=[evm_fin[jt], ("sS", P.sems["sS"][1])], sem="sV")
                                bankg[jt] = ev_f3[j]
                            f2_guard_prev = [evm_fin[3]] * 16
                    f2h_guard = ev_f3[-1]
                    ev_q4 = []
                    for kt in range(KT):
                        ev_q4.append(P.step("scalar", (lambda kt=kt: (lambda e: e.activation(
                            out=blk(f2h, kt, HTOK), in_=blk(f3, kt, HTOK), func=AF.Square)))(),
                            waits=[ev_f3[kt], f2h_guard], sem="sS"))
                    evA = evB = None
                    for kt in range(KT):
                        evA = P.step("tensor", mm_stats(p3, (lambda kt=kt: lambda: blk(f3, kt, HTOK))(), kt, kt == 0, kt == KT - 1, N=HTOK),
                                     waits=([ev_f3[kt], p34_guard] if kt == 0 else [ev_f3[kt]]),
                                     sem="sT" if kt == KT - 1 else None)
                    for kt in range(KT):
                        evB = P.step("tensor", mm_stats(p4, (lambda kt=kt: lambda: blk(f2h, kt, HTOK))(), kt, kt == 0, kt == KT - 1, N=HTOK),
                                     waits=[ev_q4[kt]], sem="sT" if kt == KT - 1 else None)
                    eba4, ebb4 = ln_rows(3 * NCH + 2 * k + h, HTOK)
                    f2h_guard = evB
                    last_norm = None
                    for kg in range(4):
                        for kl in range(2):
                            kt = kg * 2 + kl
                            P.step("vector", (lambda kt=kt, kl=kl: (lambda e: e.tensor_tensor(
                                blk(x2t, kl, HTOK), blk(f3, kt, HTOK), abc_a[:, 0:HTOK], ALU.mult)))(),
                                waits=[eba4, x2t_guard], sem="sV")
                            P.step("vector", (lambda kl=kl: (lambda e: e.tensor_tensor(
                                blk(x2t, kl, HTOK), blk(x2t, kl, HTOK), abc_b[:, 0:HTOK], ALU.subtract)))(),
                                waits=[ebb4], sem="sV")
                            evx2 = P.step("vector", (lambda kt=kt, kl=kl, hc0=hc0: (lambda e: e.tensor_tensor(
                                blk(x2t, kl, HTOK), blk(x2t, kl, HTOK),
                                x1s[:, kt * TC + hc0:kt * TC + hc0 + HTOK], ALU.add)))(), sem="sV")
                            last_norm = evx2
                            evst = P.step("sync", (lambda kt=kt, kl=kl, k=k, hc0=hc0: (lambda e: e.dma_start(
                                out=outT[kt * 128:(kt + 1) * 128, k * TC + hc0:k * TC + hc0 + HTOK],
                                in_=blk(x2t, kl, HTOK))))(),
                                waits=[evx2], sem="sST", k=16)
                            ev_out_final = evst
                        x2t_guard = ev_out_final
                    abc_guard = last_norm
                    ev_ffn_last = last_norm
                t1_evac_guard[0] = bankg[0]
                t1_evac_guard[1] = bankg[1]
                t3_guard[0] = bankg[2]
                p7_guard = bankg[3]
                ev_ffn_slotbuf = ev_ffn_last

        P.step("sync", lambda e: e.nop(), waits=[("sST", P.sems["sST"][1])])

        with nc.Block() as b:
            @b.sync
            def _(e):
                P.run("sync", e)

            @b.gpsimd
            def _(e):
                P.run("gpsimd", e)

            @b.scalar
            def _(e):
                P.run("scalar", e)

            @b.vector
            def _(e):
                P.run("vector", e)

            @b.tensor
            def _(e):
                P.run("tensor", e)

    return nc


def kernel(L0_attn_x, L0_attn_W_in, L0_attn_conv_w, L0_attn_conv_b, L0_attn_W_x,
           L0_attn_A_log, L0_attn_D_param, L0_attn_dt_bias, L0_attn_W_out,
           L0_ffn_k0, L0_ffn_k1, L0_ffn_k2, L0_ffn_s0, L0_ffn_s1, L0_ffn_s2):
    x = np.asarray(L0_attn_x, dtype=np.float32)
    W_in = np.asarray(L0_attn_W_in, dtype=np.float32)
    conv_w = np.asarray(L0_attn_conv_w, dtype=np.float32)
    conv_b = np.asarray(L0_attn_conv_b, dtype=np.float32)
    W_x = np.asarray(L0_attn_W_x, dtype=np.float32)
    A_log = np.asarray(L0_attn_A_log, dtype=np.float32)
    D_param = np.asarray(L0_attn_D_param, dtype=np.float32)
    dt_bias = np.asarray(L0_attn_dt_bias, dtype=np.float32)
    W_out = np.asarray(L0_attn_W_out, dtype=np.float32)
    k0 = np.ascontiguousarray(np.asarray(L0_ffn_k0)).view(np.uint8)
    k1 = np.ascontiguousarray(np.asarray(L0_ffn_k1)).view(np.uint8)
    k2 = np.ascontiguousarray(np.asarray(L0_ffn_k2)).view(np.uint8)
    s0 = np.asarray(L0_ffn_s0, dtype=np.float32).reshape(1, 1)
    s1 = np.asarray(L0_ffn_s1, dtype=np.float32).reshape(1, 1)
    s2 = np.asarray(L0_ffn_s2, dtype=np.float32).reshape(1, 1)

    if "nc" not in _NC_CACHE:
        _NC_CACHE["nc"] = build_nc()
    nc = _NC_CACHE["nc"]
    in_maps = make_in_maps(x, W_in, conv_w, conv_b, W_x, A_log, D_param,
                           dt_bias, W_out, k0, k1, k2, s0, s1, s2)

    res = run_bass_kernel_spmd(nc, in_maps, core_ids=list(range(8)))

    out = np.empty((4, NT, DM), np.float32)
    for c in range(8):
        b, dh = c // 2, c % 2
        oT = res.results[c]["outT"]
        for k in range(4):
            ch = 2 * k + dh
            out[b, ch * TC:(ch + 1) * TC, :] = oT[:, k * TC:(k + 1) * TC].T
    return out


def make_in_maps(x, W_in, conv_w, conv_b, W_x, A_log, D_param, dt_bias, W_out,
                 k0, k1, k2, s0, s1, s2):
    in_maps = []
    for c in range(8):
        b, dh = c // 2, c % 2
        sl = slice(dh * DI, (dh + 1) * DI)
        im = dict(
            xT=np.ascontiguousarray(x[b].T),
            wbr=np.ascontiguousarray(W_in[:, dh * DI:(dh + 1) * DI]),
            wz=np.ascontiguousarray(W_in[:, 2048 + dh * DI:2048 + (dh + 1) * DI]),
            wx=np.ascontiguousarray(W_x[sl]),
            convw=np.ascontiguousarray(conv_w[sl, 0, :]),
            convb=np.ascontiguousarray(conv_b[sl].reshape(DI, 1)),
            alog=np.ascontiguousarray(A_log[sl]),
            dpar=np.ascontiguousarray(D_param[sl].reshape(DI, 1)),
            dtb=np.ascontiguousarray(dt_bias[sl].reshape(DI, 1)),
            wout=np.ascontiguousarray(W_out[sl]),
            k0t=np.ascontiguousarray(k0[:DM]),
            k0b=np.ascontiguousarray(k0[DM:2 * DM]),
            k1u=k1, k2u=k2,
            s0i=s0, s1i=s1, s2i=s2,
            m_e=np.full((128, 1), 1.0 if dh == 0 else 0.0, np.float32),
            minv_e=np.full((128, 1), 0.0 if dh == 0 else 1.0, np.float32),
            m_o=np.full((128, 1), 1.0 if dh == 1 else 0.0, np.float32),
            minv_o=np.full((128, 1), 0.0 if dh == 1 else 1.0, np.float32),
            ones_bf=np.ones((128, 1), dtype=BF),
            neghalf=np.full((1, 128), -0.5, dtype=BF),
        )
        in_maps.append(im)
    return in_maps


if __name__ == "__main__":
    build_nc()
    print("built ok")


# version 04
# speedup vs baseline: 1.0546x; 1.0546x over previous
import numpy as np
import ml_dtypes
from contextlib import ExitStack
import concourse.bass as bass
import concourse.mybir as mybir
from concourse.bass_utils import run_bass_kernel_spmd

F32 = mybir.dt.float32
BF16 = mybir.dt.bfloat16
U8 = mybir.dt.uint8
AF = mybir.ActivationFunctionType
ALU = mybir.AluOpType
BF = ml_dtypes.bfloat16

DM = 1024
DI = 1024
NT = 4096
TC = 512
NCH = NT // TC
S = 16
QS = 4
NQ = S // QS
KT = DM // 128
IT = DI // 128
HTOK = TC // 2
LNEPS = 1e-5
GROUPS = [[0, 1], [2, 3], [4, 5], [6, 7]]

_NC_CACHE = {}


class Plan:
    ENGS = ("sync", "gpsimd", "scalar", "vector", "tensor")

    def __init__(self):
        self.items = {e: [] for e in self.ENGS}
        self.sems = {}
        self.waited = {e: {} for e in self.ENGS}

    def add_sem(self, name, handle):
        self.sems[name] = [handle, 0]

    def ensure(self, name, nc, stack):
        if name not in self.sems:
            self.add_sem(name, stack.enter_context(nc.semaphore(name)))

    def step(self, eng, fn, waits=(), sem=None, k=1):
        ws = []
        for w in waits:
            if w is None:
                continue
            sn, val = w
            if val <= 0:
                continue
            if val > self.waited[eng].get(sn, 0):
                self.waited[eng][sn] = val
                ws.append((self.sems[sn][0], val))
        ev = None
        if sem is not None:
            self.sems[sem][1] += k
            ev = (sem, self.sems[sem][1])
        self.items[eng].append((fn, tuple(ws), None if sem is None else self.sems[sem][0], k))
        return ev

    def run(self, eng_name, eng):
        for fn, ws, semh, k in self.items[eng_name]:
            for (h, v) in ws:
                eng.wait_ge(h, v)
            ins = fn(eng)
            if semh is not None:
                ins.then_inc(semh, k)


def build_nc():
    nc = bass.Bass("TRN2", target_bir_lowering=False, detect_race_conditions=False)

    def inp(name, shape, dt):
        return nc.dram_tensor(name, list(shape), dt, kind="ExternalInput").ap()

    xT = inp("xT", (DM, NT), F32)
    wbr = inp("wbr", (DM, DI), F32)
    wz = inp("wz", (DM, DI), F32)
    wx = inp("wx", (DI, 33), F32)
    convw = inp("convw", (DI, 4), F32)
    convb = inp("convb", (DI, 1), F32)
    alog = inp("alog", (DI, S), F32)
    dpar = inp("dpar", (DI, 1), F32)
    dtb = inp("dtb", (DI, 1), F32)
    wout = inp("wout", (DI, DM), F32)
    k0t = inp("k0t", (DM, 4096), U8)
    k0b = inp("k0b", (DM, 4096), U8)
    k1u = inp("k1u", (4096, 4096), U8)
    k2u = inp("k2u", (4096, DM), U8)
    s0i = inp("s0i", (1, 1), F32)
    s1i = inp("s1i", (1, 1), F32)
    s2i = inp("s2i", (1, 1), F32)
    m_e = inp("m_e", (128, 1), F32)
    minv_e = inp("minv_e", (128, 1), F32)
    m_o = inp("m_o", (128, 1), F32)
    minv_o = inp("minv_o", (128, 1), F32)
    ones_bf = inp("ones_bf", (128, 1), BF16)
    neghalf = inp("neghalf", (1, 128), BF16)

    outT = nc.dram_tensor("outT", [DM, NT // 2], F32, kind="ExternalOutput").ap()

    def ihb(name, shape, dt):
        return nc.dram_tensor(name, list(shape), dt, kind="Internal").ap()

    w0e = ihb("w0e", (KT, 128, 4096), BF16)
    k1b = ihb("k1b", (4096, 4096), BF16)
    k2b = ihb("k2b", (4096, DM), BF16)
    wbr_b = ihb("wbr_b", (DM, DI), BF16)
    wz_b = ihb("wz_b", (DM, DI), BF16)
    wout_b = ihb("wout_b", (DI, DM), BF16)
    ssm_i = ihb("ssm_i", (NCH, 33, TC), BF16)
    ssm_o = ihb("ssm_o", (NCH, 33, TC), BF16)
    wo_i = ihb("wo_i", (NCH, DM, TC), BF16)
    wo_o = ihb("wo_o", (NCH, DM, TC), BF16)
    lnr = ihb("lnr", (4 * NCH, 2, TC), F32)

    st = ExitStack()
    sb = lambda n, shp, dt: st.enter_context(nc.sbuf_tensor(n, list(shp), dt))
    ps = lambda n: st.enter_context(nc.psum_tensor(n, [128, 512], F32))

    with st:
        wwin = sb("wwin", (128, KT * DI), BF16)
        wx_s = sb("wx_s", (128, IT * 33), BF16)
        acol = sb("acol", (128, IT * S), F32)
        cwcol = sb("cwcol", (128, IT * 4), F32)
        cbcol = sb("cbcol", (128, IT), F32)
        dtbcol = sb("dtbcol", (128, IT), F32)
        dcol = sb("dcol", (128, IT), F32)
        scols = sb("scols", (128, 4), F32)
        onec = sb("onec", (128, 1), BF16)
        nhrow = sb("nhrow", (1, 128), BF16)
        mcols = sb("mcols", (128, 4), F32)
        epsc = sb("epsc", (128, 1), F32)
        onef = sb("onef", (128, 1), F32)

        ln1x = sb("ln1x", (128, KT * TC), BF16)
        xcp = sb("xcp", (128, IT * TC), BF16)
        szp = sb("szp", (128, IT * TC), BF16)
        xbrg = sb("xbrg", (128, IT * (TC + 8)), BF16)
        dtp = sb("dtp", (128, IT * TC), BF16)
        wpl = sb("wpl", (128, IT * TC), BF16)
        yp = sb("yp", (128, IT * TC), BF16)
        x1b = sb("x1b", (128, KT * TC), BF16)
        pool_a = sb("pool_a", (128, KT * TC), BF16)
        pool_b = sb("pool_b", (128, KT * TC), BF16)
        pool_c = sb("pool_c", (128, KT * TC), BF16)

        u_q = sb("u_q", (128, QS * TC), BF16)
        dA_q = sb("dA_q", (128, 2 * QS * TC), BF16)
        h_q = sb("h_q", (128, 2 * QS * TC), BF16)
        hcar = sb("hcar", (128, IT * S), F32)
        bqt = sb("bqt", (128, 2 * QS * TC), BF16)
        cqt = sb("cqt", (128, 2 * QS * TC), BF16)

        rows = sb("rows", (1, 3 * TC), F32)
        abc_a = sb("abc_a", (128, TC), BF16)
        abc_b = sb("abc_b", (128, TC), BF16)
        ssm_s = sb("ssm_s", (33, TC), BF16)
        cbc = sb("cbc", (128, TC), BF16)

        f1 = sb("f1", (128, 32 * HTOK), BF16)
        f2h = sb("f2h", (128, 16 * HTOK), BF16)
        ln3 = sb("ln3", (128, KT * TC), BF16)
        f3 = sb("f3", (128, KT * HTOK), BF16)
        kwin = sb("kwin", (128, 2 * 2048), BF16)
        rsf = sb("rsf", (1, HTOK), F32)
        rshl = sb("rshl", (1, 4 * HTOK), BF16)
        x2t = sb("x2t", (128, 2 * HTOK), F32)

        p0, p1, p2, p3, p4, p5, p6, p7 = (ps(f"p{i}") for i in range(8))

        P = Plan()
        _orig_step = P.step
        def _step(eng, fn, waits=(), sem=None, k=1):
            if sem is not None:
                P.ensure(sem, nc, st)
            return _orig_step(eng, fn, waits, sem, k)
        P.step = _step
        for sname in ("sV", "sS", "sT"):
            P.ensure(sname, nc, st)

        def blk(t, i, w):
            return t[:, i * w:(i + 1) * w]

        ev_w = {}
        P.step("gpsimd", lambda e: e.dma_start(out=wbr_b, in_=wbr), sem="sWI", k=16)
        P.step("gpsimd", lambda e: e.dma_start(out=wz_b, in_=wz), sem="sWI", k=16)
        P.step("gpsimd", lambda e: e.dma_start(out=wout_b, in_=wout), sem="sWI", k=16)
        evwi = P.step("gpsimd", lambda e: e.dma_start(
            out=wx_s[:].rearrange("p (kt j) -> p kt j", kt=IT),
            in_=wx.rearrange("(kt p) j -> p kt j", p=128)), sem="sWI", k=16)
        ev_w["wbr"] = ev_w["wz"] = ev_w["wout"] = ev_w["wx"] = evwi

        def ld_cols(dst, src, n):
            return lambda e: e.dma_start(
                out=dst.rearrange("p (kt j) -> p kt j", kt=IT),
                in_=src.rearrange("(kt p) j -> p kt j", p=128),
                allow_slow_non_contiguous=True)

        P.step("sync", ld_cols(acol[:], alog, S), sem="sSM", k=16)
        P.step("sync", ld_cols(cwcol[:], convw, 4), sem="sSM", k=16)
        P.step("sync", ld_cols(cbcol[:], convb, 1), sem="sSM", k=16)
        P.step("sync", ld_cols(dtbcol[:], dtb, 1), sem="sSM", k=16)
        P.step("sync", ld_cols(dcol[:], dpar, 1), sem="sSM", k=16)
        P.step("sync", lambda e: e.dma_start(out=onec[:], in_=ones_bf), sem="sSM", k=16)
        P.step("sync", lambda e: e.dma_start(out=nhrow[:], in_=neghalf), sem="sSM", k=16)
        P.step("sync", lambda e: e.dma_start(out=mcols[:, 0:1], in_=m_e), sem="sSM", k=16)
        P.step("sync", lambda e: e.dma_start(out=mcols[:, 1:2], in_=minv_e), sem="sSM", k=16)
        P.step("sync", lambda e: e.dma_start(out=mcols[:, 2:3], in_=m_o), sem="sSM", k=16)
        ev_small = P.step("sync", lambda e: e.dma_start(out=mcols[:, 3:4], in_=minv_o), sem="sSM", k=16)
        ev_alog = ev_small

        P.step("gpsimd", lambda e: e.dma_start(
            out=scols[:, 0:1], in_=s0i[0:1, :].squeeze(0).partition_broadcast(128)), sem="sSC", k=16)
        P.step("gpsimd", lambda e: e.dma_start(
            out=scols[:, 1:2], in_=s1i[0:1, :].squeeze(0).partition_broadcast(128)), sem="sSC", k=16)
        ev_s2x = P.step("gpsimd", lambda e: e.dma_start(
            out=scols[:, 2:3], in_=s2i[0:1, :].squeeze(0).partition_broadcast(128)), sem="sSC", k=16)
        ev_s0 = ev_s1 = ev_s2 = ev_s2x
        P.step("vector", lambda e: e.tensor_scalar_mul(scols[:, 0:1], scols[:, 0:1], 2.0),
               waits=[ev_s0], sem="sV")
        P.step("vector", lambda e: e.tensor_scalar_mul(scols[:, 1:2], scols[:, 1:2], 4.0),
               waits=[ev_s1], sem="sV")
        ev_scl = P.step("vector", lambda e: e.tensor_scalar_mul(scols[:, 2:3], scols[:, 2:3], 4.0),
                        waits=[ev_s2], sem="sV")
        ev_ax = P.step("scalar", lambda e: e.activation(out=acol[:], in_=acol[:], func=AF.Exp),
                       waits=[ev_alog], sem="sS")
        ev_acol = P.step("vector", lambda e: e.tensor_scalar_mul(acol[:], acol[:], -1.0),
                         waits=[ev_ax], sem="sV")
        for i in range(IT):
            off = i * (TC + 8)
            P.step("vector", (lambda off=off: (lambda e: e.memset(xbrg[:, off:off + 8], 0.0)))(), sem="sV")
        P.step("vector", lambda e: e.memset(x1b[:], 0.0), sem="sV")
        ev_zero = P.step("vector", lambda e: e.memset(hcar[:], 0.0), sem="sV")
        ev_eps = P.step("vector", lambda e: e.memset(epsc[:], LNEPS), sem="sV")
        ev_onef = P.step("vector", lambda e: e.memset(onef[:], 1.0), sem="sV")

        ev_k1b = ev_k2b = None
        for q in range(4):
            ev_k1b = P.step("gpsimd", (lambda q=q: (lambda e: e.dma_start(
                out=k1b[q * 1024:(q + 1) * 1024, :], in_=k1u[q * 1024:(q + 1) * 1024, :])))(),
                sem="sK1", k=16)
        for q in range(2):
            ev_k2b = P.step("gpsimd", (lambda q=q: (lambda e: e.dma_start(
                out=k2b[q * 2048:(q + 1) * 2048, :], in_=k2u[q * 2048:(q + 1) * 2048, :])))(),
                sem="sK2", k=16)
        f1u8 = f1[:].bitcast(U8)
        prev_sub = None
        prev_store = None
        ev_stage_done = None
        for kt in range(KT):
            wl = [prev_sub] if prev_sub else []
            eva = P.step("sync", (lambda kt=kt: (lambda e: e.dma_start(
                out=f1u8[:, 0:4096], in_=k0t[kt * 128:(kt + 1) * 128, :])))(),
                waits=wl, sem="sK0A", k=16)
            evb = P.step("sync", (lambda kt=kt: (lambda e: e.dma_start(
                out=f1u8[:, 4096:8192], in_=k0b[kt * 128:(kt + 1) * 128, :])))(),
                waits=wl, sem="sK0B", k=16)
            evs = P.step("vector", lambda e: e.tensor_tensor(
                f1[:, 4096:8192], f1u8[:, 0:4096], f1u8[:, 4096:8192], ALU.subtract),
                waits=[eva, evb] + ([prev_store] if prev_store else []), sem="sV")
            evd = P.step("gpsimd", (lambda kt=kt: (lambda e: e.dma_start(
                out=w0e[kt], in_=f1[:, 4096:8192])))(),
                waits=[evs], sem="sSTG", k=16)
            prev_sub, prev_store = evs, evd
            ev_stage_done = evd
        ev_f1_free = ev_stage_done

        pg = {}
        wwin_guard = None
        p2_guard = None
        p34_guard = None
        ln_dma_guard = None
        abc_guard = None
        prev_halo = [None] * IT
        halo_src_guard = [None] * IT
        prev_g = [None] * IT
        prev_carry = {}
        prev_scan = {}
        hq_user = {}
        dA_user = {}
        uq_user = None
        bq_user = {}
        cq_user = {}
        ev_dt = [None] * IT
        ev_wpl = [None] * IT
        ev_ssm_out_prev = None
        t3_uq_guard = {0: None, 1: None, 2: None, 3: None}
        ev_dt_prev_last = None
        chunk_done = None
        ev_ffn_slotbuf = None
        f2h_guard = None
        x2t_guard = None
        rsf_guard = None
        kwin_user = {0: None, 1: None}
        ev_stats_ready = None

        def mm_stats(dst, getsrc, kt, first, last, N=TC):
            return lambda e: e.matmul(dst[0:1, 0:N], onec[:, 0:1], getsrc(),
                                      start=first, stop=last)

        def ln_rows(idx, w):
            nonlocal ln_dma_guard, p34_guard, abc_guard
            r_mu = rows[0:1, 0:w]
            r_v = rows[0:1, TC:TC + w]
            r_a = rows[0:1, 2 * TC:2 * TC + w]
            r_b = r_v
            inv = 1.0 / DM
            wl = [("sT", P.sems["sT"][1])]
            if ln_dma_guard:
                wl.append(ln_dma_guard)
            e1 = P.step("vector", lambda e: e.tensor_scalar_mul(r_mu, p3[0:1, 0:w], inv),
                        waits=wl, sem="sV")
            e2 = P.step("vector", lambda e: e.tensor_scalar_mul(r_v, p4[0:1, 0:w], inv),
                        sem="sV")
            P.step("vector", lambda e: e.tensor_tensor(r_a, r_mu, r_mu, ALU.mult), sem="sV")
            ev_v = P.step("vector", lambda e: e.tensor_tensor(r_v, r_v, r_a, ALU.subtract), sem="sV")
            P.step("scalar", lambda e: e.activation(out=r_v, in_=r_v, func=AF.Ln, bias=epsc[0:1, 0:1]),
                   waits=[ev_v, ev_eps], sem="sS")
            es = P.step("scalar", lambda e: e.activation(out=r_a, in_=r_v, func=AF.Exp, scale=-0.5),
                        sem="sS")
            ea = es
            eb = P.step("vector", lambda e: e.tensor_tensor(r_b, r_mu, r_a, ALU.mult),
                        waits=[es], sem="sV")
            ed0 = P.step("sync", lambda e: e.dma_start(out=lnr[idx, 0:1, 0:w], in_=r_b),
                         waits=[ea, eb], sem="sRWB", k=16)
            ed1 = P.step("sync", lambda e: e.dma_start(out=lnr[idx, 1:2, 0:w], in_=r_a),
                         waits=[ea], sem="sRWA", k=16)
            gl = [abc_guard] if abc_guard else []
            eba = P.step("gpsimd", lambda e: e.dma_start(
                out=abc_a[:, 0:w], in_=lnr[idx, 1:2, 0:w].squeeze(0).partition_broadcast(128)),
                waits=[ed1] + gl, sem="sABA", k=16)
            ebb = P.step("gpsimd", lambda e: e.dma_start(
                out=abc_b[:, 0:w], in_=lnr[idx, 0:1, 0:w].squeeze(0).partition_broadcast(128)),
                waits=[ed0] + gl, sem="sABB", k=16)
            ln_dma_guard = ed0
            p34_guard = e2 if e2 else e1
            return eba, ebb

        ev_out_final = None
        pend = [None]

        def pump(n=1):
            g = pend[0]
            if g is None:
                return
            for _ in range(n):
                try:
                    next(g)
                except StopIteration:
                    pend[0] = None
                    return

        def drain():
            g = pend[0]
            if g is not None:
                for _ in g:
                    pass
                pend[0] = None

        def emit_ffn(k, ev_blend):
            nonlocal rsf_guard, x2t_guard, abc_guard, f2h_guard, ev_ffn_slotbuf
            x1s = x1b
            ev_q3 = []
            for kt in range(KT):
                ev_q3.append(P.step("scalar", (lambda kt=kt: (lambda e: e.activation(
                    out=blk(f2h, kt, TC), in_=blk(x1s, kt, TC), func=AF.Square)))(),
                    waits=[ev_blend, f2h_guard], sem="sS"))
            evA = evB = None
            for kt in range(KT):
                evA = P.step("tensor", mm_stats(p3, (lambda kt=kt: lambda: blk(x1s, kt, TC))(), kt, kt == 0, kt == KT - 1),
                             waits=([ev_blend, p34_guard] if kt == 0 else []),
                             sem="sT" if kt == KT - 1 else None)
            for kt in range(KT):
                evB = P.step("tensor", mm_stats(p4, (lambda kt=kt: lambda: blk(f2h, kt, TC))(), kt, kt == 0, kt == KT - 1),
                             waits=[ev_q3[kt]], sem="sT" if kt == KT - 1 else None)
            eba3, ebb3 = ln_rows(2 * NCH + k, TC)
            ev_ln3 = None
            for kt in range(KT):
                P.step("vector", (lambda kt=kt: (lambda e: e.tensor_tensor(
                    blk(ln3, kt, TC), blk(x1s, kt, TC), abc_a[:, 0:TC], ALU.mult)))(),
                    waits=[eba3], sem="sV")
                ev_ln3 = P.step("vector", (lambda kt=kt: (lambda e: e.tensor_tensor(
                    blk(ln3, kt, TC), blk(ln3, kt, TC), abc_b[:, 0:TC], ALU.subtract)))(),
                    waits=[ebb3], sem="sV")
            abc_guard = ev_ln3
            yield
            f2h_stats_done = evB

            ev_ffn_last = None
            bkey = ["p0", "p1", "p5", "p7"]
            bank_of = lambda mt: [p0, p1, p5, p7][mt]
            coff_of = lambda mt: 0
            for h in range(2):
                hc0 = h * HTOK
                ev_f1 = [None] * 32
                for mg in range(8):
                    evl = [None, None]
                    for kh in range(2):
                        wpar = (mg * 2 + kh) % 2
                        kw = kwin[:, wpar * 2048:(wpar + 1) * 2048]
                        evl[kh] = P.step("sync", (lambda mg=mg, kh=kh, kw=kw: (lambda e: e.dma_start(
                            out=kw.rearrange("p (kt j) -> p kt j", kt=4),
                            in_=w0e[kh * 4:(kh + 1) * 4, :, mg * 512:(mg + 1) * 512].rearrange("kt p j -> p kt j"))))(),
                            waits=[ev_stage_done, kwin_user[wpar]], sem="sKW%d" % wpar, k=16)
                    for kh in range(2):
                        wpar = (mg * 2 + kh) % 2
                        kw = kwin[:, wpar * 2048:(wpar + 1) * 2048]
                        for mt in range(4):
                            m = mg * 4 + mt
                            bank = bank_of(mt)
                            for kt in range(4):
                                wl = []
                                if kt == 0:
                                    wl = [evl[kh], ev_ln3, ev_f1_free]
                                    if kh == 0:
                                        wl.append(pg.get(bkey[mt]))
                                evm = P.step("tensor", (lambda bank=bank, kw=kw, kt=kt, kh=kh, mt=mt, hc0=hc0: (lambda e: e.matmul(
                                    bank[:, 0:HTOK],
                                    kw[:, kt * 512 + mt * 128:kt * 512 + (mt + 1) * 128],
                                    ln3[:, (kh * 4 + kt) * TC + hc0:(kh * 4 + kt) * TC + hc0 + HTOK],
                                    start=(kh == 0 and kt == 0), stop=(kh == 1 and kt == 3))))(),
                                    waits=wl, sem="sT" if kt == 3 else None)
                            if kh == 1:
                                eve = P.step("scalar", (lambda bank=bank, m=m: (lambda e: e.activation(
                                    out=blk(f1, m, HTOK), in_=bank[:, 0:HTOK], func=AF.Relu,
                                    scale=scols[:, 0:1])))(),
                                    waits=[evm, ev_scl], sem="sS")
                                ev_f1[m] = eve
                                pg[bkey[mt]] = eve
                        kwin_user[wpar] = evm
                yield
                evr = None
                for m in range(32):
                    wl = [ev_f1[m]]
                    if m == 0:
                        wl.append(pg.get("p6"))
                    evr = P.step("tensor", (lambda m=m: (lambda e: e.matmul(
                        p6[0:1, 0:HTOK], onec[:, 0:1], blk(f1, m, HTOK),
                        start=(m == 0), stop=(m == 31))))(),
                        waits=wl, sem="sT" if m == 31 else None)
                ev_rsf = P.step("scalar", lambda e: e.copy(
                    out=rsf[0:1, 0:HTOK], in_=p6[0:1, 0:HTOK]), waits=[evr, rsf_guard], sem="sS")
                pg["p6"] = ev_rsf
                P.step("vector", lambda e: e.tensor_copy(rshl[0:1, 0:HTOK], rsf[0:1, 0:HTOK]),
                       waits=[ev_rsf], sem="sV")
                ev_rs1 = P.step("vector", lambda e: e.tensor_tensor(
                    rshl[0:1, HTOK:2 * HTOK], rsf[0:1, 0:HTOK], rshl[0:1, 0:HTOK], ALU.subtract), sem="sV")
                rsf_guard = ev_rs1
                yield

                ev_f3 = [None] * KT
                f2_guard_prev = [f2h_stats_done] * 16
                for mh in range(2):
                    ev_f2 = [None] * 16
                    for mgl in range(4):
                        mg = mh * 4 + mgl
                        evm_fin = [None] * 4
                        for kq in range(8):
                            wpar = kq % 2
                            kw = kwin[:, wpar * 2048:(wpar + 1) * 2048]
                            evl = P.step("sync", (lambda kq=kq, mg=mg, kw=kw: (lambda e: e.dma_start(
                                out=kw.rearrange("p (kt j) -> p kt j", kt=4),
                                in_=k1b[kq * 512:(kq + 1) * 512, mg * 512:(mg + 1) * 512]
                                .rearrange("(kt p) j -> p kt j", p=128))))(),
                                waits=[ev_k1b, kwin_user[wpar]], sem="sKW%d" % wpar, k=16)
                            for mt in range(4):
                                ml = mgl * 4 + mt
                                bank, coff = bank_of(mt), coff_of(mt)
                                if kq == 0:
                                    P.step("tensor", (lambda bank=bank, coff=coff: (lambda e: e.matmul(
                                        bank[:, coff:coff + HTOK], nhrow[0:1, :],
                                        rshl[0:1, 0:HTOK], start=True, stop=False)))(),
                                        waits=[ev_rs1, pg.get(bkey[mt])])
                                    P.step("tensor", (lambda bank=bank, coff=coff: (lambda e: e.matmul(
                                        bank[:, coff:coff + HTOK], nhrow[0:1, :],
                                        rshl[0:1, HTOK:2 * HTOK], start=False, stop=False)))())
                                evm = None
                                for kt8 in range(4):
                                    evm = P.step("tensor", (lambda bank=bank, kw=kw, kt8=kt8, mt=mt, coff=coff, kq=kq: (lambda e: e.matmul(
                                        bank[:, coff:coff + HTOK],
                                        kw[:, kt8 * 512 + mt * 128:kt8 * 512 + (mt + 1) * 128],
                                        blk(f1, kq * 4 + kt8, HTOK),
                                        start=False, stop=(kq == 7 and kt8 == 3))))(),
                                        waits=([evl] if kt8 == 0 else []),
                                        sem="sT" if kt8 == 3 else None)
                                evm_fin[mt] = evm
                            kwin_user[wpar] = evm_fin[3]
                        for mt in range(4):
                            ml = mgl * 4 + mt
                            ev_f2[ml] = P.step("scalar", (lambda bank=bank_of(mt), ml=ml, coff=coff_of(mt): (lambda e: e.activation(
                                out=blk(f2h, ml, HTOK), in_=bank[:, coff:coff + HTOK], func=AF.Relu,
                                scale=scols[:, 1:2])))(),
                                waits=[evm_fin[mt], f2_guard_prev[ml]], sem="sS")
                            pg[bkey[mt]] = ev_f2[ml]
                        yield
                    evr = None
                    for ml in range(16):
                        wl = [ev_f2[ml]]
                        if ml == 0:
                            wl.append(pg.get("p6"))
                        evr = P.step("tensor", (lambda ml=ml: (lambda e: e.matmul(
                            p6[0:1, HTOK:2 * HTOK], onec[:, 0:1], blk(f2h, ml, HTOK),
                            start=(ml == 0), stop=(ml == 15))))(),
                            waits=wl, sem="sT" if ml == 15 else None)
                    ev_rsf2 = P.step("scalar", lambda e: e.copy(
                        out=rsf[0:1, 0:HTOK], in_=p6[0:1, HTOK:2 * HTOK]),
                        waits=[evr, rsf_guard], sem="sS")
                    pg["p6"] = ev_rsf2
                    P.step("vector", lambda e: e.tensor_copy(
                        rshl[0:1, 2 * HTOK:3 * HTOK], rsf[0:1, 0:HTOK]),
                        waits=[ev_rsf2], sem="sV")
                    ev_rs2 = P.step("vector", lambda e: e.tensor_tensor(
                        rshl[0:1, 3 * HTOK:4 * HTOK], rsf[0:1, 0:HTOK], rshl[0:1, 2 * HTOK:3 * HTOK], ALU.subtract), sem="sV")
                    rsf_guard = ev_rs2
                    for jg in range(2):
                        evm_fin = [None] * 4
                        for kq in range(4):
                            wpar = kq % 2
                            kw = kwin[:, wpar * 2048:(wpar + 1) * 2048]
                            krow = mh * 2048 + kq * 512
                            evl = P.step("sync", (lambda krow=krow, jg=jg, kw=kw: (lambda e: e.dma_start(
                                out=kw.rearrange("p (kt j) -> p kt j", kt=4),
                                in_=k2b[krow:krow + 512, jg * 512:(jg + 1) * 512]
                                .rearrange("(kt p) j -> p kt j", p=128))))(),
                                waits=[ev_k2b, kwin_user[wpar]], sem="sKW%d" % wpar, k=16)
                            for jt in range(4):
                                j = jg * 4 + jt
                                bank, coff = bank_of(jt), coff_of(jt)
                                if kq == 0:
                                    P.step("tensor", (lambda bank=bank, coff=coff: (lambda e: e.matmul(
                                        bank[:, coff:coff + HTOK], nhrow[0:1, :],
                                        rshl[0:1, 2 * HTOK:3 * HTOK], start=True, stop=False)))(),
                                        waits=[ev_rs2, pg.get(bkey[jt])])
                                    rsk1 = P.step("tensor", (lambda bank=bank, coff=coff: (lambda e: e.matmul(
                                        bank[:, coff:coff + HTOK], nhrow[0:1, :],
                                        rshl[0:1, 3 * HTOK:4 * HTOK], start=False, stop=False)))(),
                                        sem="sT")
                                    rsf_guard = rsk1
                                evm = None
                                for kt8 in range(4):
                                    evm = P.step("tensor", (lambda bank=bank, kw=kw, kt8=kt8, jt=jt, coff=coff, kq=kq, mh=mh: (lambda e: e.matmul(
                                        bank[:, coff:coff + HTOK],
                                        kw[:, kt8 * 512 + jt * 128:kt8 * 512 + (jt + 1) * 128],
                                        blk(f2h, kq * 4 + kt8, HTOK),
                                        start=False, stop=(kq == 3 and kt8 == 3))))(),
                                        waits=([evl] if kt8 == 0 else []),
                                        sem="sT" if kt8 == 3 else None)
                                evm_fin[jt] = evm
                            kwin_user[wpar] = evm_fin[3]
                        for jt in range(4):
                            j = jg * 4 + jt
                            if mh == 0:
                                ev_f3[j] = P.step("scalar", (lambda bank=bank_of(jt), j=j, coff=coff_of(jt): (lambda e: e.activation(
                                    out=blk(f3, j, HTOK), in_=bank[:, coff:coff + HTOK],
                                    func=AF.Identity, scale=scols[:, 2:3])))(),
                                    waits=[evm_fin[jt]], sem="sS")
                            else:
                                ev_f3[j] = P.step("vector", (lambda bank=bank_of(jt), j=j, coff=coff_of(jt): (lambda e: e.scalar_tensor_tensor(
                                    blk(f3, j, HTOK), bank[:, coff:coff + HTOK],
                                    scols[:, 2:3], blk(f3, j, HTOK), ALU.mult, ALU.add)))(),
                                    waits=[evm_fin[jt], ("sS", P.sems["sS"][1])], sem="sV")
                            pg[bkey[jt]] = ev_f3[j]
                        f2_guard_prev = [evm_fin[3]] * 16
                f2h_guard = ev_f3[-1]
                ev_q4 = []
                for kt in range(KT):
                    ev_q4.append(P.step("scalar", (lambda kt=kt: (lambda e: e.activation(
                        out=blk(f2h, kt, HTOK), in_=blk(f3, kt, HTOK), func=AF.Square)))(),
                        waits=[ev_f3[kt], f2h_guard], sem="sS"))
                evA = evB = None
                for kt in range(KT):
                    evA = P.step("tensor", mm_stats(p3, (lambda kt=kt: lambda: blk(f3, kt, HTOK))(), kt, kt == 0, kt == KT - 1, N=HTOK),
                                 waits=([ev_f3[kt], p34_guard] if kt == 0 else [ev_f3[kt]]),
                                 sem="sT" if kt == KT - 1 else None)
                for kt in range(KT):
                    evB = P.step("tensor", mm_stats(p4, (lambda kt=kt: lambda: blk(f2h, kt, HTOK))(), kt, kt == 0, kt == KT - 1, N=HTOK),
                                 waits=[ev_q4[kt]], sem="sT" if kt == KT - 1 else None)
                eba4, ebb4 = ln_rows(3 * NCH + 2 * k + h, HTOK)
                f2h_guard = evB
                last_norm = None
                for kg in range(4):
                    for kl in range(2):
                        kt = kg * 2 + kl
                        P.step("vector", (lambda kt=kt, kl=kl: (lambda e: e.tensor_tensor(
                            blk(x2t, kl, HTOK), blk(f3, kt, HTOK), abc_a[:, 0:HTOK], ALU.mult)))(),
                            waits=[eba4, x2t_guard], sem="sV")
                        P.step("vector", (lambda kl=kl: (lambda e: e.tensor_tensor(
                            blk(x2t, kl, HTOK), blk(x2t, kl, HTOK), abc_b[:, 0:HTOK], ALU.subtract)))(),
                            waits=[ebb4], sem="sV")
                        evx2 = P.step("vector", (lambda kt=kt, kl=kl, hc0=hc0: (lambda e: e.tensor_tensor(
                            blk(x2t, kl, HTOK), blk(x2t, kl, HTOK),
                            x1s[:, kt * TC + hc0:kt * TC + hc0 + HTOK], ALU.add)))(), sem="sV")
                        last_norm = evx2
                        evst = P.step("sync", (lambda kt=kt, kl=kl, k=k, hc0=hc0: (lambda e: e.dma_start(
                            out=outT[kt * 128:(kt + 1) * 128, k * TC + hc0:k * TC + hc0 + HTOK],
                            in_=blk(x2t, kl, HTOK))))(),
                            waits=[evx2], sem="sST", k=16)
                        ev_out_final = evst
                    x2t_guard = ev_out_final
                    yield
                abc_guard = last_norm
                ev_ffn_last = last_norm
            ev_ffn_slotbuf = ev_ffn_last



        for c in range(NCH):
            kslot = c // 2
            xsrc = xT[:, c * TC:(c + 1) * TC].rearrange("(kt p) t -> p kt t", p=128)

            ev_xl = P.step("gpsimd", (lambda xsrc=xsrc: (lambda e: e.dma_start(
                out=pool_a[:].rearrange("p (kt t) -> p kt t", kt=KT), in_=xsrc)))(),
                waits=[chunk_done], sem="sXL", k=16)
            ev_sq = []
            for kt in range(KT):
                ev_sq.append(P.step("scalar", (lambda kt=kt: (lambda e: e.activation(
                    out=blk(pool_b, kt, TC), in_=blk(pool_a, kt, TC), func=AF.Square)))(),
                    waits=[ev_xl], sem="sS"))
            evA = evB = None
            for kt in range(KT):
                evA = P.step("tensor", mm_stats(p3, (lambda kt=kt: lambda: blk(pool_a, kt, TC))(), kt, kt == 0, kt == KT - 1),
                             waits=([ev_xl, ev_small, p34_guard] if kt == 0 else []),
                             sem="sT" if kt == KT - 1 else None)
            for kt in range(KT):
                evB = P.step("tensor", mm_stats(p4, (lambda kt=kt: lambda: blk(pool_b, kt, TC))(), kt, kt == 0, kt == KT - 1),
                             waits=[ev_sq[kt]], sem="sT" if kt == KT - 1 else None)
            eba, ebb = ln_rows(c, TC)
            ev_ln1 = None
            for kt in range(KT):
                P.step("vector", (lambda kt=kt: (lambda e: e.tensor_tensor(
                    blk(ln1x, kt, TC), blk(pool_a, kt, TC), abc_a[:, 0:TC], ALU.mult)))(),
                    waits=[eba], sem="sV")
                ev_ln1 = P.step("vector", (lambda kt=kt: (lambda e: e.tensor_tensor(
                    blk(ln1x, kt, TC), blk(ln1x, kt, TC), abc_b[:, 0:TC], ALU.subtract)))(),
                    waits=[ebb], sem="sV")
            abc_guard = ev_ln1

            ev_t1 = []
            for part in range(2):
                src_hbm = wz_b if part else wbr_b
                evl = P.step("sync", (lambda src_hbm=src_hbm: (lambda e: e.dma_start(
                    out=wwin[:].rearrange("p (kt j) -> p kt j", kt=KT),
                    in_=src_hbm.rearrange("(kt p) j -> p kt j", p=128))))(),
                    waits=[ev_w["wz" if part else "wbr"], wwin_guard], sem="sWW", k=16)
                for j in range(IT):
                    jj = part * IT + j
                    bank = [p0, p1][jj % 2]
                    w0l = [ev_ln1, evl, pg.get("p%d" % (jj % 2))]
                    evm = None
                    for kt in range(KT):
                        evm = P.step("tensor", (lambda bank=bank, kt=kt, j=j: (lambda e: e.matmul(
                            bank[:, 0:TC], wwin[:, kt * DI + j * 128:kt * DI + (j + 1) * 128],
                            blk(ln1x, kt, TC), start=(kt == 0), stop=(kt == KT - 1))))(),
                            waits=(w0l if kt == 0 else []), sem="sT" if kt == KT - 1 else None)
                    if part:
                        eve = P.step("scalar", (lambda bank=bank, j=j: (lambda e: e.activation(
                            out=blk(szp, j, TC), in_=bank[:, 0:TC], func=AF.Silu)))(),
                            waits=[evm], sem="sS")
                    else:
                        off = j * (TC + 8) + 3
                        eve = P.step("scalar", (lambda bank=bank, off=off: (lambda e: e.copy(
                            out=xbrg[:, off:off + TC], in_=bank[:, 0:TC])))(),
                            waits=[evm, prev_halo[j], prev_g[j]], sem="sS")
                    pg["p%d" % (jj % 2)] = eve
                    ev_t1.append(eve)
                    wwin_guard = evm

            ev_conv = []
            for i in range(IT):
                off = i * (TC + 8)
                e0 = P.step("vector", (lambda i=i, off=off: (lambda e: e.tensor_scalar_mul(
                    blk(xcp, i, TC), xbrg[:, off:off + TC], cwcol[:, i * 4:i * 4 + 1])))(),
                    waits=[ev_t1[i], (prev_halo[i] if c > 0 else ev_zero)], sem="sV")
                for tau in range(1, 4):
                    e0 = P.step("vector", (lambda i=i, off=off, tau=tau: (lambda e: e.scalar_tensor_tensor(
                        blk(xcp, i, TC), xbrg[:, off + tau:off + tau + TC],
                        cwcol[:, i * 4 + tau:i * 4 + tau + 1],
                        blk(xcp, i, TC), ALU.mult, ALU.add)))(), sem="sV")
                halo_src_guard[i] = e0
                evs = P.step("scalar", (lambda i=i: (lambda e: e.activation(
                    out=blk(xcp, i, TC), in_=blk(xcp, i, TC), func=AF.Silu,
                    bias=cbcol[:, i:i + 1])))(), waits=[e0], sem="sS")
                ev_conv.append(evs)
                evh = P.step("scalar", (lambda off=off: (lambda e: e.copy(
                    out=xbrg[:, off:off + 3], in_=xbrg[:, off + TC:off + TC + 3])))(),
                    waits=[e0], sem="sS")
                prev_halo[i] = evh
            pump(2)

            evm = None
            for kt in range(IT):
                wl = [ev_conv[kt], ev_w["wx"]]
                if kt == 0 and p2_guard:
                    wl.append(p2_guard)
                evm = P.step("tensor", (lambda kt=kt: (lambda e: e.matmul(
                    p2[0:33, 0:TC], wx_s[:, kt * 33:(kt + 1) * 33], blk(xcp, kt, TC),
                    start=(kt == 0), stop=(kt == IT - 1))))(),
                    waits=wl, sem="sT" if kt == IT - 1 else None)
            ev_sevac = P.step("vector", lambda e: e.tensor_copy(ssm_s[:, 0:TC], p2[0:33, 0:TC]),
                              waits=[evm, ev_ssm_out_prev], sem="sV")
            p2_guard = ev_sevac
            ev_sout = P.step("gpsimd", (lambda c=c: (lambda e: e.dma_start(
                out=ssm_i[c], in_=ssm_s[:, 0:TC])))(), waits=[ev_sevac], sem="sSSM", k=16)
            ev_ssm_out_prev = ev_sout
            ev_ar1 = P.step("gpsimd", (lambda c=c: (lambda e: e.collective_compute(
                "AllReduce", ALU.add, replica_groups=GROUPS,
                ins=[ssm_i[c]], outs=[ssm_o[c]])))(), waits=[ev_sout], sem="sCCS")
            ev_cbc = P.step("gpsimd", (lambda c=c: (lambda e: e.dma_start(
                out=cbc[:, 0:TC], in_=ssm_o[c][32:33, :].squeeze(0).partition_broadcast(128))))(),
                waits=[ev_ar1, ev_dt_prev_last], sem="sCB", k=16)
            pump(2)

            for i in range(IT):
                P.step("scalar", (lambda i=i: (lambda e: e.activation(
                    out=blk(dtp, i, TC), in_=cbc[:, 0:TC], func=AF.Identity,
                    bias=dtbcol[:, i:i + 1])))(),
                    waits=[ev_cbc, ev_wpl[i], prev_g[i]], sem="sS")
                P.step("scalar", (lambda i=i: (lambda e: e.activation(
                    out=blk(yp, i, TC), in_=blk(dtp, i, TC), func=AF.Abs)))(), sem="sS")
                P.step("scalar", (lambda i=i: (lambda e: e.activation(
                    out=blk(yp, i, TC), in_=blk(yp, i, TC), func=AF.Exp, scale=-1.0)))(), sem="sS")
                P.step("scalar", (lambda i=i: (lambda e: e.activation(
                    out=blk(yp, i, TC), in_=blk(yp, i, TC), func=AF.Ln,
                    bias=onef[:, 0:1])))(), waits=[ev_onef], sem="sS")
                ev_relu = P.step("scalar", (lambda i=i: (lambda e: e.activation(
                    out=blk(dtp, i, TC), in_=blk(dtp, i, TC), func=AF.Relu)))(), sem="sS")
                ev_dt[i] = P.step("vector", (lambda i=i: (lambda e: e.tensor_tensor(
                    blk(dtp, i, TC), blk(dtp, i, TC), blk(yp, i, TC), ALU.add)))(),
                    waits=[ev_relu], sem="sV")
                ev_wpl[i] = P.step("vector", (lambda i=i: (lambda e: e.tensor_tensor(
                    blk(wpl, i, TC), blk(dtp, i, TC), blk(xcp, i, TC), ALU.mult)))(),
                    waits=[ev_conv[i]], sem="sV")
            ev_dt_prev_last = ev_dt[IT - 1]

            ev_y = [None] * IT
            for qg in range(NQ):
                qpar = qg % 2
                evb = P.step("gpsimd", (lambda c=c, qg=qg, qpar=qpar: (lambda e: e.dma_start(
                    out=bqt[:, qpar * QS * TC:(qpar + 1) * QS * TC].rearrange("p (s t) -> p s t", s=QS),
                    in_=ssm_o[c][qg * QS:(qg + 1) * QS, :].partition_broadcast(128))))(),
                    waits=[ev_ar1, bq_user.get(qpar)], sem="sBQ%d" % qpar, k=16)
                evcq = P.step("gpsimd", (lambda c=c, qg=qg, qpar=qpar: (lambda e: e.dma_start(
                    out=cqt[:, qpar * QS * TC:(qpar + 1) * QS * TC].rearrange("p (s t) -> p s t", s=QS),
                    in_=ssm_o[c][S + qg * QS:S + (qg + 1) * QS, :].partition_broadcast(128))))(),
                    waits=[ev_ar1, cq_user.get(qpar)], sem="sCQ%d" % qpar, k=16)
                last_u = None
                for i in range(IT):
                    it_par = (qg * IT + i) % 2
                    hbase = it_par * QS * TC
                    ev_exps = []
                    for s4 in range(QS):
                        sidx = qg * QS + s4
                        ev_exps.append(P.step("scalar",
                            (lambda i=i, s4=s4, sidx=sidx, it_par=it_par: (lambda e: e.activation(
                                out=dA_q[:, it_par * QS * TC + s4 * TC:it_par * QS * TC + (s4 + 1) * TC],
                                in_=blk(dtp, i, TC), func=AF.Exp,
                                scale=acol[:, i * S + sidx:i * S + sidx + 1])))(),
                            waits=[ev_dt[i], ev_acol, dA_user.get(it_par)], sem="sS"))
                    wrep = blk(wpl, i, TC).unsqueeze(1).broadcast_to((128, QS, TC))
                    evu = P.step("vector", (lambda qpar=qpar, wrep=wrep: (lambda e: e.tensor_tensor(
                        u_q[:].rearrange("p (s t) -> p s t", s=QS), wrep,
                        bqt[:, qpar * QS * TC:(qpar + 1) * QS * TC].rearrange("p (s t) -> p s t", s=QS),
                        ALU.mult)))(),
                        waits=[ev_wpl[i], evb, uq_user, t3_uq_guard[0], t3_uq_guard[1], t3_uq_guard[2], t3_uq_guard[3]], sem="sV")
                    ev_sc = None
                    for s4 in range(QS):
                        sidx = qg * QS + s4
                        ini = 0.0 if c == 0 else hcar[:, i * S + sidx:i * S + sidx + 1]
                        wl = [evu, ev_exps[s4], hq_user.get(it_par), prev_carry.get(("hp", it_par))]
                        if c > 0:
                            wl.append(prev_carry.get((i, qg)))
                        ev_sc = P.step("vector",
                            (lambda s4=s4, hbase=hbase, ini=ini: (lambda e: e.tensor_tensor_scan(
                                h_q[:, hbase + s4 * TC:hbase + (s4 + 1) * TC],
                                dA_q[:, hbase + s4 * TC:hbase + (s4 + 1) * TC],
                                u_q[:, s4 * TC:(s4 + 1) * TC], ini, ALU.mult, ALU.add)))(),
                            waits=wl, sem="sV")
                    prev_scan[(i, qg)] = ev_sc
                    dA_user[it_par] = ev_sc
                    evcar = P.step("scalar", (lambda i=i, qg=qg, hbase=hbase: (lambda e: e.copy(
                        out=hcar[:, i * S + qg * QS:i * S + (qg + 1) * QS],
                        in_=h_q[:, hbase:hbase + QS * TC].rearrange("p (s t) -> p s t", s=QS)[:, :, TC - 1])))(),
                        waits=[ev_sc], sem="sS")
                    prev_carry[(i, qg)] = evcar
                    prev_carry[("hp", it_par)] = evcar
                    evm1 = P.step("vector", (lambda qpar=qpar, hbase=hbase: (lambda e: e.tensor_tensor(
                        u_q[:], h_q[:, hbase:hbase + QS * TC],
                        cqt[:, qpar * QS * TC:(qpar + 1) * QS * TC], ALU.mult)))(),
                        waits=[evcq, ev_sc], sem="sV")
                    hq_user[it_par] = evm1
                    P.step("vector", lambda e: e.tensor_tensor(
                        u_q[:, 0:2 * TC], u_q[:, 0:2 * TC], u_q[:, 2 * TC:4 * TC], ALU.add), sem="sV")
                    if qg == 0:
                        evy = P.step("vector", (lambda i=i: (lambda e: e.tensor_tensor(
                            blk(yp, i, TC), u_q[:, 0:TC], u_q[:, TC:2 * TC], ALU.add)))(), sem="sV")
                    else:
                        P.step("vector", lambda e: e.tensor_tensor(
                            u_q[:, 0:TC], u_q[:, 0:TC], u_q[:, TC:2 * TC], ALU.add), sem="sV")
                        evy = P.step("vector", (lambda i=i: (lambda e: e.tensor_tensor(
                            blk(yp, i, TC), blk(yp, i, TC), u_q[:, 0:TC], ALU.add)))(), sem="sV")
                    ev_y[i] = evy
                    uq_user = evy
                    last_u = evu
                    pump(1)
                bq_user[qpar] = last_u
                cq_user[qpar] = evm1

            ev_g_new = []
            for i in range(IT):
                off = i * (TC + 8) + 3
                P.step("vector", (lambda i=i: (lambda e: e.scalar_tensor_tensor(
                    blk(pool_c, i, TC), blk(xcp, i, TC), dcol[:, i:i + 1],
                    blk(yp, i, TC), ALU.mult, ALU.add)))(),
                    waits=[ev_y[i], ev_small], sem="sV")
                e2 = P.step("vector", (lambda i=i, off=off: (lambda e: e.tensor_tensor(
                    xbrg[:, off:off + TC], blk(pool_c, i, TC), blk(szp, i, TC), ALU.mult)))(),
                    waits=[ev_t1[IT + i], prev_halo[i]], sem="sV")
                ev_g_new.append(e2)
                prev_g[i] = e2
            pump(2)

            evlw = P.step("sync", lambda e: e.dma_start(
                out=wwin[:].rearrange("p (kt j) -> p kt j", kt=IT),
                in_=wout_b.rearrange("(kt p) j -> p kt j", p=128)),
                waits=[ev_w["wout"], wwin_guard], sem="sWW", k=16)
            ev_t3d = []
            for j in range(IT):
                bank = [p5, p6][j % 2]
                evm = None
                for kt in range(IT):
                    off = kt * (TC + 8) + 3
                    wl = [ev_g_new[kt]]
                    if kt == 0:
                        wl += [evlw, pg.get("p%d" % (5 + j % 2))]
                    evm = P.step("tensor", (lambda bank=bank, kt=kt, j=j, off=off: (lambda e: e.matmul(
                        bank[:, 0:TC], wwin[:, kt * DM + j * 128:kt * DM + (j + 1) * 128],
                        xbrg[:, off:off + TC], start=(kt == 0), stop=(kt == IT - 1))))(),
                        waits=wl, sem="sT" if kt == IT - 1 else None)
                wwin_guard = evm
                uqb = u_q[:, (j % 4) * TC:((j % 4) + 1) * TC]
                eve3 = P.step("scalar", (lambda bank=bank, uqb=uqb: (lambda e: e.copy(
                    out=uqb, in_=bank[:, 0:TC])))(),
                    waits=[evm, t3_uq_guard[j % 4]], sem="sS")
                evd = P.step("sync", (lambda c=c, j=j, uqb=uqb: (lambda e: e.dma_start(
                    out=wo_i[c][j * 128:(j + 1) * 128, :], in_=uqb)))(),
                    waits=[eve3], sem="sT3%d" % (j % 4), k=16)
                pg["p%d" % (5 + j % 2)] = eve3
                t3_uq_guard[j % 4] = evd
                ev_t3d.append(evd)
            ev_ar2 = P.step("gpsimd", (lambda c=c: (lambda e: e.collective_compute(
                "AllReduce", ALU.add, replica_groups=GROUPS,
                ins=[wo_i[c]], outs=[wo_o[c]])))(), waits=ev_t3d, sem="sCCW")
            pump(4)

            ev_xm = P.step("gpsimd", (lambda c=c: (lambda e: e.dma_start(
                out=pool_a[:].rearrange("p (kt t) -> p kt t", kt=KT),
                in_=wo_o[c].rearrange("(kt p) t -> p kt t", p=128))))(),
                waits=[ev_ar2, ev_ln1], sem="sXM", k=16)
            ev_sq2 = []
            for kt in range(KT):
                ev_sq2.append(P.step("scalar", (lambda kt=kt: (lambda e: e.activation(
                    out=blk(pool_b, kt, TC), in_=blk(pool_a, kt, TC), func=AF.Square)))(),
                    waits=[ev_xm], sem="sS"))
            evA = evB = None
            for kt in range(KT):
                evA = P.step("tensor", mm_stats(p3, (lambda kt=kt: lambda: blk(pool_a, kt, TC))(), kt, kt == 0, kt == KT - 1),
                             waits=([ev_xm, p34_guard] if kt == 0 else []),
                             sem="sT" if kt == KT - 1 else None)
            for kt in range(KT):
                evB = P.step("tensor", mm_stats(p4, (lambda kt=kt: lambda: blk(pool_b, kt, TC))(), kt, kt == 0, kt == KT - 1),
                             waits=[ev_sq2[kt]], sem="sT" if kt == KT - 1 else None)
            eba2, ebb2 = ln_rows(NCH + c, TC)
            ev_xr = P.step("gpsimd", (lambda xsrc=xsrc: (lambda e: e.dma_start(
                out=pool_c[:].rearrange("p (kt t) -> p kt t", kt=KT), in_=xsrc)))(),
                waits=[ev_g_new[IT - 1]], sem="sXR", k=16)
            ev_x1c = []
            for kt in range(KT):
                P.step("vector", (lambda kt=kt: (lambda e: e.tensor_tensor(
                    blk(pool_b, kt, TC), blk(pool_a, kt, TC), abc_a[:, 0:TC], ALU.mult)))(),
                    waits=[eba2], sem="sV")
                P.step("vector", (lambda kt=kt: (lambda e: e.tensor_tensor(
                    blk(pool_b, kt, TC), blk(pool_b, kt, TC), abc_b[:, 0:TC], ALU.subtract)))(),
                    waits=[ebb2], sem="sV")
                ev_x1c.append(P.step("vector", (lambda kt=kt: (lambda e: e.tensor_tensor(
                    blk(pool_b, kt, TC), blk(pool_b, kt, TC), blk(pool_c, kt, TC), ALU.add)))(),
                    waits=[ev_xr], sem="sV"))
            abc_guard = ev_x1c[-1]
            drain()
            mc = 0 if c % 2 == 0 else 2
            ev_blend = None
            for kt in range(KT):
                P.step("vector", (lambda kt=kt, mc=mc: (lambda e: e.tensor_scalar_mul(
                    blk(pool_c, kt, TC), blk(x1b, kt, TC),
                    mcols[:, mc + 1:mc + 2])))(),
                    waits=[ev_x1c[kt], ev_ffn_slotbuf], sem="sV")
                ev_blend = P.step("vector", (lambda kt=kt, mc=mc: (lambda e: e.scalar_tensor_tensor(
                    blk(x1b, kt, TC),
                    blk(pool_b, kt, TC), mcols[:, mc:mc + 1],
                    blk(pool_c, kt, TC), ALU.mult, ALU.add)))(), sem="sV")
            chunk_done = ev_blend

            if c % 2 == 1:
                pend[0] = emit_ffn(kslot, ev_blend)

        drain()
        P.step("sync", lambda e: e.nop(), waits=[("sST", P.sems["sST"][1])])

        with nc.Block() as b:
            @b.sync
            def _(e):
                P.run("sync", e)

            @b.gpsimd
            def _(e):
                P.run("gpsimd", e)

            @b.scalar
            def _(e):
                P.run("scalar", e)

            @b.vector
            def _(e):
                P.run("vector", e)

            @b.tensor
            def _(e):
                P.run("tensor", e)

    return nc


def kernel(L0_attn_x, L0_attn_W_in, L0_attn_conv_w, L0_attn_conv_b, L0_attn_W_x,
           L0_attn_A_log, L0_attn_D_param, L0_attn_dt_bias, L0_attn_W_out,
           L0_ffn_k0, L0_ffn_k1, L0_ffn_k2, L0_ffn_s0, L0_ffn_s1, L0_ffn_s2):
    x = np.asarray(L0_attn_x, dtype=np.float32)
    W_in = np.asarray(L0_attn_W_in, dtype=np.float32)
    conv_w = np.asarray(L0_attn_conv_w, dtype=np.float32)
    conv_b = np.asarray(L0_attn_conv_b, dtype=np.float32)
    W_x = np.asarray(L0_attn_W_x, dtype=np.float32)
    A_log = np.asarray(L0_attn_A_log, dtype=np.float32)
    D_param = np.asarray(L0_attn_D_param, dtype=np.float32)
    dt_bias = np.asarray(L0_attn_dt_bias, dtype=np.float32)
    W_out = np.asarray(L0_attn_W_out, dtype=np.float32)
    k0 = np.ascontiguousarray(np.asarray(L0_ffn_k0)).view(np.uint8)
    k1 = np.ascontiguousarray(np.asarray(L0_ffn_k1)).view(np.uint8)
    k2 = np.ascontiguousarray(np.asarray(L0_ffn_k2)).view(np.uint8)
    s0 = np.asarray(L0_ffn_s0, dtype=np.float32).reshape(1, 1)
    s1 = np.asarray(L0_ffn_s1, dtype=np.float32).reshape(1, 1)
    s2 = np.asarray(L0_ffn_s2, dtype=np.float32).reshape(1, 1)

    if "nc" not in _NC_CACHE:
        _NC_CACHE["nc"] = build_nc()
    nc = _NC_CACHE["nc"]
    in_maps = make_in_maps(x, W_in, conv_w, conv_b, W_x, A_log, D_param,
                           dt_bias, W_out, k0, k1, k2, s0, s1, s2)

    res = run_bass_kernel_spmd(nc, in_maps, core_ids=list(range(8)))

    out = np.empty((4, NT, DM), np.float32)
    for c in range(8):
        b, dh = c // 2, c % 2
        oT = res.results[c]["outT"]
        for k in range(4):
            ch = 2 * k + dh
            out[b, ch * TC:(ch + 1) * TC, :] = oT[:, k * TC:(k + 1) * TC].T
    return out


def make_in_maps(x, W_in, conv_w, conv_b, W_x, A_log, D_param, dt_bias, W_out,
                 k0, k1, k2, s0, s1, s2):
    in_maps = []
    for c in range(8):
        b, dh = c // 2, c % 2
        sl = slice(dh * DI, (dh + 1) * DI)
        im = dict(
            xT=np.ascontiguousarray(x[b].T),
            wbr=np.ascontiguousarray(W_in[:, dh * DI:(dh + 1) * DI]),
            wz=np.ascontiguousarray(W_in[:, 2048 + dh * DI:2048 + (dh + 1) * DI]),
            wx=np.ascontiguousarray(W_x[sl]),
            convw=np.ascontiguousarray(conv_w[sl, 0, :]),
            convb=np.ascontiguousarray(conv_b[sl].reshape(DI, 1)),
            alog=np.ascontiguousarray(A_log[sl]),
            dpar=np.ascontiguousarray(D_param[sl].reshape(DI, 1)),
            dtb=np.ascontiguousarray(dt_bias[sl].reshape(DI, 1)),
            wout=np.ascontiguousarray(W_out[sl]),
            k0t=np.ascontiguousarray(k0[:DM]),
            k0b=np.ascontiguousarray(k0[DM:2 * DM]),
            k1u=k1, k2u=k2,
            s0i=s0, s1i=s1, s2i=s2,
            m_e=np.full((128, 1), 1.0 if dh == 0 else 0.0, np.float32),
            minv_e=np.full((128, 1), 0.0 if dh == 0 else 1.0, np.float32),
            m_o=np.full((128, 1), 1.0 if dh == 1 else 0.0, np.float32),
            minv_o=np.full((128, 1), 0.0 if dh == 1 else 1.0, np.float32),
            ones_bf=np.ones((128, 1), dtype=BF),
            neghalf=np.full((1, 128), -0.5, dtype=BF),
        )
        in_maps.append(im)
    return in_maps


if __name__ == "__main__":
    build_nc()
    print("built ok")


# version 05
# speedup vs baseline: 1.1077x; 1.0503x over previous
import numpy as np
import ml_dtypes
from contextlib import ExitStack
import concourse.bass as bass
import concourse.mybir as mybir
from concourse.bass_utils import run_bass_kernel_spmd

F32 = mybir.dt.float32
BF16 = mybir.dt.bfloat16
U8 = mybir.dt.uint8
AF = mybir.ActivationFunctionType
ALU = mybir.AluOpType
BF = ml_dtypes.bfloat16

DM = 1024
DI = 1024
NT = 4096
TC = 512
NCH = NT // TC
S = 16
QS = 4
NQ = S // QS
KT = DM // 128
IT = DI // 128
HTOK = TC // 2
LNEPS = 1e-5
GROUPS = [[0, 1], [2, 3], [4, 5], [6, 7]]

_NC_CACHE = {}


class Plan:
    ENGS = ("sync", "gpsimd", "scalar", "vector", "tensor")

    def __init__(self):
        self.items = {e: [] for e in self.ENGS}
        self.sems = {}
        self.waited = {e: {} for e in self.ENGS}

    def add_sem(self, name, handle):
        self.sems[name] = [handle, 0]

    def ensure(self, name, nc, stack):
        if name not in self.sems:
            self.add_sem(name, stack.enter_context(nc.semaphore(name)))

    def step(self, eng, fn, waits=(), sem=None, k=1):
        ws = []
        for w in waits:
            if w is None:
                continue
            sn, val = w
            if val <= 0:
                continue
            if val > self.waited[eng].get(sn, 0):
                self.waited[eng][sn] = val
                ws.append((self.sems[sn][0], val))
        ev = None
        if sem is not None:
            self.sems[sem][1] += k
            ev = (sem, self.sems[sem][1])
        self.items[eng].append((fn, tuple(ws), None if sem is None else self.sems[sem][0], k))
        return ev

    def run(self, eng_name, eng):
        for fn, ws, semh, k in self.items[eng_name]:
            for (h, v) in ws:
                eng.wait_ge(h, v)
            ins = fn(eng)
            if semh is not None:
                ins.then_inc(semh, k)


def build_nc():
    nc = bass.Bass("TRN2", target_bir_lowering=False, detect_race_conditions=False)

    def inp(name, shape, dt):
        return nc.dram_tensor(name, list(shape), dt, kind="ExternalInput").ap()

    xT = inp("xT", (DM, NT), F32)
    wbr = inp("wbr", (DM, DI), F32)
    wz = inp("wz", (DM, DI), F32)
    wx = inp("wx", (DI, 33), F32)
    convw = inp("convw", (DI, 4), F32)
    convb = inp("convb", (DI, 1), F32)
    alog = inp("alog", (DI, S), F32)
    dpar = inp("dpar", (DI, 1), F32)
    dtb = inp("dtb", (DI, 1), F32)
    wout = inp("wout", (DI, DM), F32)
    k0t = inp("k0t", (DM, 4096), U8)
    k0b = inp("k0b", (DM, 4096), U8)
    k1u = inp("k1u", (4096, 4096), U8)
    k2u = inp("k2u", (4096, DM), U8)
    s0i = inp("s0i", (1, 1), F32)
    s1i = inp("s1i", (1, 1), F32)
    s2i = inp("s2i", (1, 1), F32)
    m_e = inp("m_e", (128, 1), F32)
    minv_e = inp("minv_e", (128, 1), F32)
    m_o = inp("m_o", (128, 1), F32)
    minv_o = inp("minv_o", (128, 1), F32)
    ones_bf = inp("ones_bf", (128, 1), BF16)
    neghalf = inp("neghalf", (1, 128), BF16)

    outT = nc.dram_tensor("outT", [DM, NT // 2], F32, kind="ExternalOutput").ap()

    def ihb(name, shape, dt):
        return nc.dram_tensor(name, list(shape), dt, kind="Internal").ap()

    w0e = ihb("w0e", (KT, 128, 4096), BF16)
    k1b = ihb("k1b", (4096, 4096), BF16)
    k2b = ihb("k2b", (4096, DM), BF16)
    wbr_b = ihb("wbr_b", (DM, DI), BF16)
    wz_b = ihb("wz_b", (DM, DI), BF16)
    wout_b = ihb("wout_b", (DI, DM), BF16)
    ssm_i = ihb("ssm_i", (NCH, 33, TC), BF16)
    ssm_o = ihb("ssm_o", (NCH, 33, TC), BF16)
    wo_i = ihb("wo_i", (NCH, DM, TC), BF16)
    wo_o = ihb("wo_o", (NCH, DM, TC), BF16)
    lnr = ihb("lnr", (4 * NCH, 2, TC), F32)

    st = ExitStack()
    sb = lambda n, shp, dt: st.enter_context(nc.sbuf_tensor(n, list(shp), dt))
    ps = lambda n: st.enter_context(nc.psum_tensor(n, [128, 512], F32))

    with st:
        wwin = sb("wwin", (128, KT * DI), BF16)
        wx_s = sb("wx_s", (128, IT * 33), BF16)
        acol = sb("acol", (128, IT * S), F32)
        cwcol = sb("cwcol", (128, IT * 4), F32)
        cbcol = sb("cbcol", (128, IT), F32)
        dtbcol = sb("dtbcol", (128, IT), F32)
        dcol = sb("dcol", (128, IT), F32)
        scols = sb("scols", (128, 4), F32)
        onec = sb("onec", (128, 1), BF16)
        nhrow = sb("nhrow", (1, 128), BF16)
        mcols = sb("mcols", (128, 4), F32)
        epsc = sb("epsc", (128, 1), F32)
        onef = sb("onef", (128, 1), F32)

        ln1x = sb("ln1x", (128, KT * TC), BF16)
        xcp = sb("xcp", (128, IT * TC), BF16)
        szp = sb("szp", (128, IT * TC), BF16)
        xbrg = sb("xbrg", (128, IT * (TC + 8)), BF16)
        dtp = sb("dtp", (128, IT * TC), BF16)
        wpl = sb("wpl", (128, IT * TC), BF16)
        yp = sb("yp", (128, IT * TC), BF16)
        x1b = sb("x1b", (128, KT * TC), BF16)
        pool_a = sb("pool_a", (128, KT * TC), BF16)
        pool_b = sb("pool_b", (128, KT * TC), BF16)

        u_q = sb("u_q", (128, QS * TC), BF16)
        dA_q = sb("dA_q", (128, 2 * QS * TC), BF16)
        h_q = sb("h_q", (128, 2 * QS * TC), BF16)
        hcar = sb("hcar", (128, IT * S), F32)
        bqt = sb("bqt", (128, 2 * QS * TC), BF16)
        cqt = sb("cqt", (128, 2 * QS * TC), BF16)

        rows = sb("rows", (1, 3 * TC), F32)
        abc_a = sb("abc_a", (128, TC), BF16)
        abc_b = sb("abc_b", (128, TC), BF16)
        ssm_s = sb("ssm_s", (33, TC), BF16)
        cbc = sb("cbc", (128, TC), BF16)

        f1 = sb("f1", (128, 32 * TC), BF16)
        f2h = sb("f2h", (128, 4 * TC), BF16)
        ln3 = sb("ln3", (128, KT * TC), BF16)
        f3 = sb("f3", (128, KT * TC), BF16)
        kwin = sb("kwin", (128, 2 * 1024), BF16)
        rshl = sb("rshl", (1, 4 * TC), BF16)
        x2t = sb("x2t", (128, TC), F32)

        p0, p1, p2, p3, p4, p5, p6, p7 = (ps(f"p{i}") for i in range(8))

        P = Plan()
        _orig_step = P.step
        def _step(eng, fn, waits=(), sem=None, k=1):
            if sem is not None:
                P.ensure(sem, nc, st)
            return _orig_step(eng, fn, waits, sem, k)
        P.step = _step
        for sname in ("sV", "sS", "sT"):
            P.ensure(sname, nc, st)

        def blk(t, i, w):
            return t[:, i * w:(i + 1) * w]

        ev_w = {}
        P.step("gpsimd", lambda e: e.dma_start(out=wbr_b, in_=wbr), sem="sWI", k=16)
        P.step("gpsimd", lambda e: e.dma_start(out=wz_b, in_=wz), sem="sWI", k=16)
        P.step("gpsimd", lambda e: e.dma_start(out=wout_b, in_=wout), sem="sWI", k=16)
        evwi = P.step("gpsimd", lambda e: e.dma_start(
            out=wx_s[:].rearrange("p (kt j) -> p kt j", kt=IT),
            in_=wx.rearrange("(kt p) j -> p kt j", p=128)), sem="sWI", k=16)
        ev_w["wbr"] = ev_w["wz"] = ev_w["wout"] = ev_w["wx"] = evwi

        def ld_cols(dst, src, n):
            return lambda e: e.dma_start(
                out=dst.rearrange("p (kt j) -> p kt j", kt=IT),
                in_=src.rearrange("(kt p) j -> p kt j", p=128),
                allow_slow_non_contiguous=True)

        P.step("sync", ld_cols(acol[:], alog, S), sem="sSM", k=16)
        P.step("sync", ld_cols(cwcol[:], convw, 4), sem="sSM", k=16)
        P.step("sync", ld_cols(cbcol[:], convb, 1), sem="sSM", k=16)
        P.step("sync", ld_cols(dtbcol[:], dtb, 1), sem="sSM", k=16)
        P.step("sync", ld_cols(dcol[:], dpar, 1), sem="sSM", k=16)
        P.step("sync", lambda e: e.dma_start(out=onec[:], in_=ones_bf), sem="sSM", k=16)
        P.step("sync", lambda e: e.dma_start(out=nhrow[:], in_=neghalf), sem="sSM", k=16)
        P.step("sync", lambda e: e.dma_start(out=mcols[:, 0:1], in_=m_e), sem="sSM", k=16)
        P.step("sync", lambda e: e.dma_start(out=mcols[:, 1:2], in_=minv_e), sem="sSM", k=16)
        P.step("sync", lambda e: e.dma_start(out=mcols[:, 2:3], in_=m_o), sem="sSM", k=16)
        ev_small = P.step("sync", lambda e: e.dma_start(out=mcols[:, 3:4], in_=minv_o), sem="sSM", k=16)
        ev_alog = ev_small

        P.step("gpsimd", lambda e: e.dma_start(
            out=scols[:, 0:1], in_=s0i[0:1, :].squeeze(0).partition_broadcast(128)), sem="sSC", k=16)
        P.step("gpsimd", lambda e: e.dma_start(
            out=scols[:, 1:2], in_=s1i[0:1, :].squeeze(0).partition_broadcast(128)), sem="sSC", k=16)
        ev_s2x = P.step("gpsimd", lambda e: e.dma_start(
            out=scols[:, 2:3], in_=s2i[0:1, :].squeeze(0).partition_broadcast(128)), sem="sSC", k=16)
        ev_s0 = ev_s1 = ev_s2 = ev_s2x
        P.step("vector", lambda e: e.tensor_scalar_mul(scols[:, 0:1], scols[:, 0:1], 2.0),
               waits=[ev_s0], sem="sV")
        P.step("vector", lambda e: e.tensor_scalar_mul(scols[:, 1:2], scols[:, 1:2], 4.0),
               waits=[ev_s1], sem="sV")
        ev_scl = P.step("vector", lambda e: e.tensor_scalar_mul(scols[:, 2:3], scols[:, 2:3], 4.0),
                        waits=[ev_s2], sem="sV")
        ev_ax = P.step("scalar", lambda e: e.activation(out=acol[:], in_=acol[:], func=AF.Exp),
                       waits=[ev_alog], sem="sS")
        ev_acol = P.step("vector", lambda e: e.tensor_scalar_mul(acol[:], acol[:], -1.0),
                         waits=[ev_ax], sem="sV")
        for i in range(IT):
            off = i * (TC + 8)
            P.step("vector", (lambda off=off: (lambda e: e.memset(xbrg[:, off:off + 8], 0.0)))(), sem="sV")
        P.step("vector", lambda e: e.memset(x1b[:], 0.0), sem="sV")
        ev_zero = P.step("vector", lambda e: e.memset(hcar[:], 0.0), sem="sV")
        ev_eps = P.step("vector", lambda e: e.memset(epsc[:], LNEPS), sem="sV")
        ev_onef = P.step("vector", lambda e: e.memset(onef[:], 1.0), sem="sV")

        ev_k1b = ev_k2b = None
        for q in range(4):
            ev_k1b = P.step("gpsimd", (lambda q=q: (lambda e: e.dma_start(
                out=k1b[q * 1024:(q + 1) * 1024, :], in_=k1u[q * 1024:(q + 1) * 1024, :])))(),
                sem="sK1", k=16)
        for q in range(2):
            ev_k2b = P.step("gpsimd", (lambda q=q: (lambda e: e.dma_start(
                out=k2b[q * 2048:(q + 1) * 2048, :], in_=k2u[q * 2048:(q + 1) * 2048, :])))(),
                sem="sK2", k=16)
        f1u8 = f1[:].bitcast(U8)
        prev_sub = None
        prev_store = None
        ev_stage_done = None
        for kt in range(KT):
            wl = [prev_sub] if prev_sub else []
            eva = P.step("sync", (lambda kt=kt: (lambda e: e.dma_start(
                out=f1u8[:, 0:4096], in_=k0t[kt * 128:(kt + 1) * 128, :])))(),
                waits=wl, sem="sK0A", k=16)
            evb = P.step("sync", (lambda kt=kt: (lambda e: e.dma_start(
                out=f1u8[:, 4096:8192], in_=k0b[kt * 128:(kt + 1) * 128, :])))(),
                waits=wl, sem="sK0B", k=16)
            evs = P.step("vector", lambda e: e.tensor_tensor(
                f1[:, 4096:8192], f1u8[:, 0:4096], f1u8[:, 4096:8192], ALU.subtract),
                waits=[eva, evb] + ([prev_store] if prev_store else []), sem="sV")
            evd = P.step("gpsimd", (lambda kt=kt: (lambda e: e.dma_start(
                out=w0e[kt], in_=f1[:, 4096:8192])))(),
                waits=[evs], sem="sSTG", k=16)
            prev_sub, prev_store = evs, evd
            ev_stage_done = evd
        ev_f1_free = ev_stage_done

        pg = {}
        wwin_guard = None
        p2_guard = None
        p34_guard = None
        ln_dma_guard = None
        abc_guard = None
        prev_halo = [None] * IT
        halo_src_guard = [None] * IT
        prev_g = [None] * IT
        prev_carry = {}
        prev_scan = {}
        hq_user = {}
        dA_user = {}
        uq_user = None
        bq_user = {}
        cq_user = {}
        ev_dt = [None] * IT
        ev_wpl = [None] * IT
        ev_ssm_out_prev = None
        t3_uq_guard = {0: None, 1: None, 2: None, 3: None}
        ev_dt_prev_last = None
        chunk_done = None
        ev_ffn_slotbuf = None
        f2h_guard = None
        x2t_guard = None
        rsf_guard = None
        kwin_user = {0: None, 1: None}
        ev_stats_ready = None

        def mm_stats(dst, getsrc, kt, first, last, N=TC):
            return lambda e: e.matmul(dst[0:1, 0:N], onec[:, 0:1], getsrc(),
                                      start=first, stop=last)

        def ln_rows(idx, w):
            nonlocal ln_dma_guard, p34_guard, abc_guard
            r_mu = rows[0:1, 0:w]
            r_v = rows[0:1, TC:TC + w]
            r_a = rows[0:1, 2 * TC:2 * TC + w]
            r_b = r_v
            inv = 1.0 / DM
            wl = [("sT", P.sems["sT"][1])]
            if ln_dma_guard:
                wl.append(ln_dma_guard)
            e1 = P.step("vector", lambda e: e.tensor_scalar_mul(r_mu, p3[0:1, 0:w], inv),
                        waits=wl, sem="sV")
            e2 = P.step("vector", lambda e: e.tensor_scalar_mul(r_v, p4[0:1, 0:w], inv),
                        sem="sV")
            P.step("vector", lambda e: e.tensor_tensor(r_a, r_mu, r_mu, ALU.mult), sem="sV")
            ev_v = P.step("vector", lambda e: e.tensor_tensor(r_v, r_v, r_a, ALU.subtract), sem="sV")
            P.step("scalar", lambda e: e.activation(out=r_v, in_=r_v, func=AF.Ln, bias=epsc[0:1, 0:1]),
                   waits=[ev_v, ev_eps], sem="sS")
            es = P.step("scalar", lambda e: e.activation(out=r_a, in_=r_v, func=AF.Exp, scale=-0.5),
                        sem="sS")
            ea = es
            eb = P.step("vector", lambda e: e.tensor_tensor(r_b, r_mu, r_a, ALU.mult),
                        waits=[es], sem="sV")
            ed0 = P.step("sync", lambda e: e.dma_start(out=lnr[idx, 0:1, 0:w], in_=r_b),
                         waits=[ea, eb], sem="sRWB", k=16)
            ed1 = P.step("sync", lambda e: e.dma_start(out=lnr[idx, 1:2, 0:w], in_=r_a),
                         waits=[ea, eb], sem="sRWA", k=16)
            gl = [abc_guard] if abc_guard else []
            eba = P.step("gpsimd", lambda e: e.dma_start(
                out=abc_a[:, 0:w], in_=lnr[idx, 1:2, 0:w].squeeze(0).partition_broadcast(128)),
                waits=[ed1] + gl, sem="sABA", k=16)
            ebb = P.step("gpsimd", lambda e: e.dma_start(
                out=abc_b[:, 0:w], in_=lnr[idx, 0:1, 0:w].squeeze(0).partition_broadcast(128)),
                waits=[ed0] + gl, sem="sABB", k=16)
            ln_dma_guard = ed0
            p34_guard = e2 if e2 else e1
            return eba, ebb

        ev_out_final = None
        pend = [None]

        def pump(n=1):
            g = pend[0]
            if g is None:
                return
            for _ in range(n):
                try:
                    next(g)
                except StopIteration:
                    pend[0] = None
                    return

        def drain():
            g = pend[0]
            if g is not None:
                for _ in g:
                    pass
                pend[0] = None

        def emit_ffn(k, ev_blend):
            nonlocal rsf_guard, x2t_guard, abc_guard, f2h_guard, ev_ffn_slotbuf
            x1s = x1b
            bkey = ["p0", "p1", "p5", "p7"]
            banks4 = [p0, p1, p5, p7]
            ev_q3 = []
            for kt in range(KT):
                ev_q3.append(P.step("scalar", (lambda kt=kt: (lambda e: e.activation(
                    out=blk(f1, kt, TC), in_=blk(x1s, kt, TC), func=AF.Square)))(),
                    waits=[ev_blend, f2h_guard], sem="sS"))
            evA = evB = None
            for kt in range(KT):
                evA = P.step("tensor", mm_stats(p3, (lambda kt=kt: lambda: blk(x1s, kt, TC))(), kt, kt == 0, kt == KT - 1),
                             waits=([ev_blend, p34_guard] if kt == 0 else []),
                             sem="sT" if kt == KT - 1 else None)
            for kt in range(KT):
                evB = P.step("tensor", mm_stats(p4, (lambda kt=kt: lambda: blk(f1, kt, TC))(), kt, kt == 0, kt == KT - 1),
                             waits=[ev_q3[kt]], sem="sT" if kt == KT - 1 else None)
            eba3, ebb3 = ln_rows(2 * NCH + k, TC)
            ev_ln3 = None
            for kt in range(KT):
                P.step("vector", (lambda kt=kt: (lambda e: e.tensor_tensor(
                    blk(ln3, kt, TC), blk(x1s, kt, TC), abc_a[:, 0:TC], ALU.mult)))(),
                    waits=[eba3], sem="sV")
                ev_ln3 = P.step("vector", (lambda kt=kt: (lambda e: e.tensor_tensor(
                    blk(ln3, kt, TC), blk(ln3, kt, TC), abc_b[:, 0:TC], ALU.subtract)))(),
                    waits=[ebb3], sem="sV")
            abc_guard = ev_ln3
            f1_stats_done = evB
            yield

            ev_f1 = [None] * 32
            for mg in range(8):
                for kq in range(4):
                    wpar = kq % 2
                    kw = kwin[:, wpar * 1024:(wpar + 1) * 1024]
                    evl_k = P.step("sync", (lambda mg=mg, kq=kq, kw=kw: (lambda e: e.dma_start(
                        out=kw.rearrange("p (kt j) -> p kt j", kt=2),
                        in_=w0e[kq * 2:(kq + 1) * 2, :, mg * 512:(mg + 1) * 512].rearrange("kt p j -> p kt j"))))(),
                        waits=[ev_stage_done, kwin_user[wpar]], sem="sKW%d" % wpar, k=16)
                    evl = {kq: evl_k}
                    for mt in range(4):
                        m = mg * 4 + mt
                        bank = banks4[mt]
                        for kt in range(2):
                            wl = []
                            if kt == 0:
                                wl = [evl[kq], ev_ln3, ev_f1_free]
                                if kq == 0:
                                    wl.append(pg.get(bkey[mt]))
                                    if mg == 0:
                                        wl.append(f1_stats_done)
                            evm = P.step("tensor", (lambda bank=bank, kw=kw, kt=kt, kq=kq, mt=mt: (lambda e: e.matmul(
                                bank[:, 0:TC],
                                kw[:, kt * 512 + mt * 128:kt * 512 + (mt + 1) * 128],
                                blk(ln3, kq * 2 + kt, TC),
                                start=(kq == 0 and kt == 0), stop=(kq == 3 and kt == 1))))(),
                                waits=wl, sem="sT" if kt == 1 else None)
                        if kq == 3:
                            eve = P.step("scalar", (lambda bank=bank, m=m: (lambda e: e.activation(
                                out=blk(f1, m, TC), in_=bank[:, 0:TC], func=AF.Relu,
                                scale=scols[:, 0:1])))(),
                                waits=[evm, ev_scl, f1_stats_done], sem="sS")
                            ev_f1[m] = eve
                            pg[bkey[mt]] = eve
                    kwin_user[wpar] = evm
                yield
            evr = None
            for m in range(32):
                wl = [ev_f1[m]]
                if m == 0:
                    wl.append(pg.get("p6"))
                evr = P.step("tensor", (lambda m=m: (lambda e: e.matmul(
                    p6[0:1, 0:TC], onec[:, 0:1], blk(f1, m, TC),
                    start=(m == 0), stop=(m == 31))))(),
                    waits=wl, sem="sT" if m == 31 else None)
            ev_hi = P.step("scalar", lambda e: e.copy(
                out=rshl[0:1, 0:TC], in_=p6[0:1, 0:TC]), waits=[evr, rsf_guard], sem="sS")
            ev_rs1 = P.step("vector", lambda e: e.tensor_tensor(
                rshl[0:1, TC:2 * TC], p6[0:1, 0:TC], rshl[0:1, 0:TC], ALU.subtract),
                waits=[ev_hi], sem="sV")
            pg["p6"] = ev_rs1
            rsf_guard = ev_rs1
            yield

            ev_f3 = [None] * KT
            f2_prev = [f1_stats_done] * 4
            for mh in range(8):
                ev_f2 = [None] * 4
                evm_fin = [None] * 4
                for kq in range(8):
                    pass
                for kq in range(16):
                    wpar = kq % 2
                    kw = kwin[:, wpar * 1024:(wpar + 1) * 1024]
                    evl = P.step("sync", (lambda kq=kq, mh=mh, kw=kw: (lambda e: e.dma_start(
                        out=kw.rearrange("p (kt j) -> p kt j", kt=2),
                        in_=k1b[kq * 256:(kq + 1) * 256, mh * 512:(mh + 1) * 512]
                        .rearrange("(kt p) j -> p kt j", p=128))))(),
                        waits=[ev_k1b, kwin_user[wpar]], sem="sKW%d" % wpar, k=16)
                    for mt in range(4):
                        bank = banks4[mt]
                        if kq == 0:
                            P.step("tensor", (lambda bank=bank: (lambda e: e.matmul(
                                bank[:, 0:TC], nhrow[0:1, :],
                                rshl[0:1, 0:TC], start=True, stop=False)))(),
                                waits=[ev_rs1, pg.get(bkey[mt]), f2_prev[mt]])
                            P.step("tensor", (lambda bank=bank: (lambda e: e.matmul(
                                bank[:, 0:TC], nhrow[0:1, :],
                                rshl[0:1, TC:2 * TC], start=False, stop=False)))())
                        evm = None
                        for kt8 in range(2):
                            evm = P.step("tensor", (lambda bank=bank, kw=kw, kt8=kt8, mt=mt, kq=kq: (lambda e: e.matmul(
                                bank[:, 0:TC],
                                kw[:, kt8 * 512 + mt * 128:kt8 * 512 + (mt + 1) * 128],
                                blk(f1, kq * 2 + kt8, TC),
                                start=False, stop=(kq == 15 and kt8 == 1))))(),
                                waits=([evl] if kt8 == 0 else []),
                                sem="sT" if kt8 == 1 else None)
                        evm_fin[mt] = evm
                    kwin_user[wpar] = evm_fin[3]
                for mt in range(4):
                    ev_f2[mt] = P.step("scalar", (lambda bank=banks4[mt], mt=mt: (lambda e: e.activation(
                        out=blk(f2h, mt, TC), in_=bank[:, 0:TC], func=AF.Relu,
                        scale=scols[:, 1:2])))(),
                        waits=[evm_fin[mt], f2_prev[mt]], sem="sS")
                    pg[bkey[mt]] = ev_f2[mt]
                yield
                evr = None
                for mt in range(4):
                    wl = [ev_f2[mt]]
                    if mt == 0:
                        wl.append(pg.get("p6"))
                    evr = P.step("tensor", (lambda mt=mt: (lambda e: e.matmul(
                        p6[0:1, 0:TC], onec[:, 0:1], blk(f2h, mt, TC),
                        start=(mt == 0), stop=(mt == 3))))(),
                        waits=wl, sem="sT" if mt == 3 else None)
                ev_hi2 = P.step("scalar", lambda e: e.copy(
                    out=rshl[0:1, 2 * TC:3 * TC], in_=p6[0:1, 0:TC]),
                    waits=[evr, rsf_guard], sem="sS")
                ev_rs2 = P.step("vector", lambda e: e.tensor_tensor(
                    rshl[0:1, 3 * TC:4 * TC], p6[0:1, 0:TC], rshl[0:1, 2 * TC:3 * TC], ALU.subtract),
                    waits=[ev_hi2], sem="sV")
                pg["p6"] = ev_rs2
                rsf_guard = ev_rs2
                for jg in range(2):
                    evm_fin = [None] * 4
                    for kq in range(2):
                        wpar = kq % 2
                        kw = kwin[:, wpar * 1024:(wpar + 1) * 1024]
                        krow = mh * 512 + kq * 256
                        evl = P.step("sync", (lambda krow=krow, jg=jg, kw=kw: (lambda e: e.dma_start(
                            out=kw.rearrange("p (kt j) -> p kt j", kt=2),
                            in_=k2b[krow:krow + 256, jg * 512:(jg + 1) * 512]
                            .rearrange("(kt p) j -> p kt j", p=128))))(),
                            waits=[ev_k2b, kwin_user[wpar]], sem="sKW%d" % wpar, k=16)
                        for jt in range(4):
                            j = jg * 4 + jt
                            bank = banks4[jt]
                            if kq == 0:
                                P.step("tensor", (lambda bank=bank: (lambda e: e.matmul(
                                    bank[:, 0:TC], nhrow[0:1, :],
                                    rshl[0:1, 2 * TC:3 * TC], start=True, stop=False)))(),
                                    waits=[ev_rs2, pg.get(bkey[jt])])
                                rsk1 = P.step("tensor", (lambda bank=bank: (lambda e: e.matmul(
                                    bank[:, 0:TC], nhrow[0:1, :],
                                    rshl[0:1, 3 * TC:4 * TC], start=False, stop=False)))(),
                                    sem="sT")
                                rsf_guard = rsk1
                            evm = None
                            for kt8 in range(2):
                                evm = P.step("tensor", (lambda bank=bank, kw=kw, kt8=kt8, jt=jt, kq=kq: (lambda e: e.matmul(
                                    bank[:, 0:TC],
                                    kw[:, kt8 * 512 + jt * 128:kt8 * 512 + (jt + 1) * 128],
                                    blk(f2h, kq * 2 + kt8, TC),
                                    start=False, stop=(kq == 1 and kt8 == 1))))(),
                                    waits=([evl] if kt8 == 0 else []),
                                    sem="sT" if kt8 == 1 else None)
                            evm_fin[jt] = evm
                        kwin_user[wpar] = evm_fin[3]
                    for jt in range(4):
                        j = jg * 4 + jt
                        if mh == 0:
                            ev_f3[j] = P.step("scalar", (lambda bank=banks4[jt], j=j: (lambda e: e.activation(
                                out=blk(f3, j, TC), in_=bank[:, 0:TC],
                                func=AF.Identity, scale=scols[:, 2:3])))(),
                                waits=[evm_fin[jt]], sem="sS")
                        else:
                            ev_f3[j] = P.step("vector", (lambda bank=banks4[jt], j=j: (lambda e: e.scalar_tensor_tensor(
                                blk(f3, j, TC), bank[:, 0:TC],
                                scols[:, 2:3], blk(f3, j, TC), ALU.mult, ALU.add)))(),
                                waits=[evm_fin[jt], ("sS", P.sems["sS"][1])], sem="sV")
                        pg[bkey[jt]] = ev_f3[j]
                    yield
                f2_prev = [evm_fin[jt] for jt in range(4)]
            f2h_guard = ev_f3[-1]
            ev_q4 = []
            for kt in range(KT):
                ev_q4.append(P.step("scalar", (lambda kt=kt: (lambda e: e.activation(
                    out=blk(f1, kt, TC), in_=blk(f3, kt, TC), func=AF.Square)))(),
                    waits=[ev_f3[kt], ("sT", P.sems["sT"][1])], sem="sS"))
            evA = evB = None
            for kt in range(KT):
                evA = P.step("tensor", mm_stats(p3, (lambda kt=kt: lambda: blk(f3, kt, TC))(), kt, kt == 0, kt == KT - 1),
                             waits=([ev_f3[kt], p34_guard] if kt == 0 else [ev_f3[kt]]),
                             sem="sT" if kt == KT - 1 else None)
            for kt in range(KT):
                evB = P.step("tensor", mm_stats(p4, (lambda kt=kt: lambda: blk(f1, kt, TC))(), kt, kt == 0, kt == KT - 1),
                             waits=[ev_q4[kt]], sem="sT" if kt == KT - 1 else None)
            eba4, ebb4 = ln_rows(3 * NCH + k, TC)
            last_norm = None
            yield
            for kt in range(KT):
                P.step("vector", (lambda kt=kt: (lambda e: e.tensor_tensor(
                    x2t[:, 0:TC], blk(f3, kt, TC), abc_a[:, 0:TC], ALU.mult)))(),
                    waits=[eba4, x2t_guard], sem="sV")
                P.step("vector", (lambda kt=kt: (lambda e: e.tensor_tensor(
                    x2t[:, 0:TC], x2t[:, 0:TC], abc_b[:, 0:TC], ALU.subtract)))(),
                    waits=[ebb4], sem="sV")
                evx2 = P.step("vector", (lambda kt=kt: (lambda e: e.tensor_tensor(
                    x2t[:, 0:TC], x2t[:, 0:TC], blk(x1s, kt, TC), ALU.add)))(), sem="sV")
                last_norm = evx2
                evst = P.step("sync", (lambda kt=kt, k=k: (lambda e: e.dma_start(
                    out=outT[kt * 128:(kt + 1) * 128, k * TC:(k + 1) * TC],
                    in_=x2t[:, 0:TC])))(),
                    waits=[evx2], sem="sST", k=16)
                ev_out_final = evst
                x2t_guard = evst
            abc_guard = last_norm
            ev_ffn_slotbuf = last_norm

        for c in range(NCH):
            kslot = c // 2
            xsrc = xT[:, c * TC:(c + 1) * TC].rearrange("(kt p) t -> p kt t", p=128)

            ev_xl = P.step("gpsimd", (lambda xsrc=xsrc: (lambda e: e.dma_start(
                out=pool_a[:].rearrange("p (kt t) -> p kt t", kt=KT), in_=xsrc)))(),
                waits=[chunk_done], sem="sXL", k=16)
            ev_sq = []
            for kt in range(KT):
                ev_sq.append(P.step("scalar", (lambda kt=kt: (lambda e: e.activation(
                    out=blk(pool_b, kt, TC), in_=blk(pool_a, kt, TC), func=AF.Square)))(),
                    waits=[ev_xl], sem="sS"))
            evA = evB = None
            for kt in range(KT):
                evA = P.step("tensor", mm_stats(p3, (lambda kt=kt: lambda: blk(pool_a, kt, TC))(), kt, kt == 0, kt == KT - 1),
                             waits=([ev_xl, ev_small, p34_guard] if kt == 0 else []),
                             sem="sT" if kt == KT - 1 else None)
            for kt in range(KT):
                evB = P.step("tensor", mm_stats(p4, (lambda kt=kt: lambda: blk(pool_b, kt, TC))(), kt, kt == 0, kt == KT - 1),
                             waits=[ev_sq[kt]], sem="sT" if kt == KT - 1 else None)
            eba, ebb = ln_rows(c, TC)
            ev_ln1 = None
            for kt in range(KT):
                P.step("vector", (lambda kt=kt: (lambda e: e.tensor_tensor(
                    blk(ln1x, kt, TC), blk(pool_a, kt, TC), abc_a[:, 0:TC], ALU.mult)))(),
                    waits=[eba], sem="sV")
                ev_ln1 = P.step("vector", (lambda kt=kt: (lambda e: e.tensor_tensor(
                    blk(ln1x, kt, TC), blk(ln1x, kt, TC), abc_b[:, 0:TC], ALU.subtract)))(),
                    waits=[ebb], sem="sV")
            abc_guard = ev_ln1

            ev_t1 = []
            for part in range(2):
                src_hbm = wz_b if part else wbr_b
                evl = P.step("sync", (lambda src_hbm=src_hbm: (lambda e: e.dma_start(
                    out=wwin[:].rearrange("p (kt j) -> p kt j", kt=KT),
                    in_=src_hbm.rearrange("(kt p) j -> p kt j", p=128))))(),
                    waits=[ev_w["wz" if part else "wbr"], wwin_guard], sem="sWW", k=16)
                for j in range(IT):
                    jj = part * IT + j
                    bank = [p0, p1][jj % 2]
                    w0l = [ev_ln1, evl, pg.get("p%d" % (jj % 2))]
                    evm = None
                    for kt in range(KT):
                        evm = P.step("tensor", (lambda bank=bank, kt=kt, j=j: (lambda e: e.matmul(
                            bank[:, 0:TC], wwin[:, kt * DI + j * 128:kt * DI + (j + 1) * 128],
                            blk(ln1x, kt, TC), start=(kt == 0), stop=(kt == KT - 1))))(),
                            waits=(w0l if kt == 0 else []), sem="sT" if kt == KT - 1 else None)
                    if part:
                        eve = P.step("scalar", (lambda bank=bank, j=j: (lambda e: e.activation(
                            out=blk(szp, j, TC), in_=bank[:, 0:TC], func=AF.Silu)))(),
                            waits=[evm], sem="sS")
                    else:
                        off = j * (TC + 8) + 3
                        eve = P.step("scalar", (lambda bank=bank, off=off: (lambda e: e.copy(
                            out=xbrg[:, off:off + TC], in_=bank[:, 0:TC])))(),
                            waits=[evm, prev_halo[j], prev_g[j]], sem="sS")
                    pg["p%d" % (jj % 2)] = eve
                    ev_t1.append(eve)
                    wwin_guard = evm

            ev_conv = []
            for i in range(IT):
                off = i * (TC + 8)
                e0 = P.step("vector", (lambda i=i, off=off: (lambda e: e.tensor_scalar_mul(
                    blk(xcp, i, TC), xbrg[:, off:off + TC], cwcol[:, i * 4:i * 4 + 1])))(),
                    waits=[ev_t1[i], (prev_halo[i] if c > 0 else ev_zero)], sem="sV")
                for tau in range(1, 4):
                    e0 = P.step("vector", (lambda i=i, off=off, tau=tau: (lambda e: e.scalar_tensor_tensor(
                        blk(xcp, i, TC), xbrg[:, off + tau:off + tau + TC],
                        cwcol[:, i * 4 + tau:i * 4 + tau + 1],
                        blk(xcp, i, TC), ALU.mult, ALU.add)))(), sem="sV")
                halo_src_guard[i] = e0
                evs = P.step("scalar", (lambda i=i: (lambda e: e.activation(
                    out=blk(xcp, i, TC), in_=blk(xcp, i, TC), func=AF.Silu,
                    bias=cbcol[:, i:i + 1])))(), waits=[e0], sem="sS")
                ev_conv.append(evs)
                evh = P.step("scalar", (lambda off=off: (lambda e: e.copy(
                    out=xbrg[:, off:off + 3], in_=xbrg[:, off + TC:off + TC + 3])))(),
                    waits=[e0], sem="sS")
                prev_halo[i] = evh
            pump(2)

            evm = None
            for kt in range(IT):
                wl = [ev_conv[kt], ev_w["wx"]]
                if kt == 0 and p2_guard:
                    wl.append(p2_guard)
                evm = P.step("tensor", (lambda kt=kt: (lambda e: e.matmul(
                    p2[0:33, 0:TC], wx_s[:, kt * 33:(kt + 1) * 33], blk(xcp, kt, TC),
                    start=(kt == 0), stop=(kt == IT - 1))))(),
                    waits=wl, sem="sT" if kt == IT - 1 else None)
            ev_sevac = P.step("vector", lambda e: e.tensor_copy(ssm_s[:, 0:TC], p2[0:33, 0:TC]),
                              waits=[evm, ev_ssm_out_prev], sem="sV")
            p2_guard = ev_sevac
            ev_sout = P.step("gpsimd", (lambda c=c: (lambda e: e.dma_start(
                out=ssm_i[c], in_=ssm_s[:, 0:TC])))(), waits=[ev_sevac], sem="sSSM", k=16)
            ev_ssm_out_prev = ev_sout
            ev_ar1 = P.step("gpsimd", (lambda c=c: (lambda e: e.collective_compute(
                "AllReduce", ALU.add, replica_groups=GROUPS,
                ins=[ssm_i[c]], outs=[ssm_o[c]])))(), waits=[ev_sout], sem="sCCS")
            ev_cbc = P.step("gpsimd", (lambda c=c: (lambda e: e.dma_start(
                out=cbc[:, 0:TC], in_=ssm_o[c][32:33, :].squeeze(0).partition_broadcast(128))))(),
                waits=[ev_ar1, ev_dt_prev_last], sem="sCB", k=16)
            pump(2)

            for i in range(IT):
                P.step("scalar", (lambda i=i: (lambda e: e.activation(
                    out=blk(dtp, i, TC), in_=cbc[:, 0:TC], func=AF.Identity,
                    bias=dtbcol[:, i:i + 1])))(),
                    waits=[ev_cbc, ev_wpl[i], prev_g[i], chunk_done], sem="sS")
                P.step("scalar", (lambda i=i: (lambda e: e.activation(
                    out=blk(yp, i, TC), in_=blk(dtp, i, TC), func=AF.Abs)))(), sem="sS")
                P.step("scalar", (lambda i=i: (lambda e: e.activation(
                    out=blk(yp, i, TC), in_=blk(yp, i, TC), func=AF.Exp, scale=-1.0)))(), sem="sS")
                P.step("scalar", (lambda i=i: (lambda e: e.activation(
                    out=blk(yp, i, TC), in_=blk(yp, i, TC), func=AF.Ln,
                    bias=onef[:, 0:1])))(), waits=[ev_onef], sem="sS")
                ev_relu = P.step("scalar", (lambda i=i: (lambda e: e.activation(
                    out=blk(dtp, i, TC), in_=blk(dtp, i, TC), func=AF.Relu)))(), sem="sS")
                ev_dt[i] = P.step("vector", (lambda i=i: (lambda e: e.tensor_tensor(
                    blk(dtp, i, TC), blk(dtp, i, TC), blk(yp, i, TC), ALU.add)))(),
                    waits=[ev_relu], sem="sV")
                ev_wpl[i] = P.step("vector", (lambda i=i: (lambda e: e.tensor_tensor(
                    blk(wpl, i, TC), blk(dtp, i, TC), blk(xcp, i, TC), ALU.mult)))(),
                    waits=[ev_conv[i]], sem="sV")
            ev_dt_prev_last = ev_dt[IT - 1]

            ev_y = [None] * IT
            for qg in range(NQ):
                qpar = qg % 2
                evb = P.step("gpsimd", (lambda c=c, qg=qg, qpar=qpar: (lambda e: e.dma_start(
                    out=bqt[:, qpar * QS * TC:(qpar + 1) * QS * TC].rearrange("p (s t) -> p s t", s=QS),
                    in_=ssm_o[c][qg * QS:(qg + 1) * QS, :].partition_broadcast(128))))(),
                    waits=[ev_ar1, bq_user.get(qpar)], sem="sBQ%d" % qpar, k=16)
                evcq = P.step("gpsimd", (lambda c=c, qg=qg, qpar=qpar: (lambda e: e.dma_start(
                    out=cqt[:, qpar * QS * TC:(qpar + 1) * QS * TC].rearrange("p (s t) -> p s t", s=QS),
                    in_=ssm_o[c][S + qg * QS:S + (qg + 1) * QS, :].partition_broadcast(128))))(),
                    waits=[ev_ar1, cq_user.get(qpar)], sem="sCQ%d" % qpar, k=16)
                last_u = None
                for i in range(IT):
                    it_par = (qg * IT + i) % 2
                    hbase = it_par * QS * TC
                    ev_exps = []
                    for s4 in range(QS):
                        sidx = qg * QS + s4
                        ev_exps.append(P.step("scalar",
                            (lambda i=i, s4=s4, sidx=sidx, it_par=it_par: (lambda e: e.activation(
                                out=dA_q[:, it_par * QS * TC + s4 * TC:it_par * QS * TC + (s4 + 1) * TC],
                                in_=blk(dtp, i, TC), func=AF.Exp,
                                scale=acol[:, i * S + sidx:i * S + sidx + 1])))(),
                            waits=[ev_dt[i], ev_acol, dA_user.get(it_par)], sem="sS"))
                    wrep = blk(wpl, i, TC).unsqueeze(1).broadcast_to((128, QS, TC))
                    evu = P.step("vector", (lambda qpar=qpar, wrep=wrep: (lambda e: e.tensor_tensor(
                        u_q[:].rearrange("p (s t) -> p s t", s=QS), wrep,
                        bqt[:, qpar * QS * TC:(qpar + 1) * QS * TC].rearrange("p (s t) -> p s t", s=QS),
                        ALU.mult)))(),
                        waits=[ev_wpl[i], evb, uq_user, t3_uq_guard[0], t3_uq_guard[1], t3_uq_guard[2], t3_uq_guard[3]], sem="sV")
                    ev_sc = None
                    for s4 in range(QS):
                        sidx = qg * QS + s4
                        ini = 0.0 if c == 0 else hcar[:, i * S + sidx:i * S + sidx + 1]
                        wl = [evu, ev_exps[s4], hq_user.get(it_par), prev_carry.get(("hp", it_par))]
                        if c > 0:
                            wl.append(prev_carry.get((i, qg)))
                        ev_sc = P.step("vector",
                            (lambda s4=s4, hbase=hbase, ini=ini: (lambda e: e.tensor_tensor_scan(
                                h_q[:, hbase + s4 * TC:hbase + (s4 + 1) * TC],
                                dA_q[:, hbase + s4 * TC:hbase + (s4 + 1) * TC],
                                u_q[:, s4 * TC:(s4 + 1) * TC], ini, ALU.mult, ALU.add)))(),
                            waits=wl, sem="sV")
                    prev_scan[(i, qg)] = ev_sc
                    dA_user[it_par] = ev_sc
                    evcar = P.step("scalar", (lambda i=i, qg=qg, hbase=hbase: (lambda e: e.copy(
                        out=hcar[:, i * S + qg * QS:i * S + (qg + 1) * QS],
                        in_=h_q[:, hbase:hbase + QS * TC].rearrange("p (s t) -> p s t", s=QS)[:, :, TC - 1])))(),
                        waits=[ev_sc], sem="sS")
                    prev_carry[(i, qg)] = evcar
                    prev_carry[("hp", it_par)] = evcar
                    evm1 = P.step("vector", (lambda qpar=qpar, hbase=hbase: (lambda e: e.tensor_tensor(
                        u_q[:], h_q[:, hbase:hbase + QS * TC],
                        cqt[:, qpar * QS * TC:(qpar + 1) * QS * TC], ALU.mult)))(),
                        waits=[evcq, ev_sc], sem="sV")
                    hq_user[it_par] = evm1
                    P.step("vector", lambda e: e.tensor_tensor(
                        u_q[:, 0:2 * TC], u_q[:, 0:2 * TC], u_q[:, 2 * TC:4 * TC], ALU.add), sem="sV")
                    if qg == 0:
                        evy = P.step("vector", (lambda i=i: (lambda e: e.tensor_tensor(
                            blk(yp, i, TC), u_q[:, 0:TC], u_q[:, TC:2 * TC], ALU.add)))(), sem="sV")
                    else:
                        P.step("vector", lambda e: e.tensor_tensor(
                            u_q[:, 0:TC], u_q[:, 0:TC], u_q[:, TC:2 * TC], ALU.add), sem="sV")
                        evy = P.step("vector", (lambda i=i: (lambda e: e.tensor_tensor(
                            blk(yp, i, TC), blk(yp, i, TC), u_q[:, 0:TC], ALU.add)))(), sem="sV")
                    ev_y[i] = evy
                    uq_user = evy
                    last_u = evu
                    pump(1)
                bq_user[qpar] = last_u
                cq_user[qpar] = evm1

            ev_g_new = []
            for i in range(IT):
                off = i * (TC + 8) + 3
                P.step("vector", (lambda i=i: (lambda e: e.scalar_tensor_tensor(
                    u_q[:, 0:TC], blk(xcp, i, TC), dcol[:, i:i + 1],
                    blk(yp, i, TC), ALU.mult, ALU.add)))(),
                    waits=[ev_y[i], ev_small], sem="sV")
                e2 = P.step("vector", (lambda i=i, off=off: (lambda e: e.tensor_tensor(
                    xbrg[:, off:off + TC], u_q[:, 0:TC], blk(szp, i, TC), ALU.mult)))(),
                    waits=[ev_t1[IT + i], prev_halo[i]], sem="sV")
                ev_g_new.append(e2)
                prev_g[i] = e2
            pump(2)

            evlw = P.step("sync", lambda e: e.dma_start(
                out=wwin[:].rearrange("p (kt j) -> p kt j", kt=IT),
                in_=wout_b.rearrange("(kt p) j -> p kt j", p=128)),
                waits=[ev_w["wout"], wwin_guard], sem="sWW", k=16)
            ev_t3d = []
            for j in range(IT):
                bank = [p5, p6][j % 2]
                evm = None
                for kt in range(IT):
                    off = kt * (TC + 8) + 3
                    wl = [ev_g_new[kt]]
                    if kt == 0:
                        wl += [evlw, pg.get("p%d" % (5 + j % 2))]
                    evm = P.step("tensor", (lambda bank=bank, kt=kt, j=j, off=off: (lambda e: e.matmul(
                        bank[:, 0:TC], wwin[:, kt * DM + j * 128:kt * DM + (j + 1) * 128],
                        xbrg[:, off:off + TC], start=(kt == 0), stop=(kt == IT - 1))))(),
                        waits=wl, sem="sT" if kt == IT - 1 else None)
                wwin_guard = evm
                uqb = u_q[:, (j % 4) * TC:((j % 4) + 1) * TC]
                eve3 = P.step("scalar", (lambda bank=bank, uqb=uqb: (lambda e: e.copy(
                    out=uqb, in_=bank[:, 0:TC])))(),
                    waits=[evm, t3_uq_guard[j % 4]], sem="sS")
                evd = P.step("sync", (lambda c=c, j=j, uqb=uqb: (lambda e: e.dma_start(
                    out=wo_i[c][j * 128:(j + 1) * 128, :], in_=uqb)))(),
                    waits=[eve3], sem="sT3%d" % (j % 4), k=16)
                pg["p%d" % (5 + j % 2)] = eve3
                t3_uq_guard[j % 4] = evd
                ev_t3d.append(evd)
            ev_ar2 = P.step("gpsimd", (lambda c=c: (lambda e: e.collective_compute(
                "AllReduce", ALU.add, replica_groups=GROUPS,
                ins=[wo_i[c]], outs=[wo_o[c]])))(), waits=ev_t3d, sem="sCCW")
            pump(4)

            ev_xm = P.step("gpsimd", (lambda c=c: (lambda e: e.dma_start(
                out=pool_a[:].rearrange("p (kt t) -> p kt t", kt=KT),
                in_=wo_o[c].rearrange("(kt p) t -> p kt t", p=128))))(),
                waits=[ev_ar2, ev_ln1], sem="sXM", k=16)
            ev_sq2 = []
            for kt in range(KT):
                ev_sq2.append(P.step("scalar", (lambda kt=kt: (lambda e: e.activation(
                    out=blk(pool_b, kt, TC), in_=blk(pool_a, kt, TC), func=AF.Square)))(),
                    waits=[ev_xm], sem="sS"))
            evA = evB = None
            for kt in range(KT):
                evA = P.step("tensor", mm_stats(p3, (lambda kt=kt: lambda: blk(pool_a, kt, TC))(), kt, kt == 0, kt == KT - 1),
                             waits=([ev_xm, p34_guard] if kt == 0 else []),
                             sem="sT" if kt == KT - 1 else None)
            for kt in range(KT):
                evB = P.step("tensor", mm_stats(p4, (lambda kt=kt: lambda: blk(pool_b, kt, TC))(), kt, kt == 0, kt == KT - 1),
                             waits=[ev_sq2[kt]], sem="sT" if kt == KT - 1 else None)
            eba2, ebb2 = ln_rows(NCH + c, TC)
            ev_xr = P.step("gpsimd", (lambda xsrc=xsrc: (lambda e: e.dma_start(
                out=szp[:].rearrange("p (kt t) -> p kt t", kt=KT), in_=xsrc)))(),
                waits=[ev_g_new[IT - 1]], sem="sXR", k=16)
            ev_x1c = []
            for kt in range(KT):
                P.step("vector", (lambda kt=kt: (lambda e: e.tensor_tensor(
                    blk(pool_b, kt, TC), blk(pool_a, kt, TC), abc_a[:, 0:TC], ALU.mult)))(),
                    waits=[eba2], sem="sV")
                P.step("vector", (lambda kt=kt: (lambda e: e.tensor_tensor(
                    blk(pool_b, kt, TC), blk(pool_b, kt, TC), abc_b[:, 0:TC], ALU.subtract)))(),
                    waits=[ebb2], sem="sV")
                ev_x1c.append(P.step("vector", (lambda kt=kt: (lambda e: e.tensor_tensor(
                    blk(pool_b, kt, TC), blk(pool_b, kt, TC), blk(szp, kt, TC), ALU.add)))(),
                    waits=[ev_xr], sem="sV"))
            abc_guard = ev_x1c[-1]
            drain()
            mc = 0 if c % 2 == 0 else 2
            ev_blend = None
            for kt in range(KT):
                P.step("vector", (lambda kt=kt, mc=mc: (lambda e: e.tensor_scalar_mul(
                    blk(yp, kt, TC), blk(x1b, kt, TC),
                    mcols[:, mc + 1:mc + 2])))(),
                    waits=[ev_x1c[kt], ev_ffn_slotbuf], sem="sV")
                ev_blend = P.step("vector", (lambda kt=kt, mc=mc: (lambda e: e.scalar_tensor_tensor(
                    blk(x1b, kt, TC),
                    blk(pool_b, kt, TC), mcols[:, mc:mc + 1],
                    blk(yp, kt, TC), ALU.mult, ALU.add)))(), sem="sV")
            chunk_done = ev_blend

            if c % 2 == 1:
                pend[0] = emit_ffn(kslot, ev_blend)

        drain()
        P.step("sync", lambda e: e.nop(), waits=[("sST", P.sems["sST"][1])])

        with nc.Block() as b:
            @b.sync
            def _(e):
                P.run("sync", e)

            @b.gpsimd
            def _(e):
                P.run("gpsimd", e)

            @b.scalar
            def _(e):
                P.run("scalar", e)

            @b.vector
            def _(e):
                P.run("vector", e)

            @b.tensor
            def _(e):
                P.run("tensor", e)

    return nc


def kernel(L0_attn_x, L0_attn_W_in, L0_attn_conv_w, L0_attn_conv_b, L0_attn_W_x,
           L0_attn_A_log, L0_attn_D_param, L0_attn_dt_bias, L0_attn_W_out,
           L0_ffn_k0, L0_ffn_k1, L0_ffn_k2, L0_ffn_s0, L0_ffn_s1, L0_ffn_s2):
    x = np.asarray(L0_attn_x, dtype=np.float32)
    W_in = np.asarray(L0_attn_W_in, dtype=np.float32)
    conv_w = np.asarray(L0_attn_conv_w, dtype=np.float32)
    conv_b = np.asarray(L0_attn_conv_b, dtype=np.float32)
    W_x = np.asarray(L0_attn_W_x, dtype=np.float32)
    A_log = np.asarray(L0_attn_A_log, dtype=np.float32)
    D_param = np.asarray(L0_attn_D_param, dtype=np.float32)
    dt_bias = np.asarray(L0_attn_dt_bias, dtype=np.float32)
    W_out = np.asarray(L0_attn_W_out, dtype=np.float32)
    k0 = np.ascontiguousarray(np.asarray(L0_ffn_k0)).view(np.uint8)
    k1 = np.ascontiguousarray(np.asarray(L0_ffn_k1)).view(np.uint8)
    k2 = np.ascontiguousarray(np.asarray(L0_ffn_k2)).view(np.uint8)
    s0 = np.asarray(L0_ffn_s0, dtype=np.float32).reshape(1, 1)
    s1 = np.asarray(L0_ffn_s1, dtype=np.float32).reshape(1, 1)
    s2 = np.asarray(L0_ffn_s2, dtype=np.float32).reshape(1, 1)

    if "nc" not in _NC_CACHE:
        _NC_CACHE["nc"] = build_nc()
    nc = _NC_CACHE["nc"]
    in_maps = make_in_maps(x, W_in, conv_w, conv_b, W_x, A_log, D_param,
                           dt_bias, W_out, k0, k1, k2, s0, s1, s2)

    res = run_bass_kernel_spmd(nc, in_maps, core_ids=list(range(8)))

    out = np.empty((4, NT, DM), np.float32)
    for c in range(8):
        b, dh = c // 2, c % 2
        oT = res.results[c]["outT"]
        for k in range(4):
            ch = 2 * k + dh
            out[b, ch * TC:(ch + 1) * TC, :] = oT[:, k * TC:(k + 1) * TC].T
    return out


def make_in_maps(x, W_in, conv_w, conv_b, W_x, A_log, D_param, dt_bias, W_out,
                 k0, k1, k2, s0, s1, s2):
    in_maps = []
    for c in range(8):
        b, dh = c // 2, c % 2
        sl = slice(dh * DI, (dh + 1) * DI)
        im = dict(
            xT=np.ascontiguousarray(x[b].T),
            wbr=np.ascontiguousarray(W_in[:, dh * DI:(dh + 1) * DI]),
            wz=np.ascontiguousarray(W_in[:, 2048 + dh * DI:2048 + (dh + 1) * DI]),
            wx=np.ascontiguousarray(W_x[sl]),
            convw=np.ascontiguousarray(conv_w[sl, 0, :]),
            convb=np.ascontiguousarray(conv_b[sl].reshape(DI, 1)),
            alog=np.ascontiguousarray(A_log[sl]),
            dpar=np.ascontiguousarray(D_param[sl].reshape(DI, 1)),
            dtb=np.ascontiguousarray(dt_bias[sl].reshape(DI, 1)),
            wout=np.ascontiguousarray(W_out[sl]),
            k0t=np.ascontiguousarray(k0[:DM]),
            k0b=np.ascontiguousarray(k0[DM:2 * DM]),
            k1u=k1, k2u=k2,
            s0i=s0, s1i=s1, s2i=s2,
            m_e=np.full((128, 1), 1.0 if dh == 0 else 0.0, np.float32),
            minv_e=np.full((128, 1), 0.0 if dh == 0 else 1.0, np.float32),
            m_o=np.full((128, 1), 1.0 if dh == 1 else 0.0, np.float32),
            minv_o=np.full((128, 1), 0.0 if dh == 1 else 1.0, np.float32),
            ones_bf=np.ones((128, 1), dtype=BF),
            neghalf=np.full((1, 128), -0.5, dtype=BF),
        )
        in_maps.append(im)
    return in_maps


if __name__ == "__main__":
    build_nc()
    print("built ok")


# version 08
# speedup vs baseline: 1.1545x; 1.0423x over previous
import numpy as np
import ml_dtypes
from contextlib import ExitStack
import concourse.bass as bass
import concourse.mybir as mybir
from concourse.bass_utils import run_bass_kernel_spmd

F32 = mybir.dt.float32
BF16 = mybir.dt.bfloat16
U8 = mybir.dt.uint8
AF = mybir.ActivationFunctionType
ALU = mybir.AluOpType
BF = ml_dtypes.bfloat16

DM = 1024
DI = 1024
NT = 4096
TC = 512
NCH = NT // TC
S = 16
QS = 4
NQ = S // QS
KT = DM // 128
IT = DI // 128
HTOK = TC // 2
LNEPS = 1e-5
GROUPS = [[0, 1], [2, 3], [4, 5], [6, 7]]

_NC_CACHE = {}


class Plan:
    ENGS = ("sync", "gpsimd", "scalar", "vector", "tensor")

    def __init__(self):
        self.items = {e: [] for e in self.ENGS}
        self.sems = {}
        self.waited = {e: {} for e in self.ENGS}

    def add_sem(self, name, handle):
        self.sems[name] = [handle, 0]

    def ensure(self, name, nc, stack):
        if name not in self.sems:
            self.add_sem(name, stack.enter_context(nc.semaphore(name)))

    def step(self, eng, fn, waits=(), sem=None, k=1):
        ws = []
        for w in waits:
            if w is None:
                continue
            sn, val = w
            if val <= 0:
                continue
            if val > self.waited[eng].get(sn, 0):
                self.waited[eng][sn] = val
                ws.append((self.sems[sn][0], val))
        ev = None
        if sem is not None:
            self.sems[sem][1] += k
            ev = (sem, self.sems[sem][1])
        self.items[eng].append((fn, tuple(ws), None if sem is None else self.sems[sem][0], k))
        return ev

    def run(self, eng_name, eng):
        for fn, ws, semh, k in self.items[eng_name]:
            for (h, v) in ws:
                eng.wait_ge(h, v)
            ins = fn(eng)
            if semh is not None:
                ins.then_inc(semh, k)


def build_nc():
    nc = bass.Bass("TRN2", target_bir_lowering=False, detect_race_conditions=False)

    def inp(name, shape, dt):
        return nc.dram_tensor(name, list(shape), dt, kind="ExternalInput").ap()

    xT = inp("xT", (DM, NT), F32)
    wbr = inp("wbr", (DM, DI), F32)
    wz = inp("wz", (DM, DI), F32)
    wx = inp("wx", (DI, 33), F32)
    convw = inp("convw", (DI, 4), F32)
    convb = inp("convb", (DI, 1), F32)
    alog = inp("alog", (DI, S), F32)
    dpar = inp("dpar", (DI, 1), F32)
    dtb = inp("dtb", (DI, 1), F32)
    wout = inp("wout", (DI, DM), F32)
    k0t = inp("k0t", (DM, 4096), U8)
    k0b = inp("k0b", (DM, 4096), U8)
    k1u = inp("k1u", (4096, 4096), U8)
    k2u = inp("k2u", (4096, DM), U8)
    s0i = inp("s0i", (1, 1), F32)
    s1i = inp("s1i", (1, 1), F32)
    s2i = inp("s2i", (1, 1), F32)
    m_e = inp("m_e", (128, 1), F32)
    minv_e = inp("minv_e", (128, 1), F32)
    m_o = inp("m_o", (128, 1), F32)
    minv_o = inp("minv_o", (128, 1), F32)
    ones_bf = inp("ones_bf", (128, 1), BF16)
    neghalf = inp("neghalf", (1, 128), BF16)

    outT = nc.dram_tensor("outT", [DM, NT // 2], F32, kind="ExternalOutput").ap()

    def ihb(name, shape, dt):
        return nc.dram_tensor(name, list(shape), dt, kind="Internal").ap()

    w0e = ihb("w0e", (KT, 128, 4096), BF16)
    k1b = ihb("k1b", (4096, 4096), BF16)
    k2b = ihb("k2b", (4096, DM), BF16)
    wbr_b = ihb("wbr_b", (DM, DI), BF16)
    wz_b = ihb("wz_b", (DM, DI), BF16)
    wout_b = ihb("wout_b", (DI, DM), BF16)
    ssm_i = ihb("ssm_i", (NCH, 33, TC), BF16)
    ssm_o = ihb("ssm_o", (NCH, 33, TC), BF16)
    wo_i = ihb("wo_i", (NCH, DM, TC), BF16)
    wo_o = ihb("wo_o", (NCH, DM, TC), BF16)
    lnr = ihb("lnr", (4 * NCH, 2, TC), F32)

    st = ExitStack()
    sb = lambda n, shp, dt: st.enter_context(nc.sbuf_tensor(n, list(shp), dt))
    ps = lambda n: st.enter_context(nc.psum_tensor(n, [128, 512], F32))

    with st:
        wwin = sb("wwin", (128, KT * DI), BF16)
        wx_s = sb("wx_s", (128, IT * 33), BF16)
        acol = sb("acol", (128, IT * S), F32)
        cwcol = sb("cwcol", (128, IT * 4), F32)
        cbcol = sb("cbcol", (128, IT), F32)
        dtbcol = sb("dtbcol", (128, IT), F32)
        dcol = sb("dcol", (128, IT), F32)
        scols = sb("scols", (128, 4), F32)
        onec = sb("onec", (128, 1), BF16)
        nhrow = sb("nhrow", (1, 128), BF16)
        mcols = sb("mcols", (128, 4), F32)
        epsc = sb("epsc", (128, 1), F32)
        onef = sb("onef", (128, 1), F32)

        ln1x = sb("ln1x", (128, KT * TC), BF16)
        xcp = sb("xcp", (128, IT * TC), BF16)
        szp = sb("szp", (128, IT * TC), BF16)
        xbrg = sb("xbrg", (128, IT * (TC + 8)), BF16)
        dtp = sb("dtp", (128, IT * TC), BF16)
        wpl = sb("wpl", (128, IT * TC), BF16)
        yp = sb("yp", (128, IT * TC), BF16)
        x1b = sb("x1b", (128, KT * TC), BF16)
        pool_a = sb("pool_a", (128, KT * TC), BF16)
        pool_b = sb("pool_b", (128, KT * TC), BF16)

        u_q = sb("u_q", (128, QS * TC), BF16)
        cbc = u_q
        dA_q = sb("dA_q", (128, 2 * QS * TC), BF16)
        h_q = sb("h_q", (128, 2 * QS * TC), BF16)
        hcar = sb("hcar", (128, IT * S), F32)
        bqt = sb("bqt", (128, 2 * QS * TC), BF16)
        cqt = sb("cqt", (128, 2 * QS * TC), BF16)

        rows = sb("rows", (1, 3 * TC), F32)
        abc_a = sb("abc_a", (128, TC), BF16)
        abc_b = sb("abc_b", (128, TC), BF16)
        ssm_s = sb("ssm_s", (33, TC), BF16)

        f1 = sb("f1", (128, 32 * TC), BF16)
        f2h = sb("f2h", (128, 4 * TC), BF16)
        ln3 = sb("ln3", (128, KT * TC), BF16)
        f3 = sb("f3", (128, KT * TC), BF16)
        kwin = sb("kwin", (128, 3 * 1024), BF16)
        rshl = sb("rshl", (1, 4 * TC), BF16)
        x2t = sb("x2t", (128, TC), F32)

        p0, p1, p2, p3, p4, p5, p6, p7 = (ps(f"p{i}") for i in range(8))

        P = Plan()
        _orig_step = P.step
        def _step(eng, fn, waits=(), sem=None, k=1):
            if sem is not None:
                P.ensure(sem, nc, st)
            return _orig_step(eng, fn, waits, sem, k)
        P.step = _step
        for sname in ("sV", "sS", "sT"):
            P.ensure(sname, nc, st)

        def blk(t, i, w):
            return t[:, i * w:(i + 1) * w]

        ev_w = {}
        P.step("gpsimd", lambda e: e.dma_start(out=wbr_b, in_=wbr), sem="sWI", k=16)
        P.step("gpsimd", lambda e: e.dma_start(out=wz_b, in_=wz), sem="sWI", k=16)
        P.step("gpsimd", lambda e: e.dma_start(out=wout_b, in_=wout), sem="sWI", k=16)
        evwi = P.step("gpsimd", lambda e: e.dma_start(
            out=wx_s[:].rearrange("p (kt j) -> p kt j", kt=IT),
            in_=wx.rearrange("(kt p) j -> p kt j", p=128)), sem="sWI", k=16)
        ev_w["wbr"] = ev_w["wz"] = ev_w["wout"] = ev_w["wx"] = evwi

        def ld_cols(dst, src, n):
            return lambda e: e.dma_start(
                out=dst.rearrange("p (kt j) -> p kt j", kt=IT),
                in_=src.rearrange("(kt p) j -> p kt j", p=128),
                allow_slow_non_contiguous=True)

        P.step("sync", ld_cols(acol[:], alog, S), sem="sSM", k=16)
        P.step("sync", ld_cols(cwcol[:], convw, 4), sem="sSM", k=16)
        P.step("sync", ld_cols(cbcol[:], convb, 1), sem="sSM", k=16)
        P.step("sync", ld_cols(dtbcol[:], dtb, 1), sem="sSM", k=16)
        P.step("sync", ld_cols(dcol[:], dpar, 1), sem="sSM", k=16)
        P.step("sync", lambda e: e.dma_start(out=onec[:], in_=ones_bf), sem="sSM", k=16)
        P.step("sync", lambda e: e.dma_start(out=nhrow[:], in_=neghalf), sem="sSM", k=16)
        P.step("sync", lambda e: e.dma_start(out=mcols[:, 0:1], in_=m_e), sem="sSM", k=16)
        P.step("sync", lambda e: e.dma_start(out=mcols[:, 1:2], in_=minv_e), sem="sSM", k=16)
        P.step("sync", lambda e: e.dma_start(out=mcols[:, 2:3], in_=m_o), sem="sSM", k=16)
        ev_small = P.step("sync", lambda e: e.dma_start(out=mcols[:, 3:4], in_=minv_o), sem="sSM", k=16)
        ev_alog = ev_small

        P.step("gpsimd", lambda e: e.dma_start(
            out=scols[:, 0:1], in_=s0i[0:1, :].squeeze(0).partition_broadcast(128)), sem="sSC", k=16)
        P.step("gpsimd", lambda e: e.dma_start(
            out=scols[:, 1:2], in_=s1i[0:1, :].squeeze(0).partition_broadcast(128)), sem="sSC", k=16)
        ev_s2x = P.step("gpsimd", lambda e: e.dma_start(
            out=scols[:, 2:3], in_=s2i[0:1, :].squeeze(0).partition_broadcast(128)), sem="sSC", k=16)
        ev_s0 = ev_s1 = ev_s2 = ev_s2x
        P.step("vector", lambda e: e.tensor_scalar_mul(scols[:, 0:1], scols[:, 0:1], 2.0),
               waits=[ev_s0], sem="sV")
        P.step("vector", lambda e: e.tensor_scalar_mul(scols[:, 1:2], scols[:, 1:2], 4.0),
               waits=[ev_s1], sem="sV")
        ev_scl = P.step("vector", lambda e: e.tensor_scalar_mul(scols[:, 2:3], scols[:, 2:3], 4.0),
                        waits=[ev_s2], sem="sV")
        ev_ax = P.step("scalar", lambda e: e.activation(out=acol[:], in_=acol[:], func=AF.Exp),
                       waits=[ev_alog], sem="sS")
        ev_acol = P.step("vector", lambda e: e.tensor_scalar_mul(acol[:], acol[:], -1.0),
                         waits=[ev_ax], sem="sV")
        for i in range(IT):
            off = i * (TC + 8)
            P.step("vector", (lambda off=off: (lambda e: e.memset(xbrg[:, off:off + 8], 0.0)))(), sem="sV")
        P.step("vector", lambda e: e.memset(x1b[:], 0.0), sem="sV")
        ev_zero = P.step("vector", lambda e: e.memset(hcar[:], 0.0), sem="sV")
        ev_eps = P.step("vector", lambda e: e.memset(epsc[:], LNEPS), sem="sV")
        ev_onef = P.step("vector", lambda e: e.memset(onef[:], 1.0), sem="sV")

        ev_k1b = ev_k2b = None
        for q in range(4):
            ev_k1b = P.step("gpsimd", (lambda q=q: (lambda e: e.dma_start(
                out=k1b[q * 1024:(q + 1) * 1024, :], in_=k1u[q * 1024:(q + 1) * 1024, :])))(),
                sem="sK1", k=16)
        for q in range(2):
            ev_k2b = P.step("gpsimd", (lambda q=q: (lambda e: e.dma_start(
                out=k2b[q * 2048:(q + 1) * 2048, :], in_=k2u[q * 2048:(q + 1) * 2048, :])))(),
                sem="sK2", k=16)
        f1u8 = f1[:].bitcast(U8)
        prev_sub = None
        prev_store = None
        ev_stage_done = None
        for kt in range(KT):
            wl = [prev_sub] if prev_sub else []
            eva = P.step("sync", (lambda kt=kt: (lambda e: e.dma_start(
                out=f1u8[:, 0:4096], in_=k0t[kt * 128:(kt + 1) * 128, :])))(),
                waits=wl, sem="sK0A", k=16)
            evb = P.step("sync", (lambda kt=kt: (lambda e: e.dma_start(
                out=f1u8[:, 4096:8192], in_=k0b[kt * 128:(kt + 1) * 128, :])))(),
                waits=wl, sem="sK0B", k=16)
            evs = P.step("vector", lambda e: e.tensor_tensor(
                f1[:, 4096:8192], f1u8[:, 0:4096], f1u8[:, 4096:8192], ALU.subtract),
                waits=[eva, evb] + ([prev_store] if prev_store else []), sem="sV")
            evd = P.step("gpsimd", (lambda kt=kt: (lambda e: e.dma_start(
                out=w0e[kt], in_=f1[:, 4096:8192])))(),
                waits=[evs], sem="sSTG", k=16)
            prev_sub, prev_store = evs, evd
            ev_stage_done = evd
        ev_f1_free = ev_stage_done

        pg = {}
        wwin_guard = None
        p2_guard = None
        p34_guard = None
        ln_dma_guard = None
        abc_guard = None
        prev_halo = [None] * IT
        halo_src_guard = [None] * IT
        prev_g = [None] * IT
        prev_carry = {}
        prev_scan = {}
        hq_user = {}
        dA_user = {}
        uq_user = None
        bq_user = {}
        cq_user = {}
        ev_dt = [None] * IT
        ev_wpl = [None] * IT
        ev_ssm_out_prev = None
        t3_uq_guard = {0: None, 1: None, 2: None, 3: None}
        ev_dt_prev_last = None
        chunk_done = None
        ev_ffn_slotbuf = None
        f2h_guard = None
        x2t_guard = None
        rsf_guard = None
        kwin_user = {0: None, 1: None, 2: None}
        ev_stats_ready = None

        def mm_stats(dst, getsrc, kt, first, last, N=TC):
            return lambda e: e.matmul(dst[0:1, 0:N], onec[:, 0:1], getsrc(),
                                      start=first, stop=last)

        def ln_rows(idx, w):
            nonlocal ln_dma_guard, p34_guard, abc_guard
            r_mu = rows[0:1, 0:w]
            r_v = rows[0:1, TC:TC + w]
            r_a = rows[0:1, 2 * TC:2 * TC + w]
            r_b = r_v
            inv = 1.0 / DM
            wl = [("sT", P.sems["sT"][1])]
            if ln_dma_guard:
                wl.append(ln_dma_guard)
            e1 = P.step("vector", lambda e: e.tensor_scalar_mul(r_mu, p3[0:1, 0:w], inv),
                        waits=wl, sem="sV")
            e2 = P.step("vector", lambda e: e.tensor_scalar_mul(r_v, p4[0:1, 0:w], inv),
                        sem="sV")
            P.step("vector", lambda e: e.tensor_tensor(r_a, r_mu, r_mu, ALU.mult), sem="sV")
            ev_v = P.step("vector", lambda e: e.tensor_tensor(r_v, r_v, r_a, ALU.subtract), sem="sV")
            P.step("scalar", lambda e: e.activation(out=r_v, in_=r_v, func=AF.Ln, bias=epsc[0:1, 0:1]),
                   waits=[ev_v, ev_eps], sem="sS")
            es = P.step("scalar", lambda e: e.activation(out=r_a, in_=r_v, func=AF.Exp, scale=-0.5),
                        sem="sS")
            ea = es
            eb = P.step("vector", lambda e: e.tensor_tensor(r_b, r_mu, r_a, ALU.mult),
                        waits=[es], sem="sV")
            ed0 = P.step("sync", lambda e: e.dma_start(out=lnr[idx, 0:1, 0:w], in_=r_b),
                         waits=[ea, eb], sem="sRWB", k=16)
            ed1 = P.step("sync", lambda e: e.dma_start(out=lnr[idx, 1:2, 0:w], in_=r_a),
                         waits=[ea, eb], sem="sRWA", k=16)
            gl = [abc_guard] if abc_guard else []
            eba = P.step("gpsimd", lambda e: e.dma_start(
                out=abc_a[:, 0:w], in_=lnr[idx, 1:2, 0:w].squeeze(0).partition_broadcast(128)),
                waits=[ed1] + gl, sem="sABA", k=16)
            ebb = P.step("gpsimd", lambda e: e.dma_start(
                out=abc_b[:, 0:w], in_=lnr[idx, 0:1, 0:w].squeeze(0).partition_broadcast(128)),
                waits=[ed0] + gl, sem="sABB", k=16)
            ln_dma_guard = ed0
            p34_guard = e2 if e2 else e1
            return eba, ebb

        ev_out_final = None
        pend = [None]

        def pump(n=1):
            g = pend[0]
            if g is None:
                return
            for _ in range(n):
                try:
                    next(g)
                except StopIteration:
                    pend[0] = None
                    return

        def drain():
            g = pend[0]
            if g is not None:
                for _ in g:
                    pass
                pend[0] = None

        def emit_ffn(k, ev_blend):
            nonlocal rsf_guard, x2t_guard, abc_guard, f2h_guard, ev_ffn_slotbuf
            x1s = x1b
            bkey = ["p0", "p1", "p5", "p7"]
            banks4 = [p0, p1, p5, p7]
            ev_q3 = []
            for kt in range(KT):
                ev_q3.append(P.step("scalar", (lambda kt=kt: (lambda e: e.activation(
                    out=blk(f1, kt, TC), in_=blk(x1s, kt, TC), func=AF.Square)))(),
                    waits=[ev_blend, f2h_guard], sem="sS"))
            evA = evB = None
            for kt in range(KT):
                evA = P.step("tensor", mm_stats(p3, (lambda kt=kt: lambda: blk(x1s, kt, TC))(), kt, kt == 0, kt == KT - 1),
                             waits=([ev_blend, p34_guard] if kt == 0 else []),
                             sem="sT" if kt == KT - 1 else None)
            for kt in range(KT):
                evB = P.step("tensor", mm_stats(p4, (lambda kt=kt: lambda: blk(f1, kt, TC))(), kt, kt == 0, kt == KT - 1),
                             waits=[ev_q3[kt]], sem="sT" if kt == KT - 1 else None)
            eba3, ebb3 = ln_rows(2 * NCH + k, TC)
            ev_ln3 = None
            for kt in range(KT):
                P.step("vector", (lambda kt=kt: (lambda e: e.tensor_tensor(
                    blk(ln3, kt, TC), blk(x1s, kt, TC), abc_a[:, 0:TC], ALU.mult)))(),
                    waits=[eba3], sem="sV")
                ev_ln3 = P.step("vector", (lambda kt=kt: (lambda e: e.tensor_tensor(
                    blk(ln3, kt, TC), blk(ln3, kt, TC), abc_b[:, 0:TC], ALU.subtract)))(),
                    waits=[ebb3], sem="sV")
            abc_guard = ev_ln3
            f1_stats_done = evB
            yield

            ev_f1 = [None] * 32
            for mg in range(8):
                ev_l1mm = [None] * 4
                for kq in range(4):
                    wpar = (mg * 4 + kq) % 3
                    kw = kwin[:, wpar * 1024:(wpar + 1) * 1024]
                    evl_k = P.step("sync", (lambda mg=mg, kq=kq, kw=kw: (lambda e: e.dma_start(
                        out=kw.rearrange("p (kt j) -> p kt j", kt=2),
                        in_=w0e[kq * 2:(kq + 1) * 2, :, mg * 512:(mg + 1) * 512].rearrange("kt p j -> p kt j"))))(),
                        waits=[ev_stage_done, kwin_user[wpar]], sem="sKW%d" % wpar, k=16)
                    for mt in range(4):
                        m = mg * 4 + mt
                        bank = banks4[mt]
                        for kt in range(2):
                            wl = []
                            if kt == 0:
                                wl = [evl_k, ev_ln3, ev_f1_free]
                                if kq == 0:
                                    wl.append(pg.get(bkey[mt]))
                                    if mg == 0:
                                        wl.append(f1_stats_done)
                            evm = P.step("tensor", (lambda bank=bank, kw=kw, kt=kt, kq=kq, mt=mt: (lambda e: e.matmul(
                                bank[:, 0:TC],
                                kw[:, kt * 512 + mt * 128:kt * 512 + (mt + 1) * 128],
                                blk(ln3, kq * 2 + kt, TC),
                                start=(kq == 0 and kt == 0), stop=(kq == 3 and kt == 1))))(),
                                waits=wl, sem="sT" if kt == 1 else None)
                        if kq == 3:
                            ev_l1mm[mt] = evm
                    kwin_user[wpar] = evm
                yield
                for mt in range(4):
                    m = mg * 4 + mt
                    eve = P.step("scalar", (lambda bank=banks4[mt], m=m: (lambda e: e.activation(
                        out=blk(f1, m, TC), in_=bank[:, 0:TC], func=AF.Relu,
                        scale=scols[:, 0:1])))(),
                        waits=[ev_l1mm[mt], ev_scl, f1_stats_done], sem="sS")
                    ev_f1[m] = eve
                    pg[bkey[mt]] = eve
            evr = None
            for m in range(32):
                wl = [ev_f1[m]]
                if m == 0:
                    wl.append(pg.get("p6"))
                evr = P.step("tensor", (lambda m=m: (lambda e: e.matmul(
                    p6[0:1, 0:TC], onec[:, 0:1], blk(f1, m, TC),
                    start=(m == 0), stop=(m == 31))))(),
                    waits=wl, sem="sT" if m == 31 else None)
            ev_hi = P.step("scalar", lambda e: e.copy(
                out=rshl[0:1, 0:TC], in_=p6[0:1, 0:TC]), waits=[evr, rsf_guard], sem="sS")
            ev_rs1 = P.step("vector", lambda e: e.tensor_tensor(
                rshl[0:1, TC:2 * TC], p6[0:1, 0:TC], rshl[0:1, 0:TC], ALU.subtract),
                waits=[ev_hi], sem="sV")
            pg["p6"] = ev_rs1
            rsf_guard = ev_rs1
            yield

            ev_f3 = [None] * KT
            f2_prev = [f1_stats_done] * 4
            for mh in range(8):
                ev_f2 = [None] * 4
                evm_fin = [None] * 4
                for kq in range(8):
                    pass
                for kq in range(16):
                    wpar = (mh * 16 + kq) % 3
                    kw = kwin[:, wpar * 1024:(wpar + 1) * 1024]
                    evl = P.step("sync", (lambda kq=kq, mh=mh, kw=kw: (lambda e: e.dma_start(
                        out=kw.rearrange("p (kt j) -> p kt j", kt=2),
                        in_=k1b[kq * 256:(kq + 1) * 256, mh * 512:(mh + 1) * 512]
                        .rearrange("(kt p) j -> p kt j", p=128))))(),
                        waits=[ev_k1b, kwin_user[wpar]], sem="sKW%d" % wpar, k=16)
                    for mt in range(4):
                        bank = banks4[mt]
                        if kq == 0:
                            P.step("tensor", (lambda bank=bank: (lambda e: e.matmul(
                                bank[:, 0:TC], nhrow[0:1, :],
                                rshl[0:1, 0:TC], start=True, stop=False)))(),
                                waits=[ev_rs1, pg.get(bkey[mt]), f2_prev[mt]])
                            P.step("tensor", (lambda bank=bank: (lambda e: e.matmul(
                                bank[:, 0:TC], nhrow[0:1, :],
                                rshl[0:1, TC:2 * TC], start=False, stop=False)))())
                        evm = None
                        for kt8 in range(2):
                            evm = P.step("tensor", (lambda bank=bank, kw=kw, kt8=kt8, mt=mt, kq=kq: (lambda e: e.matmul(
                                bank[:, 0:TC],
                                kw[:, kt8 * 512 + mt * 128:kt8 * 512 + (mt + 1) * 128],
                                blk(f1, kq * 2 + kt8, TC),
                                start=False, stop=(kq == 15 and kt8 == 1))))(),
                                waits=([evl] if kt8 == 0 else []),
                                sem="sT" if kt8 == 1 else None)
                        evm_fin[mt] = evm
                    kwin_user[wpar] = evm_fin[3]
                yield
                for mt in range(4):
                    ev_f2[mt] = P.step("scalar", (lambda bank=banks4[mt], mt=mt: (lambda e: e.activation(
                        out=blk(f2h, mt, TC), in_=bank[:, 0:TC], func=AF.Relu,
                        scale=scols[:, 1:2])))(),
                        waits=[evm_fin[mt], f2_prev[mt]], sem="sS")
                    pg[bkey[mt]] = ev_f2[mt]
                evr = None
                for mt in range(4):
                    wl = [ev_f2[mt]]
                    if mt == 0:
                        wl.append(pg.get("p6"))
                    evr = P.step("tensor", (lambda mt=mt: (lambda e: e.matmul(
                        p6[0:1, 0:TC], onec[:, 0:1], blk(f2h, mt, TC),
                        start=(mt == 0), stop=(mt == 3))))(),
                        waits=wl, sem="sT" if mt == 3 else None)
                ev_hi2 = P.step("scalar", lambda e: e.copy(
                    out=rshl[0:1, 2 * TC:3 * TC], in_=p6[0:1, 0:TC]),
                    waits=[evr, rsf_guard], sem="sS")
                ev_rs2 = P.step("vector", lambda e: e.tensor_tensor(
                    rshl[0:1, 3 * TC:4 * TC], p6[0:1, 0:TC], rshl[0:1, 2 * TC:3 * TC], ALU.subtract),
                    waits=[ev_hi2], sem="sV")
                pg["p6"] = ev_rs2
                rsf_guard = ev_rs2
                for jg in range(2):
                    evm_fin = [None] * 4
                    for kq in range(2):
                        wpar = (mh * 4 + jg * 2 + kq) % 3
                        kw = kwin[:, wpar * 1024:(wpar + 1) * 1024]
                        krow = mh * 512 + kq * 256
                        evl = P.step("sync", (lambda krow=krow, jg=jg, kw=kw: (lambda e: e.dma_start(
                            out=kw.rearrange("p (kt j) -> p kt j", kt=2),
                            in_=k2b[krow:krow + 256, jg * 512:(jg + 1) * 512]
                            .rearrange("(kt p) j -> p kt j", p=128))))(),
                            waits=[ev_k2b, kwin_user[wpar]], sem="sKW%d" % wpar, k=16)
                        for jt in range(4):
                            j = jg * 4 + jt
                            bank = banks4[jt]
                            if kq == 0:
                                P.step("tensor", (lambda bank=bank: (lambda e: e.matmul(
                                    bank[:, 0:TC], nhrow[0:1, :],
                                    rshl[0:1, 2 * TC:3 * TC], start=True, stop=False)))(),
                                    waits=[ev_rs2, pg.get(bkey[jt])])
                                rsk1 = P.step("tensor", (lambda bank=bank: (lambda e: e.matmul(
                                    bank[:, 0:TC], nhrow[0:1, :],
                                    rshl[0:1, 3 * TC:4 * TC], start=False, stop=False)))(),
                                    sem="sT")
                                rsf_guard = rsk1
                            evm = None
                            for kt8 in range(2):
                                evm = P.step("tensor", (lambda bank=bank, kw=kw, kt8=kt8, jt=jt, kq=kq: (lambda e: e.matmul(
                                    bank[:, 0:TC],
                                    kw[:, kt8 * 512 + jt * 128:kt8 * 512 + (jt + 1) * 128],
                                    blk(f2h, kq * 2 + kt8, TC),
                                    start=False, stop=(kq == 1 and kt8 == 1))))(),
                                    waits=([evl] if kt8 == 0 else []),
                                    sem="sT" if kt8 == 1 else None)
                            evm_fin[jt] = evm
                        kwin_user[wpar] = evm_fin[3]
                    yield
                    for jt in range(4):
                        j = jg * 4 + jt
                        if mh == 0:
                            ev_f3[j] = P.step("scalar", (lambda bank=banks4[jt], j=j: (lambda e: e.activation(
                                out=blk(f3, j, TC), in_=bank[:, 0:TC],
                                func=AF.Identity, scale=scols[:, 2:3])))(),
                                waits=[evm_fin[jt]], sem="sS")
                        else:
                            ev_f3[j] = P.step("vector", (lambda bank=banks4[jt], j=j: (lambda e: e.scalar_tensor_tensor(
                                blk(f3, j, TC), bank[:, 0:TC],
                                scols[:, 2:3], blk(f3, j, TC), ALU.mult, ALU.add)))(),
                                waits=[evm_fin[jt], ("sS", P.sems["sS"][1])], sem="sV")
                        pg[bkey[jt]] = ev_f3[j]
                f2_prev = [evm_fin[jt] for jt in range(4)]
            f2h_guard = ev_f3[-1]
            ev_q4 = []
            for kt in range(KT):
                ev_q4.append(P.step("scalar", (lambda kt=kt: (lambda e: e.activation(
                    out=blk(f1, kt, TC), in_=blk(f3, kt, TC), func=AF.Square)))(),
                    waits=[ev_f3[kt], ("sT", P.sems["sT"][1])], sem="sS"))
            evA = evB = None
            for kt in range(KT):
                evA = P.step("tensor", mm_stats(p3, (lambda kt=kt: lambda: blk(f3, kt, TC))(), kt, kt == 0, kt == KT - 1),
                             waits=([ev_f3[kt], p34_guard] if kt == 0 else [ev_f3[kt]]),
                             sem="sT" if kt == KT - 1 else None)
            for kt in range(KT):
                evB = P.step("tensor", mm_stats(p4, (lambda kt=kt: lambda: blk(f1, kt, TC))(), kt, kt == 0, kt == KT - 1),
                             waits=[ev_q4[kt]], sem="sT" if kt == KT - 1 else None)
            eba4, ebb4 = ln_rows(3 * NCH + k, TC)
            last_norm = None
            yield
            for kt in range(KT):
                P.step("vector", (lambda kt=kt: (lambda e: e.tensor_tensor(
                    x2t[:, 0:TC], blk(f3, kt, TC), abc_a[:, 0:TC], ALU.mult)))(),
                    waits=[eba4, x2t_guard], sem="sV")
                P.step("vector", (lambda kt=kt: (lambda e: e.tensor_tensor(
                    x2t[:, 0:TC], x2t[:, 0:TC], abc_b[:, 0:TC], ALU.subtract)))(),
                    waits=[ebb4], sem="sV")
                evx2 = P.step("vector", (lambda kt=kt: (lambda e: e.tensor_tensor(
                    x2t[:, 0:TC], x2t[:, 0:TC], blk(x1s, kt, TC), ALU.add)))(), sem="sV")
                last_norm = evx2
                evst = P.step("sync", (lambda kt=kt, k=k: (lambda e: e.dma_start(
                    out=outT[kt * 128:(kt + 1) * 128, k * TC:(k + 1) * TC],
                    in_=x2t[:, 0:TC])))(),
                    waits=[evx2], sem="sST", k=16)
                ev_out_final = evst
                x2t_guard = evst
            abc_guard = last_norm
            ev_ffn_slotbuf = last_norm

        for c in range(NCH):
            kslot = c // 2
            xsrc = xT[:, c * TC:(c + 1) * TC].rearrange("(kt p) t -> p kt t", p=128)

            ev_xl = P.step("gpsimd", (lambda xsrc=xsrc: (lambda e: e.dma_start(
                out=pool_a[:].rearrange("p (kt t) -> p kt t", kt=KT), in_=xsrc)))(),
                waits=[chunk_done], sem="sXL", k=16)
            ev_sq = []
            for kt in range(KT):
                ev_sq.append(P.step("scalar", (lambda kt=kt: (lambda e: e.activation(
                    out=blk(pool_b, kt, TC), in_=blk(pool_a, kt, TC), func=AF.Square)))(),
                    waits=[ev_xl], sem="sS"))
            evA = evB = None
            for kt in range(KT):
                evA = P.step("tensor", mm_stats(p3, (lambda kt=kt: lambda: blk(pool_a, kt, TC))(), kt, kt == 0, kt == KT - 1),
                             waits=([ev_xl, ev_small, p34_guard] if kt == 0 else []),
                             sem="sT" if kt == KT - 1 else None)
            for kt in range(KT):
                evB = P.step("tensor", mm_stats(p4, (lambda kt=kt: lambda: blk(pool_b, kt, TC))(), kt, kt == 0, kt == KT - 1),
                             waits=[ev_sq[kt]], sem="sT" if kt == KT - 1 else None)
            eba, ebb = ln_rows(c, TC)
            ev_ln1 = None
            for kt in range(KT):
                P.step("vector", (lambda kt=kt: (lambda e: e.tensor_tensor(
                    blk(ln1x, kt, TC), blk(pool_a, kt, TC), abc_a[:, 0:TC], ALU.mult)))(),
                    waits=[eba], sem="sV")
                ev_ln1 = P.step("vector", (lambda kt=kt: (lambda e: e.tensor_tensor(
                    blk(ln1x, kt, TC), blk(ln1x, kt, TC), abc_b[:, 0:TC], ALU.subtract)))(),
                    waits=[ebb], sem="sV")
            abc_guard = ev_ln1

            ev_t1 = []
            for part in range(2):
                src_hbm = wz_b if part else wbr_b
                evl = P.step("sync", (lambda src_hbm=src_hbm: (lambda e: e.dma_start(
                    out=wwin[:].rearrange("p (kt j) -> p kt j", kt=KT),
                    in_=src_hbm.rearrange("(kt p) j -> p kt j", p=128))))(),
                    waits=[ev_w["wz" if part else "wbr"], wwin_guard], sem="sWW", k=16)
                for j in range(IT):
                    jj = part * IT + j
                    bank = [p0, p1][jj % 2]
                    w0l = [ev_ln1, evl, pg.get("p%d" % (jj % 2))]
                    evm = None
                    for kt in range(KT):
                        evm = P.step("tensor", (lambda bank=bank, kt=kt, j=j: (lambda e: e.matmul(
                            bank[:, 0:TC], wwin[:, kt * DI + j * 128:kt * DI + (j + 1) * 128],
                            blk(ln1x, kt, TC), start=(kt == 0), stop=(kt == KT - 1))))(),
                            waits=(w0l if kt == 0 else []), sem="sT" if kt == KT - 1 else None)
                    if part:
                        eve = P.step("scalar", (lambda bank=bank, j=j: (lambda e: e.activation(
                            out=blk(szp, j, TC), in_=bank[:, 0:TC], func=AF.Silu)))(),
                            waits=[evm], sem="sS")
                    else:
                        off = j * (TC + 8) + 3
                        eve = P.step("scalar", (lambda bank=bank, off=off: (lambda e: e.copy(
                            out=xbrg[:, off:off + TC], in_=bank[:, 0:TC])))(),
                            waits=[evm, prev_halo[j], prev_g[j]], sem="sS")
                    pg["p%d" % (jj % 2)] = eve
                    ev_t1.append(eve)
                    wwin_guard = evm

            ev_conv = []
            for i in range(IT):
                off = i * (TC + 8)
                e0 = P.step("vector", (lambda i=i, off=off: (lambda e: e.tensor_scalar_mul(
                    blk(xcp, i, TC), xbrg[:, off:off + TC], cwcol[:, i * 4:i * 4 + 1])))(),
                    waits=[ev_t1[i], (prev_halo[i] if c > 0 else ev_zero)], sem="sV")
                for tau in range(1, 4):
                    e0 = P.step("vector", (lambda i=i, off=off, tau=tau: (lambda e: e.scalar_tensor_tensor(
                        blk(xcp, i, TC), xbrg[:, off + tau:off + tau + TC],
                        cwcol[:, i * 4 + tau:i * 4 + tau + 1],
                        blk(xcp, i, TC), ALU.mult, ALU.add)))(), sem="sV")
                halo_src_guard[i] = e0
                evs = P.step("scalar", (lambda i=i: (lambda e: e.activation(
                    out=blk(xcp, i, TC), in_=blk(xcp, i, TC), func=AF.Silu,
                    bias=cbcol[:, i:i + 1])))(), waits=[e0], sem="sS")
                ev_conv.append(evs)
                evh = P.step("scalar", (lambda off=off: (lambda e: e.copy(
                    out=xbrg[:, off:off + 3], in_=xbrg[:, off + TC:off + TC + 3])))(),
                    waits=[e0], sem="sS")
                prev_halo[i] = evh
            pump(2)

            evm = None
            for kt in range(IT):
                wl = [ev_conv[kt], ev_w["wx"]]
                if kt == 0 and p2_guard:
                    wl.append(p2_guard)
                evm = P.step("tensor", (lambda kt=kt: (lambda e: e.matmul(
                    p2[0:33, 0:TC], wx_s[:, kt * 33:(kt + 1) * 33], blk(xcp, kt, TC),
                    start=(kt == 0), stop=(kt == IT - 1))))(),
                    waits=wl, sem="sT" if kt == IT - 1 else None)
            ev_sevac = P.step("vector", lambda e: e.tensor_copy(ssm_s[:, 0:TC], p2[0:33, 0:TC]),
                              waits=[evm, ev_ssm_out_prev], sem="sV")
            p2_guard = ev_sevac
            ev_sout = P.step("gpsimd", (lambda c=c: (lambda e: e.dma_start(
                out=ssm_i[c], in_=ssm_s[:, 0:TC])))(), waits=[ev_sevac], sem="sSSM", k=16)
            ev_ssm_out_prev = ev_sout
            ev_ar1 = P.step("gpsimd", (lambda c=c: (lambda e: e.collective_compute(
                "AllReduce", ALU.add, replica_groups=GROUPS,
                ins=[ssm_i[c]], outs=[ssm_o[c]])))(), waits=[ev_sout], sem="sCCS")
            ev_cbc = P.step("gpsimd", (lambda c=c: (lambda e: e.dma_start(
                out=cbc[:, 0:TC], in_=ssm_o[c][32:33, :].squeeze(0).partition_broadcast(128))))(),
                waits=[ev_ar1, ev_dt_prev_last, uq_user,
                       t3_uq_guard[0], t3_uq_guard[1], t3_uq_guard[2], t3_uq_guard[3]], sem="sCB", k=16)
            pump(2)

            for i in range(IT):
                P.step("scalar", (lambda i=i: (lambda e: e.activation(
                    out=blk(dtp, i, TC), in_=cbc[:, 0:TC], func=AF.Identity,
                    bias=dtbcol[:, i:i + 1])))(),
                    waits=[ev_cbc, ev_wpl[i], prev_g[i], chunk_done], sem="sS")
                P.step("scalar", (lambda i=i: (lambda e: e.activation(
                    out=blk(yp, i, TC), in_=blk(dtp, i, TC), func=AF.Abs)))(), sem="sS")
                P.step("scalar", (lambda i=i: (lambda e: e.activation(
                    out=blk(yp, i, TC), in_=blk(yp, i, TC), func=AF.Exp, scale=-1.0)))(), sem="sS")
                P.step("scalar", (lambda i=i: (lambda e: e.activation(
                    out=blk(yp, i, TC), in_=blk(yp, i, TC), func=AF.Ln,
                    bias=onef[:, 0:1])))(), waits=[ev_onef], sem="sS")
                ev_relu = P.step("scalar", (lambda i=i: (lambda e: e.activation(
                    out=blk(dtp, i, TC), in_=blk(dtp, i, TC), func=AF.Relu)))(), sem="sS")
                ev_dt[i] = P.step("vector", (lambda i=i: (lambda e: e.tensor_tensor(
                    blk(dtp, i, TC), blk(dtp, i, TC), blk(yp, i, TC), ALU.add)))(),
                    waits=[ev_relu], sem="sV")
                ev_wpl[i] = P.step("vector", (lambda i=i: (lambda e: e.tensor_tensor(
                    blk(wpl, i, TC), blk(dtp, i, TC), blk(xcp, i, TC), ALU.mult)))(),
                    waits=[ev_conv[i]], sem="sV")
            ev_dt_prev_last = ev_dt[IT - 1]

            ev_y = [None] * IT
            for qg in range(NQ):
                qpar = qg % 2
                evb = P.step("gpsimd", (lambda c=c, qg=qg, qpar=qpar: (lambda e: e.dma_start(
                    out=bqt[:, qpar * QS * TC:(qpar + 1) * QS * TC].rearrange("p (s t) -> p s t", s=QS),
                    in_=ssm_o[c][qg * QS:(qg + 1) * QS, :].partition_broadcast(128))))(),
                    waits=[ev_ar1, bq_user.get(qpar)], sem="sBQ%d" % qpar, k=16)
                evcq = P.step("gpsimd", (lambda c=c, qg=qg, qpar=qpar: (lambda e: e.dma_start(
                    out=cqt[:, qpar * QS * TC:(qpar + 1) * QS * TC].rearrange("p (s t) -> p s t", s=QS),
                    in_=ssm_o[c][S + qg * QS:S + (qg + 1) * QS, :].partition_broadcast(128))))(),
                    waits=[ev_ar1, cq_user.get(qpar)], sem="sCQ%d" % qpar, k=16)
                last_u = None
                for i in range(IT):
                    it_par = (qg * IT + i) % 2
                    hbase = it_par * QS * TC
                    ev_exps = []
                    for s4 in range(QS):
                        sidx = qg * QS + s4
                        ev_exps.append(P.step("scalar",
                            (lambda i=i, s4=s4, sidx=sidx, it_par=it_par: (lambda e: e.activation(
                                out=dA_q[:, it_par * QS * TC + s4 * TC:it_par * QS * TC + (s4 + 1) * TC],
                                in_=blk(dtp, i, TC), func=AF.Exp,
                                scale=acol[:, i * S + sidx:i * S + sidx + 1])))(),
                            waits=[ev_dt[i], ev_acol, dA_user.get(it_par)], sem="sS"))
                    wrep = blk(wpl, i, TC).unsqueeze(1).broadcast_to((128, QS, TC))
                    evu = P.step("vector", (lambda qpar=qpar, wrep=wrep: (lambda e: e.tensor_tensor(
                        u_q[:].rearrange("p (s t) -> p s t", s=QS), wrep,
                        bqt[:, qpar * QS * TC:(qpar + 1) * QS * TC].rearrange("p (s t) -> p s t", s=QS),
                        ALU.mult)))(),
                        waits=[ev_wpl[i], evb, uq_user, ev_dt[IT - 1],
                               t3_uq_guard[0], t3_uq_guard[1], t3_uq_guard[2], t3_uq_guard[3]], sem="sV")
                    ev_sc = None
                    for s4 in range(QS):
                        sidx = qg * QS + s4
                        ini = 0.0 if c == 0 else hcar[:, i * S + sidx:i * S + sidx + 1]
                        wl = [evu, ev_exps[s4], hq_user.get(it_par), prev_carry.get(("hp", it_par))]
                        if c > 0:
                            wl.append(prev_carry.get((i, qg)))
                        ev_sc = P.step("vector",
                            (lambda s4=s4, hbase=hbase, ini=ini: (lambda e: e.tensor_tensor_scan(
                                h_q[:, hbase + s4 * TC:hbase + (s4 + 1) * TC],
                                dA_q[:, hbase + s4 * TC:hbase + (s4 + 1) * TC],
                                u_q[:, s4 * TC:(s4 + 1) * TC], ini, ALU.mult, ALU.add)))(),
                            waits=wl, sem="sV")
                    prev_scan[(i, qg)] = ev_sc
                    dA_user[it_par] = ev_sc
                    evcar = P.step("scalar", (lambda i=i, qg=qg, hbase=hbase: (lambda e: e.copy(
                        out=hcar[:, i * S + qg * QS:i * S + (qg + 1) * QS],
                        in_=h_q[:, hbase:hbase + QS * TC].rearrange("p (s t) -> p s t", s=QS)[:, :, TC - 1])))(),
                        waits=[ev_sc], sem="sS")
                    prev_carry[(i, qg)] = evcar
                    prev_carry[("hp", it_par)] = evcar
                    evm1 = P.step("vector", (lambda qpar=qpar, hbase=hbase: (lambda e: e.tensor_tensor(
                        u_q[:], h_q[:, hbase:hbase + QS * TC],
                        cqt[:, qpar * QS * TC:(qpar + 1) * QS * TC], ALU.mult)))(),
                        waits=[evcq, ev_sc], sem="sV")
                    hq_user[it_par] = evm1
                    P.step("vector", lambda e: e.tensor_tensor(
                        u_q[:, 0:2 * TC], u_q[:, 0:2 * TC], u_q[:, 2 * TC:4 * TC], ALU.add), sem="sV")
                    if qg == 0:
                        evy = P.step("vector", (lambda i=i: (lambda e: e.tensor_tensor(
                            blk(yp, i, TC), u_q[:, 0:TC], u_q[:, TC:2 * TC], ALU.add)))(), sem="sV")
                    else:
                        P.step("vector", lambda e: e.tensor_tensor(
                            u_q[:, 0:TC], u_q[:, 0:TC], u_q[:, TC:2 * TC], ALU.add), sem="sV")
                        evy = P.step("vector", (lambda i=i: (lambda e: e.tensor_tensor(
                            blk(yp, i, TC), blk(yp, i, TC), u_q[:, 0:TC], ALU.add)))(), sem="sV")
                    ev_y[i] = evy
                    uq_user = evy
                    last_u = evu
                    pump(2)
                bq_user[qpar] = last_u
                cq_user[qpar] = evm1

            ev_g_new = []
            for i in range(IT):
                off = i * (TC + 8) + 3
                P.step("vector", (lambda i=i: (lambda e: e.scalar_tensor_tensor(
                    u_q[:, 0:TC], blk(xcp, i, TC), dcol[:, i:i + 1],
                    blk(yp, i, TC), ALU.mult, ALU.add)))(),
                    waits=[ev_y[i], ev_small], sem="sV")
                e2 = P.step("vector", (lambda i=i, off=off: (lambda e: e.tensor_tensor(
                    xbrg[:, off:off + TC], u_q[:, 0:TC], blk(szp, i, TC), ALU.mult)))(),
                    waits=[ev_t1[IT + i], prev_halo[i]], sem="sV")
                ev_g_new.append(e2)
                prev_g[i] = e2
            pump(2)

            evlw = P.step("sync", lambda e: e.dma_start(
                out=wwin[:].rearrange("p (kt j) -> p kt j", kt=IT),
                in_=wout_b.rearrange("(kt p) j -> p kt j", p=128)),
                waits=[ev_w["wout"], wwin_guard], sem="sWW", k=16)
            ev_t3d = []
            for j in range(IT):
                bank = [p5, p6][j % 2]
                evm = None
                for kt in range(IT):
                    off = kt * (TC + 8) + 3
                    wl = [ev_g_new[kt]]
                    if kt == 0:
                        wl += [evlw, pg.get("p%d" % (5 + j % 2))]
                    evm = P.step("tensor", (lambda bank=bank, kt=kt, j=j, off=off: (lambda e: e.matmul(
                        bank[:, 0:TC], wwin[:, kt * DM + j * 128:kt * DM + (j + 1) * 128],
                        xbrg[:, off:off + TC], start=(kt == 0), stop=(kt == IT - 1))))(),
                        waits=wl, sem="sT" if kt == IT - 1 else None)
                wwin_guard = evm
                uqb = u_q[:, (j % 4) * TC:((j % 4) + 1) * TC]
                eve3 = P.step("scalar", (lambda bank=bank, uqb=uqb: (lambda e: e.copy(
                    out=uqb, in_=bank[:, 0:TC])))(),
                    waits=[evm, t3_uq_guard[j % 4]], sem="sS")
                evd = P.step("sync", (lambda c=c, j=j, uqb=uqb: (lambda e: e.dma_start(
                    out=wo_i[c][j * 128:(j + 1) * 128, :], in_=uqb)))(),
                    waits=[eve3], sem="sT3%d" % (j % 4), k=16)
                pg["p%d" % (5 + j % 2)] = eve3
                t3_uq_guard[j % 4] = evd
                ev_t3d.append(evd)
            ev_ar2 = P.step("gpsimd", (lambda c=c: (lambda e: e.collective_compute(
                "AllReduce", ALU.add, replica_groups=GROUPS,
                ins=[wo_i[c]], outs=[wo_o[c]])))(), waits=ev_t3d, sem="sCCW")
            pump(4)

            ev_xm = P.step("gpsimd", (lambda c=c: (lambda e: e.dma_start(
                out=pool_a[:].rearrange("p (kt t) -> p kt t", kt=KT),
                in_=wo_o[c].rearrange("(kt p) t -> p kt t", p=128))))(),
                waits=[ev_ar2, ev_ln1], sem="sXM", k=16)
            ev_sq2 = []
            for kt in range(KT):
                ev_sq2.append(P.step("scalar", (lambda kt=kt: (lambda e: e.activation(
                    out=blk(pool_b, kt, TC), in_=blk(pool_a, kt, TC), func=AF.Square)))(),
                    waits=[ev_xm], sem="sS"))
            evA = evB = None
            for kt in range(KT):
                evA = P.step("tensor", mm_stats(p3, (lambda kt=kt: lambda: blk(pool_a, kt, TC))(), kt, kt == 0, kt == KT - 1),
                             waits=([ev_xm, p34_guard] if kt == 0 else []),
                             sem="sT" if kt == KT - 1 else None)
            for kt in range(KT):
                evB = P.step("tensor", mm_stats(p4, (lambda kt=kt: lambda: blk(pool_b, kt, TC))(), kt, kt == 0, kt == KT - 1),
                             waits=[ev_sq2[kt]], sem="sT" if kt == KT - 1 else None)
            eba2, ebb2 = ln_rows(NCH + c, TC)
            ev_xr = P.step("gpsimd", (lambda xsrc=xsrc: (lambda e: e.dma_start(
                out=szp[:].rearrange("p (kt t) -> p kt t", kt=KT), in_=xsrc)))(),
                waits=[ev_g_new[IT - 1]], sem="sXR", k=16)
            ev_x1c = []
            for kt in range(KT):
                P.step("vector", (lambda kt=kt: (lambda e: e.tensor_tensor(
                    blk(pool_b, kt, TC), blk(pool_a, kt, TC), abc_a[:, 0:TC], ALU.mult)))(),
                    waits=[eba2], sem="sV")
                P.step("vector", (lambda kt=kt: (lambda e: e.tensor_tensor(
                    blk(pool_b, kt, TC), blk(pool_b, kt, TC), abc_b[:, 0:TC], ALU.subtract)))(),
                    waits=[ebb2], sem="sV")
                ev_x1c.append(P.step("vector", (lambda kt=kt: (lambda e: e.tensor_tensor(
                    blk(pool_b, kt, TC), blk(pool_b, kt, TC), blk(szp, kt, TC), ALU.add)))(),
                    waits=[ev_xr], sem="sV"))
            abc_guard = ev_x1c[-1]
            drain()
            mc = 0 if c % 2 == 0 else 2
            ev_blend = None
            for kt in range(KT):
                P.step("vector", (lambda kt=kt, mc=mc: (lambda e: e.tensor_scalar_mul(
                    blk(yp, kt, TC), blk(x1b, kt, TC),
                    mcols[:, mc + 1:mc + 2])))(),
                    waits=[ev_x1c[kt], ev_ffn_slotbuf], sem="sV")
                ev_blend = P.step("vector", (lambda kt=kt, mc=mc: (lambda e: e.scalar_tensor_tensor(
                    blk(x1b, kt, TC),
                    blk(pool_b, kt, TC), mcols[:, mc:mc + 1],
                    blk(yp, kt, TC), ALU.mult, ALU.add)))(), sem="sV")
            chunk_done = ev_blend

            if c % 2 == 1:
                pend[0] = emit_ffn(kslot, ev_blend)

        drain()
        P.step("sync", lambda e: e.nop(), waits=[("sST", P.sems["sST"][1])])

        with nc.Block() as b:
            @b.sync
            def _(e):
                P.run("sync", e)

            @b.gpsimd
            def _(e):
                P.run("gpsimd", e)

            @b.scalar
            def _(e):
                P.run("scalar", e)

            @b.vector
            def _(e):
                P.run("vector", e)

            @b.tensor
            def _(e):
                P.run("tensor", e)

    return nc


def kernel(L0_attn_x, L0_attn_W_in, L0_attn_conv_w, L0_attn_conv_b, L0_attn_W_x,
           L0_attn_A_log, L0_attn_D_param, L0_attn_dt_bias, L0_attn_W_out,
           L0_ffn_k0, L0_ffn_k1, L0_ffn_k2, L0_ffn_s0, L0_ffn_s1, L0_ffn_s2):
    x = np.asarray(L0_attn_x, dtype=np.float32)
    W_in = np.asarray(L0_attn_W_in, dtype=np.float32)
    conv_w = np.asarray(L0_attn_conv_w, dtype=np.float32)
    conv_b = np.asarray(L0_attn_conv_b, dtype=np.float32)
    W_x = np.asarray(L0_attn_W_x, dtype=np.float32)
    A_log = np.asarray(L0_attn_A_log, dtype=np.float32)
    D_param = np.asarray(L0_attn_D_param, dtype=np.float32)
    dt_bias = np.asarray(L0_attn_dt_bias, dtype=np.float32)
    W_out = np.asarray(L0_attn_W_out, dtype=np.float32)
    k0 = np.ascontiguousarray(np.asarray(L0_ffn_k0)).view(np.uint8)
    k1 = np.ascontiguousarray(np.asarray(L0_ffn_k1)).view(np.uint8)
    k2 = np.ascontiguousarray(np.asarray(L0_ffn_k2)).view(np.uint8)
    s0 = np.asarray(L0_ffn_s0, dtype=np.float32).reshape(1, 1)
    s1 = np.asarray(L0_ffn_s1, dtype=np.float32).reshape(1, 1)
    s2 = np.asarray(L0_ffn_s2, dtype=np.float32).reshape(1, 1)

    if "nc" not in _NC_CACHE:
        _NC_CACHE["nc"] = build_nc()
    nc = _NC_CACHE["nc"]
    in_maps = make_in_maps(x, W_in, conv_w, conv_b, W_x, A_log, D_param,
                           dt_bias, W_out, k0, k1, k2, s0, s1, s2)

    res = run_bass_kernel_spmd(nc, in_maps, core_ids=list(range(8)))

    out = np.empty((4, NT, DM), np.float32)
    for c in range(8):
        b, dh = c // 2, c % 2
        oT = res.results[c]["outT"]
        for k in range(4):
            ch = 2 * k + dh
            out[b, ch * TC:(ch + 1) * TC, :] = oT[:, k * TC:(k + 1) * TC].T
    return out


def make_in_maps(x, W_in, conv_w, conv_b, W_x, A_log, D_param, dt_bias, W_out,
                 k0, k1, k2, s0, s1, s2):
    in_maps = []
    for c in range(8):
        b, dh = c // 2, c % 2
        sl = slice(dh * DI, (dh + 1) * DI)
        im = dict(
            xT=np.ascontiguousarray(x[b].T),
            wbr=np.ascontiguousarray(W_in[:, dh * DI:(dh + 1) * DI]),
            wz=np.ascontiguousarray(W_in[:, 2048 + dh * DI:2048 + (dh + 1) * DI]),
            wx=np.ascontiguousarray(W_x[sl]),
            convw=np.ascontiguousarray(conv_w[sl, 0, :]),
            convb=np.ascontiguousarray(conv_b[sl].reshape(DI, 1)),
            alog=np.ascontiguousarray(A_log[sl]),
            dpar=np.ascontiguousarray(D_param[sl].reshape(DI, 1)),
            dtb=np.ascontiguousarray(dt_bias[sl].reshape(DI, 1)),
            wout=np.ascontiguousarray(W_out[sl]),
            k0t=np.ascontiguousarray(k0[:DM]),
            k0b=np.ascontiguousarray(k0[DM:2 * DM]),
            k1u=k1, k2u=k2,
            s0i=s0, s1i=s1, s2i=s2,
            m_e=np.full((128, 1), 1.0 if dh == 0 else 0.0, np.float32),
            minv_e=np.full((128, 1), 0.0 if dh == 0 else 1.0, np.float32),
            m_o=np.full((128, 1), 1.0 if dh == 1 else 0.0, np.float32),
            minv_o=np.full((128, 1), 0.0 if dh == 1 else 1.0, np.float32),
            ones_bf=np.ones((128, 1), dtype=BF),
            neghalf=np.full((1, 128), -0.5, dtype=BF),
        )
        in_maps.append(im)
    return in_maps


if __name__ == "__main__":
    build_nc()
    print("built ok")


# version 09
# speedup vs baseline: 1.2506x; 1.0832x over previous
import numpy as np
import ml_dtypes
from contextlib import ExitStack
import concourse.bass as bass
import concourse.mybir as mybir
from concourse.bass_utils import run_bass_kernel_spmd

F32 = mybir.dt.float32
BF16 = mybir.dt.bfloat16
U8 = mybir.dt.uint8
AF = mybir.ActivationFunctionType
ALU = mybir.AluOpType
BF = ml_dtypes.bfloat16

DM = 1024
DI = 1024
NT = 4096
TC = 512
NCH = NT // TC
S = 16
QS = 4
NQ = S // QS
KT = DM // 128
IT = DI // 128
HTOK = TC // 2
LNEPS = 1e-5
GROUPS = [[0, 1], [2, 3], [4, 5], [6, 7]]

_NC_CACHE = {}


class Plan:
    ENGS = ("sync", "gpsimd", "scalar", "vector", "tensor")

    def __init__(self):
        self.items = {e: [] for e in self.ENGS}
        self.sems = {}
        self.waited = {e: {} for e in self.ENGS}

    def add_sem(self, name, handle):
        self.sems[name] = [handle, 0]

    def ensure(self, name, nc, stack):
        if name not in self.sems:
            self.add_sem(name, stack.enter_context(nc.semaphore(name)))

    def step(self, eng, fn, waits=(), sem=None, k=1):
        ws = []
        for w in waits:
            if w is None:
                continue
            sn, val = w
            if val <= 0:
                continue
            if val > self.waited[eng].get(sn, 0):
                self.waited[eng][sn] = val
                ws.append((self.sems[sn][0], val))
        ev = None
        if sem is not None:
            self.sems[sem][1] += k
            ev = (sem, self.sems[sem][1])
        self.items[eng].append((fn, tuple(ws), None if sem is None else self.sems[sem][0], k))
        return ev

    def run(self, eng_name, eng):
        for fn, ws, semh, k in self.items[eng_name]:
            for (h, v) in ws:
                eng.wait_ge(h, v)
            ins = fn(eng)
            if semh is not None:
                ins.then_inc(semh, k)


def build_nc():
    nc = bass.Bass("TRN2", target_bir_lowering=False, detect_race_conditions=False)

    def inp(name, shape, dt):
        return nc.dram_tensor(name, list(shape), dt, kind="ExternalInput").ap()

    xT = inp("xT", (DM, NT), F32)
    wbr = inp("wbr", (DM, DI), F32)
    wz = inp("wz", (DM, DI), F32)
    wx = inp("wx", (DI, 33), F32)
    convw = inp("convw", (DI, 4), F32)
    convb = inp("convb", (DI, 1), F32)
    alog = inp("alog", (DI, S), F32)
    dpar = inp("dpar", (DI, 1), F32)
    dtb = inp("dtb", (DI, 1), F32)
    wout = inp("wout", (DI, DM), F32)
    k0t = inp("k0t", (DM, 4096), U8)
    k0b = inp("k0b", (DM, 4096), U8)
    k1u = inp("k1u", (4096, 4096), U8)
    k2u = inp("k2u", (4096, DM), U8)
    s0i = inp("s0i", (1, 1), F32)
    s1i = inp("s1i", (1, 1), F32)
    s2i = inp("s2i", (1, 1), F32)
    m_e = inp("m_e", (128, 1), F32)
    minv_e = inp("minv_e", (128, 1), F32)
    m_o = inp("m_o", (128, 1), F32)
    minv_o = inp("minv_o", (128, 1), F32)
    ones_bf = inp("ones_bf", (128, 1), BF16)
    neghalf = inp("neghalf", (1, 128), BF16)

    outT = nc.dram_tensor("outT", [DM, NT // 2], F32, kind="ExternalOutput").ap()

    def ihb(name, shape, dt):
        return nc.dram_tensor(name, list(shape), dt, kind="Internal").ap()

    w0e = ihb("w0e", (KT, 128, 4096), BF16)
    k1b = ihb("k1b", (4096, 4096), BF16)
    k2b = ihb("k2b", (4096, DM), BF16)
    wbr_b = ihb("wbr_b", (DM, DI), BF16)
    wz_b = ihb("wz_b", (DM, DI), BF16)
    wout_b = ihb("wout_b", (DI, DM), BF16)
    ssm_i = ihb("ssm_i", (NCH, 33, TC), BF16)
    ssm_o = ihb("ssm_o", (NCH, 33, TC), BF16)
    wo_i = ihb("wo_i", (NCH, DM, TC), BF16)
    wo_o = ihb("wo_o", (NCH, DM, TC), BF16)
    lnr = ihb("lnr", (4 * NCH, 2, TC), F32)

    st = ExitStack()
    sb = lambda n, shp, dt: st.enter_context(nc.sbuf_tensor(n, list(shp), dt))
    ps = lambda n: st.enter_context(nc.psum_tensor(n, [128, 512], F32))

    with st:
        wwin = sb("wwin", (128, KT * DI), BF16)
        wx_s = sb("wx_s", (128, IT * 33), BF16)
        acol = sb("acol", (128, IT * S), F32)
        cwcol = sb("cwcol", (128, IT * 4), F32)
        cbcol = sb("cbcol", (128, IT), F32)
        dtbcol = sb("dtbcol", (128, IT), F32)
        dcol = sb("dcol", (128, IT), F32)
        scols = sb("scols", (128, 4), F32)
        onec = sb("onec", (128, 1), BF16)
        nhrow = sb("nhrow", (1, 128), BF16)
        mcols = sb("mcols", (128, 4), F32)
        epsc = sb("epsc", (128, 1), F32)
        onef = sb("onef", (128, 1), F32)

        ln1x = sb("ln1x", (128, KT * TC), BF16)
        xcp = sb("xcp", (128, IT * TC), BF16)
        szp = sb("szp", (128, IT * TC), BF16)
        xbrg = sb("xbrg", (128, IT * (TC + 8)), BF16)
        dtp = sb("dtp", (128, IT * TC), BF16)
        wpl = sb("wpl", (128, IT * TC), BF16)
        yp = sb("yp", (128, IT * TC), BF16)
        x1b = sb("x1b", (128, KT * TC), BF16)
        pool_a = sb("pool_a", (128, KT * TC), BF16)
        pool_b = sb("pool_b", (128, KT * TC), BF16)

        u_q = sb("u_q", (128, QS * TC), BF16)
        cbc = u_q
        dA_q = sb("dA_q", (128, 2 * QS * TC), BF16)
        h_q = sb("h_q", (128, 2 * QS * TC), BF16)
        hcar = sb("hcar", (128, IT * S), F32)
        bqt = sb("bqt", (128, 2 * QS * TC), BF16)
        cqt = sb("cqt", (128, 2 * QS * TC), BF16)

        rows = sb("rows", (1, 3 * TC), F32)
        abc_a = sb("abc_a", (128, TC), BF16)
        abc_b = sb("abc_b", (128, TC), BF16)
        ssm_s = sb("ssm_s", (33, TC), BF16)

        f1 = sb("f1", (128, 32 * TC), BF16)
        f2h = sb("f2h", (128, 4 * TC), BF16)
        ln3 = sb("ln3", (128, KT * TC), BF16)
        f3 = sb("f3", (128, KT * TC), BF16)
        kwin = sb("kwin", (128, 3 * 1024), BF16)
        rshl = sb("rshl", (1, 4 * TC), BF16)
        x2t = sb("x2t", (128, TC), F32)

        p0, p1, p2, p3, p4, p5, p6, p7 = (ps(f"p{i}") for i in range(8))

        P = Plan()
        _orig_step = P.step
        def _step(eng, fn, waits=(), sem=None, k=1):
            if sem is not None:
                P.ensure(sem, nc, st)
            return _orig_step(eng, fn, waits, sem, k)
        P.step = _step
        for sname in ("sV", "sS", "sT"):
            P.ensure(sname, nc, st)

        def blk(t, i, w):
            return t[:, i * w:(i + 1) * w]

        ev_w = {}
        P.step("gpsimd", lambda e: e.dma_start(out=wbr_b, in_=wbr), sem="sWI", k=16)
        P.step("gpsimd", lambda e: e.dma_start(out=wz_b, in_=wz), sem="sWI", k=16)
        P.step("gpsimd", lambda e: e.dma_start(out=wout_b, in_=wout), sem="sWI", k=16)
        evwi = P.step("gpsimd", lambda e: e.dma_start(
            out=wx_s[:].rearrange("p (kt j) -> p kt j", kt=IT),
            in_=wx.rearrange("(kt p) j -> p kt j", p=128)), sem="sWI", k=16)
        ev_w["wbr"] = ev_w["wz"] = ev_w["wout"] = ev_w["wx"] = evwi

        def ld_cols(dst, src, n):
            return lambda e: e.dma_start(
                out=dst.rearrange("p (kt j) -> p kt j", kt=IT),
                in_=src.rearrange("(kt p) j -> p kt j", p=128),
                allow_slow_non_contiguous=True)

        P.step("sync", ld_cols(acol[:], alog, S), sem="sSM", k=16)
        P.step("sync", ld_cols(cwcol[:], convw, 4), sem="sSM", k=16)
        P.step("sync", ld_cols(cbcol[:], convb, 1), sem="sSM", k=16)
        P.step("sync", ld_cols(dtbcol[:], dtb, 1), sem="sSM", k=16)
        P.step("sync", ld_cols(dcol[:], dpar, 1), sem="sSM", k=16)
        P.step("sync", lambda e: e.dma_start(out=onec[:], in_=ones_bf), sem="sSM", k=16)
        P.step("sync", lambda e: e.dma_start(out=nhrow[:], in_=neghalf), sem="sSM", k=16)
        P.step("sync", lambda e: e.dma_start(out=mcols[:, 0:1], in_=m_e), sem="sSM", k=16)
        P.step("sync", lambda e: e.dma_start(out=mcols[:, 1:2], in_=minv_e), sem="sSM", k=16)
        P.step("sync", lambda e: e.dma_start(out=mcols[:, 2:3], in_=m_o), sem="sSM", k=16)
        ev_small = P.step("sync", lambda e: e.dma_start(out=mcols[:, 3:4], in_=minv_o), sem="sSM", k=16)
        ev_alog = ev_small

        P.step("gpsimd", lambda e: e.dma_start(
            out=scols[:, 0:1], in_=s0i[0:1, :].squeeze(0).partition_broadcast(128)), sem="sSC", k=16)
        P.step("gpsimd", lambda e: e.dma_start(
            out=scols[:, 1:2], in_=s1i[0:1, :].squeeze(0).partition_broadcast(128)), sem="sSC", k=16)
        ev_s2x = P.step("gpsimd", lambda e: e.dma_start(
            out=scols[:, 2:3], in_=s2i[0:1, :].squeeze(0).partition_broadcast(128)), sem="sSC", k=16)
        ev_s0 = ev_s1 = ev_s2 = ev_s2x
        P.step("vector", lambda e: e.tensor_scalar_mul(scols[:, 0:1], scols[:, 0:1], 2.0),
               waits=[ev_s0], sem="sV")
        P.step("vector", lambda e: e.tensor_scalar_mul(scols[:, 1:2], scols[:, 1:2], 4.0),
               waits=[ev_s1], sem="sV")
        ev_scl = P.step("vector", lambda e: e.tensor_scalar_mul(scols[:, 2:3], scols[:, 2:3], 4.0),
                        waits=[ev_s2], sem="sV")
        ev_ax = P.step("scalar", lambda e: e.activation(out=acol[:], in_=acol[:], func=AF.Exp),
                       waits=[ev_alog], sem="sS")
        ev_acol = P.step("vector", lambda e: e.tensor_scalar_mul(acol[:], acol[:], -1.0),
                         waits=[ev_ax], sem="sV")
        for i in range(IT):
            off = i * (TC + 8)
            P.step("vector", (lambda off=off: (lambda e: e.memset(xbrg[:, off:off + 8], 0.0)))(), sem="sV")
        P.step("vector", lambda e: e.memset(x1b[:], 0.0), sem="sV")
        ev_zero = P.step("vector", lambda e: e.memset(hcar[:], 0.0), sem="sV")
        ev_eps = P.step("vector", lambda e: e.memset(epsc[:], LNEPS), sem="sV")
        ev_onef = P.step("vector", lambda e: e.memset(onef[:], 1.0), sem="sV")

        ev_k1b = ev_k2b = None
        for q in range(4):
            ev_k1b = P.step("gpsimd", (lambda q=q: (lambda e: e.dma_start(
                out=k1b[q * 1024:(q + 1) * 1024, :], in_=k1u[q * 1024:(q + 1) * 1024, :])))(),
                sem="sK1", k=16)
        for q in range(2):
            ev_k2b = P.step("gpsimd", (lambda q=q: (lambda e: e.dma_start(
                out=k2b[q * 2048:(q + 1) * 2048, :], in_=k2u[q * 2048:(q + 1) * 2048, :])))(),
                sem="sK2", k=16)
        f1u8 = f1[:].bitcast(U8)
        prev_sub = None
        prev_store = None
        ev_stage_done = None
        for kt in range(KT):
            wl = [prev_sub] if prev_sub else []
            eva = P.step("sync", (lambda kt=kt: (lambda e: e.dma_start(
                out=f1u8[:, 0:4096], in_=k0t[kt * 128:(kt + 1) * 128, :])))(),
                waits=wl, sem="sK0A", k=16)
            evb = P.step("sync", (lambda kt=kt: (lambda e: e.dma_start(
                out=f1u8[:, 4096:8192], in_=k0b[kt * 128:(kt + 1) * 128, :])))(),
                waits=wl, sem="sK0B", k=16)
            evs = P.step("vector", lambda e: e.tensor_tensor(
                f1[:, 4096:8192], f1u8[:, 0:4096], f1u8[:, 4096:8192], ALU.subtract),
                waits=[eva, evb] + ([prev_store] if prev_store else []), sem="sV")
            evd = P.step("gpsimd", (lambda kt=kt: (lambda e: e.dma_start(
                out=w0e[kt], in_=f1[:, 4096:8192])))(),
                waits=[evs], sem="sSTG", k=16)
            prev_sub, prev_store = evs, evd
            ev_stage_done = evd
        ev_f1_free = ev_stage_done

        pg = {}
        wwin_guard = None
        p2_guard = None
        p34_guard = None
        ln_dma_guard = None
        abc_guard = None
        abc2_guard = None
        prev_halo = [None] * IT
        halo_src_guard = [None] * IT
        prev_carry = {}
        prev_scan = {}
        hq_user = {}
        dA_user = {}
        uq_user = None
        bq_user = {}
        cq_user = {}
        ev_dt = [None] * IT
        ev_wpl = [None] * IT
        ev_ssm_out_prev = None
        t3_uq_guard = {0: None, 1: None, 2: None, 3: None}
        ev_dt_prev_last = None
        chunk_done = None
        ev_ffn_slotbuf = None
        f2h_guard = None
        x2t_guard = None
        rsf_guard = None
        kwin_user = {0: None, 1: None, 2: None}
        st_conv = [None] * IT
        st_zev = [None] * IT
        st_z_mm = None
        st_brev = [None] * IT
        st_ar1 = None
        st_a_stats = None
        st_ln1 = None
        st_t3_mm = None
        st_x1c = None
        ev_g_new = [None] * IT
        ev_y = [None] * IT
        ev_xr = None
        eba2 = ebb2 = None
        evB2 = None
        ev_xm = None

        def mm_stats(dst, getsrc, kt, first, last, N=TC):
            return lambda e: e.matmul(dst[0:1, 0:N], onec[:, 0:1], getsrc(),
                                      start=first, stop=last)

        def ln_rows(idx, w, which=0):
            nonlocal ln_dma_guard, p34_guard, abc_guard, abc2_guard
            if which == 0:
                dst_a, dst_b = abc_a[:, 0:w], abc_b[:, 0:w]
            else:
                dst_a, dst_b = cqt[:, 4 * TC:4 * TC + w], cqt[:, 5 * TC:5 * TC + w]
            r_mu = rows[0:1, 0:w]
            r_v = rows[0:1, TC:TC + w]
            r_a = rows[0:1, 2 * TC:2 * TC + w]
            r_b = r_v
            inv = 1.0 / DM
            wl = [("sT", P.sems["sT"][1])]
            if ln_dma_guard:
                wl.append(ln_dma_guard)
            e1 = P.step("vector", lambda e: e.tensor_scalar_mul(r_mu, p3[0:1, 0:w], inv),
                        waits=wl, sem="sV")
            e2 = P.step("vector", lambda e: e.tensor_scalar_mul(r_v, p4[0:1, 0:w], inv),
                        sem="sV")
            P.step("vector", lambda e: e.tensor_tensor(r_a, r_mu, r_mu, ALU.mult), sem="sV")
            ev_v = P.step("vector", lambda e: e.tensor_tensor(r_v, r_v, r_a, ALU.subtract), sem="sV")
            P.step("scalar", lambda e: e.activation(out=r_v, in_=r_v, func=AF.Ln, bias=epsc[0:1, 0:1]),
                   waits=[ev_v, ev_eps], sem="sS")
            es = P.step("scalar", lambda e: e.activation(out=r_a, in_=r_v, func=AF.Exp, scale=-0.5),
                        sem="sS")
            ea = es
            eb = P.step("vector", lambda e: e.tensor_tensor(r_b, r_mu, r_a, ALU.mult),
                        waits=[es], sem="sV")
            ed0 = P.step("sync", lambda e: e.dma_start(out=lnr[idx, 0:1, 0:w], in_=r_b),
                         waits=[ea, eb], sem="sRWB", k=16)
            ed1 = P.step("sync", lambda e: e.dma_start(out=lnr[idx, 1:2, 0:w], in_=r_a),
                         waits=[ea, eb], sem="sRWA", k=16)
            g = abc_guard if which == 0 else abc2_guard
            gl = [g] if g else []
            if which == 1:
                gl.append(cq_user.get(1))
            eba = P.step("gpsimd", (lambda dst_a=dst_a: (lambda e: e.dma_start(
                out=dst_a, in_=lnr[idx, 1:2, 0:w].squeeze(0).partition_broadcast(128))))(),
                waits=[ed1] + gl, sem="sABA", k=16)
            ebb = P.step("gpsimd", (lambda dst_b=dst_b: (lambda e: e.dma_start(
                out=dst_b, in_=lnr[idx, 0:1, 0:w].squeeze(0).partition_broadcast(128))))(),
                waits=[ed0] + gl, sem="sABB", k=16)
            ln_dma_guard = ed0
            p34_guard = e2 if e2 else e1
            return eba, ebb

        ev_out_final = None
        pend = [None]

        def pump(n=1):
            g = pend[0]
            if g is None:
                return
            for _ in range(n):
                try:
                    next(g)
                except StopIteration:
                    pend[0] = None
                    return

        def drain():
            g = pend[0]
            if g is not None:
                for _ in g:
                    pass
                pend[0] = None

        def emit_ffn(k, ev_blend):
            nonlocal rsf_guard, x2t_guard, abc_guard, f2h_guard, ev_ffn_slotbuf
            x1s = x1b
            bkey = ["p0", "p1", "p5", "p7"]
            banks4 = [p0, p1, p5, p7]
            ev_q3 = []
            for kt in range(KT):
                ev_q3.append(P.step("scalar", (lambda kt=kt: (lambda e: e.activation(
                    out=blk(f1, kt, TC), in_=blk(x1s, kt, TC), func=AF.Square)))(),
                    waits=[ev_blend, f2h_guard], sem="sS"))
            evA = evB = None
            for kt in range(KT):
                evA = P.step("tensor", mm_stats(p3, (lambda kt=kt: lambda: blk(x1s, kt, TC))(), kt, kt == 0, kt == KT - 1),
                             waits=([ev_blend, p34_guard] if kt == 0 else []),
                             sem="sT" if kt == KT - 1 else None)
            for kt in range(KT):
                evB = P.step("tensor", mm_stats(p4, (lambda kt=kt: lambda: blk(f1, kt, TC))(), kt, kt == 0, kt == KT - 1),
                             waits=[ev_q3[kt]], sem="sT" if kt == KT - 1 else None)
            eba3, ebb3 = ln_rows(2 * NCH + k, TC)
            ev_ln3 = None
            for kt in range(KT):
                P.step("vector", (lambda kt=kt: (lambda e: e.tensor_tensor(
                    blk(ln3, kt, TC), blk(x1s, kt, TC), abc_a[:, 0:TC], ALU.mult)))(),
                    waits=[eba3], sem="sV")
                ev_ln3 = P.step("vector", (lambda kt=kt: (lambda e: e.tensor_tensor(
                    blk(ln3, kt, TC), blk(ln3, kt, TC), abc_b[:, 0:TC], ALU.subtract)))(),
                    waits=[ebb3], sem="sV")
            abc_guard = ev_ln3
            f1_stats_done = evB
            yield

            ev_f1 = [None] * 32
            for mg in range(8):
                ev_l1mm = [None] * 4
                for kq in range(4):
                    wpar = (mg * 4 + kq) % 3
                    kw = kwin[:, wpar * 1024:(wpar + 1) * 1024]
                    evl_k = P.step("sync", (lambda mg=mg, kq=kq, kw=kw: (lambda e: e.dma_start(
                        out=kw.rearrange("p (kt j) -> p kt j", kt=2),
                        in_=w0e[kq * 2:(kq + 1) * 2, :, mg * 512:(mg + 1) * 512].rearrange("kt p j -> p kt j"))))(),
                        waits=[ev_stage_done, kwin_user[wpar]], sem="sKW%d" % wpar, k=16)
                    for mt in range(4):
                        m = mg * 4 + mt
                        bank = banks4[mt]
                        for kt in range(2):
                            wl = []
                            if kt == 0:
                                wl = [evl_k, ev_ln3, ev_f1_free]
                                if kq == 0:
                                    wl.append(pg.get(bkey[mt]))
                                    if mg == 0:
                                        wl.append(f1_stats_done)
                            evm = P.step("tensor", (lambda bank=bank, kw=kw, kt=kt, kq=kq, mt=mt: (lambda e: e.matmul(
                                bank[:, 0:TC],
                                kw[:, kt * 512 + mt * 128:kt * 512 + (mt + 1) * 128],
                                blk(ln3, kq * 2 + kt, TC),
                                start=(kq == 0 and kt == 0), stop=(kq == 3 and kt == 1))))(),
                                waits=wl, sem="sT" if kt == 1 else None)
                        if kq == 3:
                            ev_l1mm[mt] = evm
                    kwin_user[wpar] = evm
                for mt in range(4):
                    m = mg * 4 + mt
                    eve = P.step("scalar", (lambda bank=banks4[mt], m=m: (lambda e: e.activation(
                        out=blk(f1, m, TC), in_=bank[:, 0:TC], func=AF.Relu,
                        scale=scols[:, 0:1])))(),
                        waits=[ev_l1mm[mt], ev_scl, f1_stats_done], sem="sS")
                    ev_f1[m] = eve
                    pg[bkey[mt]] = eve
                yield
            evr = None
            for m in range(32):
                wl = [ev_f1[m]]
                if m == 0:
                    wl.append(pg.get("p6"))
                evr = P.step("tensor", (lambda m=m: (lambda e: e.matmul(
                    p6[0:1, 0:TC], onec[:, 0:1], blk(f1, m, TC),
                    start=(m == 0), stop=(m == 31))))(),
                    waits=wl, sem="sT" if m == 31 else None)
            ev_hi = P.step("scalar", lambda e: e.copy(
                out=rshl[0:1, 0:TC], in_=p6[0:1, 0:TC]), waits=[evr, rsf_guard], sem="sS")
            ev_rs1 = P.step("vector", lambda e: e.tensor_tensor(
                rshl[0:1, TC:2 * TC], p6[0:1, 0:TC], rshl[0:1, 0:TC], ALU.subtract),
                waits=[ev_hi], sem="sV")
            pg["p6"] = ev_rs1
            rsf_guard = ev_rs1
            yield

            ev_f3 = [None] * KT
            f2_prev = [f1_stats_done] * 4
            for mh in range(8):
                ev_f2 = [None] * 4
                evm_fin = [None] * 4
                for kq in range(8):
                    pass
                for kq in range(16):
                    wpar = (mh * 16 + kq) % 3
                    kw = kwin[:, wpar * 1024:(wpar + 1) * 1024]
                    evl = P.step("sync", (lambda kq=kq, mh=mh, kw=kw: (lambda e: e.dma_start(
                        out=kw.rearrange("p (kt j) -> p kt j", kt=2),
                        in_=k1b[kq * 256:(kq + 1) * 256, mh * 512:(mh + 1) * 512]
                        .rearrange("(kt p) j -> p kt j", p=128))))(),
                        waits=[ev_k1b, kwin_user[wpar]], sem="sKW%d" % wpar, k=16)
                    for mt in range(4):
                        bank = banks4[mt]
                        if kq == 0:
                            P.step("tensor", (lambda bank=bank: (lambda e: e.matmul(
                                bank[:, 0:TC], nhrow[0:1, :],
                                rshl[0:1, 0:TC], start=True, stop=False)))(),
                                waits=[ev_rs1, pg.get(bkey[mt]), f2_prev[mt]])
                            P.step("tensor", (lambda bank=bank: (lambda e: e.matmul(
                                bank[:, 0:TC], nhrow[0:1, :],
                                rshl[0:1, TC:2 * TC], start=False, stop=False)))())
                        evm = None
                        for kt8 in range(2):
                            evm = P.step("tensor", (lambda bank=bank, kw=kw, kt8=kt8, mt=mt, kq=kq: (lambda e: e.matmul(
                                bank[:, 0:TC],
                                kw[:, kt8 * 512 + mt * 128:kt8 * 512 + (mt + 1) * 128],
                                blk(f1, kq * 2 + kt8, TC),
                                start=False, stop=(kq == 15 and kt8 == 1))))(),
                                waits=([evl] if kt8 == 0 else []),
                                sem="sT" if kt8 == 1 else None)
                        evm_fin[mt] = evm
                    kwin_user[wpar] = evm_fin[3]
                for mt in range(4):
                    ev_f2[mt] = P.step("scalar", (lambda bank=banks4[mt], mt=mt: (lambda e: e.activation(
                        out=blk(f2h, mt, TC), in_=bank[:, 0:TC], func=AF.Relu,
                        scale=scols[:, 1:2])))(),
                        waits=[evm_fin[mt], f2_prev[mt]], sem="sS")
                    pg[bkey[mt]] = ev_f2[mt]
                evr = None
                for mt in range(4):
                    wl = [ev_f2[mt]]
                    if mt == 0:
                        wl.append(pg.get("p6"))
                    evr = P.step("tensor", (lambda mt=mt: (lambda e: e.matmul(
                        p6[0:1, 0:TC], onec[:, 0:1], blk(f2h, mt, TC),
                        start=(mt == 0), stop=(mt == 3))))(),
                        waits=wl, sem="sT" if mt == 3 else None)
                ev_hi2 = P.step("scalar", lambda e: e.copy(
                    out=rshl[0:1, 2 * TC:3 * TC], in_=p6[0:1, 0:TC]),
                    waits=[evr, rsf_guard], sem="sS")
                ev_rs2 = P.step("vector", lambda e: e.tensor_tensor(
                    rshl[0:1, 3 * TC:4 * TC], p6[0:1, 0:TC], rshl[0:1, 2 * TC:3 * TC], ALU.subtract),
                    waits=[ev_hi2], sem="sV")
                pg["p6"] = ev_rs2
                rsf_guard = ev_rs2
                yield
                for jg in range(2):
                    evm_fin = [None] * 4
                    for kq in range(2):
                        wpar = (mh * 4 + jg * 2 + kq) % 3
                        kw = kwin[:, wpar * 1024:(wpar + 1) * 1024]
                        krow = mh * 512 + kq * 256
                        evl = P.step("sync", (lambda krow=krow, jg=jg, kw=kw: (lambda e: e.dma_start(
                            out=kw.rearrange("p (kt j) -> p kt j", kt=2),
                            in_=k2b[krow:krow + 256, jg * 512:(jg + 1) * 512]
                            .rearrange("(kt p) j -> p kt j", p=128))))(),
                            waits=[ev_k2b, kwin_user[wpar]], sem="sKW%d" % wpar, k=16)
                        for jt in range(4):
                            j = jg * 4 + jt
                            bank = banks4[jt]
                            if kq == 0:
                                P.step("tensor", (lambda bank=bank: (lambda e: e.matmul(
                                    bank[:, 0:TC], nhrow[0:1, :],
                                    rshl[0:1, 2 * TC:3 * TC], start=True, stop=False)))(),
                                    waits=[ev_rs2, pg.get(bkey[jt])])
                                rsk1 = P.step("tensor", (lambda bank=bank: (lambda e: e.matmul(
                                    bank[:, 0:TC], nhrow[0:1, :],
                                    rshl[0:1, 3 * TC:4 * TC], start=False, stop=False)))(),
                                    sem="sT")
                                rsf_guard = rsk1
                            evm = None
                            for kt8 in range(2):
                                evm = P.step("tensor", (lambda bank=bank, kw=kw, kt8=kt8, jt=jt, kq=kq: (lambda e: e.matmul(
                                    bank[:, 0:TC],
                                    kw[:, kt8 * 512 + jt * 128:kt8 * 512 + (jt + 1) * 128],
                                    blk(f2h, kq * 2 + kt8, TC),
                                    start=False, stop=(kq == 1 and kt8 == 1))))(),
                                    waits=([evl] if kt8 == 0 else []),
                                    sem="sT" if kt8 == 1 else None)
                            evm_fin[jt] = evm
                        kwin_user[wpar] = evm_fin[3]
                    for jt in range(4):
                        j = jg * 4 + jt
                        if mh == 0:
                            ev_f3[j] = P.step("scalar", (lambda bank=banks4[jt], j=j: (lambda e: e.activation(
                                out=blk(f3, j, TC), in_=bank[:, 0:TC],
                                func=AF.Identity, scale=scols[:, 2:3])))(),
                                waits=[evm_fin[jt]], sem="sS")
                        else:
                            ev_f3[j] = P.step("vector", (lambda bank=banks4[jt], j=j: (lambda e: e.scalar_tensor_tensor(
                                blk(f3, j, TC), bank[:, 0:TC],
                                scols[:, 2:3], blk(f3, j, TC), ALU.mult, ALU.add)))(),
                                waits=[evm_fin[jt], ("sS", P.sems["sS"][1])], sem="sV")
                        pg[bkey[jt]] = ev_f3[j]
                    yield
                f2_prev = [evm_fin[jt] for jt in range(4)]
            f2h_guard = ev_f3[-1]
            ev_q4 = []
            for kt in range(KT):
                ev_q4.append(P.step("scalar", (lambda kt=kt: (lambda e: e.activation(
                    out=blk(f1, kt, TC), in_=blk(f3, kt, TC), func=AF.Square)))(),
                    waits=[ev_f3[kt], ("sT", P.sems["sT"][1])], sem="sS"))
            evA = evB = None
            for kt in range(KT):
                evA = P.step("tensor", mm_stats(p3, (lambda kt=kt: lambda: blk(f3, kt, TC))(), kt, kt == 0, kt == KT - 1),
                             waits=([ev_f3[kt], p34_guard] if kt == 0 else [ev_f3[kt]]),
                             sem="sT" if kt == KT - 1 else None)
            for kt in range(KT):
                evB = P.step("tensor", mm_stats(p4, (lambda kt=kt: lambda: blk(f1, kt, TC))(), kt, kt == 0, kt == KT - 1),
                             waits=[ev_q4[kt]], sem="sT" if kt == KT - 1 else None)
            eba4, ebb4 = ln_rows(3 * NCH + k, TC)
            last_norm = None
            yield
            for kt in range(KT):
                P.step("vector", (lambda kt=kt: (lambda e: e.tensor_tensor(
                    x2t[:, 0:TC], blk(f3, kt, TC), abc_a[:, 0:TC], ALU.mult)))(),
                    waits=[eba4, x2t_guard], sem="sV")
                P.step("vector", (lambda kt=kt: (lambda e: e.tensor_tensor(
                    x2t[:, 0:TC], x2t[:, 0:TC], abc_b[:, 0:TC], ALU.subtract)))(),
                    waits=[ebb4], sem="sV")
                evx2 = P.step("vector", (lambda kt=kt: (lambda e: e.tensor_tensor(
                    x2t[:, 0:TC], x2t[:, 0:TC], blk(x1s, kt, TC), ALU.add)))(), sem="sV")
                last_norm = evx2
                evst = P.step("sync", (lambda kt=kt, k=k: (lambda e: e.dma_start(
                    out=outT[kt * 128:(kt + 1) * 128, k * TC:(k + 1) * TC],
                    in_=x2t[:, 0:TC])))(),
                    waits=[evx2], sem="sST", k=16)
                ev_out_final = evst
                x2t_guard = evst
            abc_guard = last_norm
            ev_ffn_slotbuf = last_norm

        def xsrc_of(cn):
            return xT[:, cn * TC:(cn + 1) * TC].rearrange("(kt p) t -> p kt t", p=128)

        def emit_a(cn):
            nonlocal st_a_stats, st_ln1, abc_guard
            ev_xl = P.step("gpsimd", (lambda cn=cn: (lambda e: e.dma_start(
                out=ln1x[:].rearrange("p (kt t) -> p kt t", kt=KT), in_=xsrc_of(cn))))(),
                waits=[st_z_mm], sem="sXL", k=16)
            ev_sq = []
            for kt in range(KT):
                ev_sq.append(P.step("scalar", (lambda kt=kt: (lambda e: e.activation(
                    out=blk(pool_b, kt, TC), in_=blk(ln1x, kt, TC), func=AF.Square)))(),
                    waits=[ev_xl, chunk_done], sem="sS"))
            evA = evB = None
            for kt in range(KT):
                evA = P.step("tensor", mm_stats(p3, (lambda kt=kt: lambda: blk(ln1x, kt, TC))(), kt, kt == 0, kt == KT - 1),
                             waits=([ev_xl, ev_small, p34_guard] if kt == 0 else []),
                             sem="sT" if kt == KT - 1 else None)
            for kt in range(KT):
                evB = P.step("tensor", mm_stats(p4, (lambda kt=kt: lambda: blk(pool_b, kt, TC))(), kt, kt == 0, kt == KT - 1),
                             waits=[ev_sq[kt]], sem="sT" if kt == KT - 1 else None)
            st_a_stats = evB
            eba, ebb = ln_rows(cn, TC)
            ev_ln1 = None
            for kt in range(KT):
                P.step("vector", (lambda kt=kt: (lambda e: e.tensor_tensor(
                    blk(ln1x, kt, TC), blk(ln1x, kt, TC), abc_a[:, 0:TC], ALU.mult)))(),
                    waits=[eba, evA, ev_sq[kt]], sem="sV")
                ev_ln1 = P.step("vector", (lambda kt=kt: (lambda e: e.tensor_tensor(
                    blk(ln1x, kt, TC), blk(ln1x, kt, TC), abc_b[:, 0:TC], ALU.subtract)))(),
                    waits=[ebb], sem="sV")
            st_ln1 = ev_ln1
            abc_guard = ev_ln1

        def emit_wbr_load():
            return P.step("sync", lambda e: e.dma_start(
                out=wwin[:].rearrange("p (kt j) -> p kt j", kt=KT),
                in_=wbr_b.rearrange("(kt p) j -> p kt j", p=128)),
                waits=[ev_w["wbr"], wwin_guard], sem="sWW", k=16)

        def emit_b_pair(j, ev_ldbr):
            nonlocal p2_guard, wwin_guard
            evm = None
            for kt in range(KT):
                wl = [st_ln1, ev_ldbr, p2_guard] if kt == 0 else []
                evm = P.step("tensor", (lambda kt=kt, j=j: (lambda e: e.matmul(
                    p2[:, 0:TC], wwin[:, kt * DI + j * 128:kt * DI + (j + 1) * 128],
                    blk(ln1x, kt, TC), start=(kt == 0), stop=(kt == KT - 1))))(),
                    waits=wl, sem="sT" if kt == KT - 1 else None)
            wwin_guard = evm
            off = j * (TC + 8) + 3
            eve = P.step("scalar", (lambda off=off: (lambda e: e.copy(
                out=xbrg[:, off:off + TC], in_=p2[:, 0:TC])))(),
                waits=[evm, prev_halo[j], halo_src_guard[j], ev_zero], sem="sS")
            p2_guard = eve
            st_brev[j] = eve

        def emit_z(cn):
            nonlocal wwin_guard, st_z_mm
            evl = P.step("sync", lambda e: e.dma_start(
                out=wwin[:].rearrange("p (kt j) -> p kt j", kt=KT),
                in_=wz_b.rearrange("(kt p) j -> p kt j", p=128)),
                waits=[ev_w["wz"], wwin_guard], sem="sWW", k=16)
            evm = None
            for j in range(IT):
                bank = [p0, p1][j % 2]
                bn = "p%d" % (j % 2)
                for kt in range(KT):
                    wl = [st_ln1, evl, pg.get(bn)] if kt == 0 else []
                    evm = P.step("tensor", (lambda bank=bank, kt=kt, j=j: (lambda e: e.matmul(
                        bank[:, 0:TC], wwin[:, kt * DI + j * 128:kt * DI + (j + 1) * 128],
                        blk(ln1x, kt, TC), start=(kt == 0), stop=(kt == KT - 1))))(),
                        waits=wl, sem="sT" if kt == KT - 1 else None)
                eve = P.step("scalar", (lambda bank=bank, j=j: (lambda e: e.activation(
                    out=blk(szp, j, TC), in_=bank[:, 0:TC], func=AF.Silu)))(),
                    waits=[evm, st_x1c], sem="sS")
                pg[bn] = eve
                st_zev[j] = eve
                wwin_guard = evm
            st_z_mm = evm

        def emit_c(cn, first):
            nonlocal p2_guard, ev_ssm_out_prev, st_ar1
            for i in range(IT):
                off = i * (TC + 8)
                e0 = P.step("vector", (lambda i=i, off=off: (lambda e: e.tensor_scalar_mul(
                    blk(xcp, i, TC), xbrg[:, off:off + TC], cwcol[:, i * 4:i * 4 + 1])))(),
                    waits=[st_brev[i], (ev_zero if first else prev_halo[i])], sem="sV")
                for tau in range(1, 4):
                    e0 = P.step("vector", (lambda i=i, off=off, tau=tau: (lambda e: e.scalar_tensor_tensor(
                        blk(xcp, i, TC), xbrg[:, off + tau:off + tau + TC],
                        cwcol[:, i * 4 + tau:i * 4 + tau + 1],
                        blk(xcp, i, TC), ALU.mult, ALU.add)))(), sem="sV")
                halo_src_guard[i] = e0
                evs = P.step("scalar", (lambda i=i: (lambda e: e.activation(
                    out=blk(xcp, i, TC), in_=blk(xcp, i, TC), func=AF.Silu,
                    bias=cbcol[:, i:i + 1])))(), waits=[e0], sem="sS")
                st_conv[i] = evs
                evh = P.step("scalar", (lambda off=off: (lambda e: e.copy(
                    out=xbrg[:, off:off + 3], in_=xbrg[:, off + TC:off + TC + 3])))(),
                    waits=[e0], sem="sS")
                prev_halo[i] = evh
            evm = None
            for kt in range(IT):
                wl = [st_conv[kt], ev_w["wx"]]
                if kt == 0:
                    wl.append(p2_guard)
                evm = P.step("tensor", (lambda kt=kt: (lambda e: e.matmul(
                    p2[0:33, 0:TC], wx_s[:, kt * 33:(kt + 1) * 33], blk(xcp, kt, TC),
                    start=(kt == 0), stop=(kt == IT - 1))))(),
                    waits=wl, sem="sT" if kt == IT - 1 else None)
            ev_sevac = P.step("vector", lambda e: e.tensor_copy(ssm_s[:, 0:TC], p2[0:33, 0:TC]),
                              waits=[evm, ev_ssm_out_prev], sem="sV")
            p2_guard = ev_sevac
            ev_sout = P.step("gpsimd", (lambda cn=cn: (lambda e: e.dma_start(
                out=ssm_i[cn], in_=ssm_s[:, 0:TC])))(), waits=[ev_sevac], sem="sSSM", k=16)
            ev_ssm_out_prev = ev_sout
            st_ar1 = P.step("gpsimd", (lambda cn=cn: (lambda e: e.collective_compute(
                "AllReduce", ALU.add, replica_groups=GROUPS,
                ins=[ssm_i[cn]], outs=[ssm_o[cn]])))(), waits=[ev_sout], sem="sCCS")

        def emit_tail(cp):
            nonlocal st_x1c, x2t_guard, chunk_done, abc2_guard
            drain()
            ev_x1c = [None] * KT
            for kt in range(KT):
                P.step("vector", (lambda kt=kt: (lambda e: e.tensor_tensor(
                    blk(pool_b, kt, TC), blk(pool_a, kt, TC), cqt[:, 4 * TC:5 * TC], ALU.mult)))(),
                    waits=[eba2, evB2, st_t3_mm], sem="sV")
                P.step("vector", (lambda kt=kt: (lambda e: e.tensor_tensor(
                    blk(pool_b, kt, TC), blk(pool_b, kt, TC), cqt[:, 5 * TC:6 * TC], ALU.subtract)))(),
                    waits=[ebb2], sem="sV")
                ev_x1c[kt] = P.step("vector", (lambda kt=kt: (lambda e: e.tensor_tensor(
                    blk(pool_b, kt, TC), blk(pool_b, kt, TC), blk(szp, kt, TC), ALU.add)))(),
                    waits=[ev_xr], sem="sV")
            st_x1c = ev_x1c[KT - 1]
            abc2_guard = ev_x1c[KT - 1]
            mc = 0 if cp % 2 == 0 else 2
            ev_blend = None
            for kt in range(KT):
                P.step("vector", (lambda kt=kt, mc=mc: (lambda e: e.tensor_scalar_mul(
                    x2t[:, 0:TC], blk(x1b, kt, TC), mcols[:, mc + 1:mc + 2])))(),
                    waits=[ev_x1c[kt], ev_ffn_slotbuf, x2t_guard], sem="sV")
                ev_blend = P.step("vector", (lambda kt=kt, mc=mc: (lambda e: e.scalar_tensor_tensor(
                    blk(x1b, kt, TC), blk(pool_b, kt, TC), mcols[:, mc:mc + 1],
                    x2t[:, 0:TC], ALU.mult, ALU.add)))(), sem="sV")
            x2t_guard = ev_blend
            chunk_done = ev_blend
            if cp % 2 == 1:
                pend[0] = emit_ffn(cp // 2, ev_blend)

        emit_a(0)
        ev_ldbr = emit_wbr_load()
        for j in range(IT):
            emit_b_pair(j, ev_ldbr)
        emit_c(0, True)

        for c in range(NCH):
            ev_cbc = P.step("gpsimd", (lambda c=c: (lambda e: e.dma_start(
                out=cbc[:, 0:TC], in_=ssm_o[c][32:33, :].squeeze(0).partition_broadcast(128))))(),
                waits=[st_ar1, ev_dt_prev_last, uq_user,
                       t3_uq_guard[0], t3_uq_guard[1], t3_uq_guard[2], t3_uq_guard[3]], sem="sCB", k=16)
            for i in range(IT):
                P.step("scalar", (lambda i=i: (lambda e: e.activation(
                    out=blk(dtp, i, TC), in_=cbc[:, 0:TC], func=AF.Identity,
                    bias=dtbcol[:, i:i + 1])))(),
                    waits=[ev_cbc, ev_wpl[i], ev_g_new[i]], sem="sS")
                P.step("scalar", (lambda i=i: (lambda e: e.activation(
                    out=blk(yp, i, TC), in_=blk(dtp, i, TC), func=AF.Abs)))(), sem="sS")
                P.step("scalar", (lambda i=i: (lambda e: e.activation(
                    out=blk(yp, i, TC), in_=blk(yp, i, TC), func=AF.Exp, scale=-1.0)))(), sem="sS")
                P.step("scalar", (lambda i=i: (lambda e: e.activation(
                    out=blk(yp, i, TC), in_=blk(yp, i, TC), func=AF.Ln,
                    bias=onef[:, 0:1])))(), waits=[ev_onef], sem="sS")
                ev_relu = P.step("scalar", (lambda i=i: (lambda e: e.activation(
                    out=blk(dtp, i, TC), in_=blk(dtp, i, TC), func=AF.Relu)))(), sem="sS")
                ev_dt[i] = P.step("vector", (lambda i=i: (lambda e: e.tensor_tensor(
                    blk(dtp, i, TC), blk(dtp, i, TC), blk(yp, i, TC), ALU.add)))(),
                    waits=[ev_relu], sem="sV")
                ev_wpl[i] = P.step("vector", (lambda i=i: (lambda e: e.tensor_tensor(
                    blk(wpl, i, TC), blk(dtp, i, TC), blk(xcp, i, TC), ALU.mult)))(),
                    waits=[st_conv[i]], sem="sV")
            ev_dt_prev_last = ev_dt[IT - 1]

            bidx = 0
            for qg in range(NQ):
                qpar = qg % 2
                evb = P.step("gpsimd", (lambda c=c, qg=qg, qpar=qpar: (lambda e: e.dma_start(
                    out=bqt[:, qpar * QS * TC:(qpar + 1) * QS * TC].rearrange("p (s t) -> p s t", s=QS),
                    in_=ssm_o[c][qg * QS:(qg + 1) * QS, :].partition_broadcast(128))))(),
                    waits=[st_ar1, bq_user.get(qpar)], sem="sBQ%d" % qpar, k=16)
                evcq = P.step("gpsimd", (lambda c=c, qg=qg, qpar=qpar: (lambda e: e.dma_start(
                    out=cqt[:, qpar * QS * TC:(qpar + 1) * QS * TC].rearrange("p (s t) -> p s t", s=QS),
                    in_=ssm_o[c][S + qg * QS:S + (qg + 1) * QS, :].partition_broadcast(128))))(),
                    waits=[st_ar1, cq_user.get(qpar)] + ([abc2_guard] if qpar == 1 else []),
                    sem="sCQ%d" % qpar, k=16)
                last_u = None
                for i in range(IT):
                    it_par = (qg * IT + i) % 2
                    hbase = it_par * QS * TC
                    ev_exps = []
                    for s4 in range(QS):
                        sidx = qg * QS + s4
                        ev_exps.append(P.step("scalar",
                            (lambda i=i, s4=s4, sidx=sidx, it_par=it_par: (lambda e: e.activation(
                                out=dA_q[:, it_par * QS * TC + s4 * TC:it_par * QS * TC + (s4 + 1) * TC],
                                in_=blk(dtp, i, TC), func=AF.Exp,
                                scale=acol[:, i * S + sidx:i * S + sidx + 1])))(),
                            waits=[ev_dt[i], ev_acol, dA_user.get(it_par)], sem="sS"))
                    wrep = blk(wpl, i, TC).unsqueeze(1).broadcast_to((128, QS, TC))
                    evu = P.step("vector", (lambda qpar=qpar, wrep=wrep: (lambda e: e.tensor_tensor(
                        u_q[:].rearrange("p (s t) -> p s t", s=QS), wrep,
                        bqt[:, qpar * QS * TC:(qpar + 1) * QS * TC].rearrange("p (s t) -> p s t", s=QS),
                        ALU.mult)))(),
                        waits=[ev_wpl[i], evb, uq_user, ev_dt[IT - 1],
                               t3_uq_guard[0], t3_uq_guard[1], t3_uq_guard[2], t3_uq_guard[3]], sem="sV")
                    ev_sc = None
                    for s4 in range(QS):
                        sidx = qg * QS + s4
                        ini = 0.0 if c == 0 else hcar[:, i * S + sidx:i * S + sidx + 1]
                        wl = [evu, ev_exps[s4], hq_user.get(it_par), prev_carry.get(("hp", it_par))]
                        if c > 0:
                            wl.append(prev_carry.get((i, qg)))
                        ev_sc = P.step("vector",
                            (lambda s4=s4, hbase=hbase, ini=ini: (lambda e: e.tensor_tensor_scan(
                                h_q[:, hbase + s4 * TC:hbase + (s4 + 1) * TC],
                                dA_q[:, hbase + s4 * TC:hbase + (s4 + 1) * TC],
                                u_q[:, s4 * TC:(s4 + 1) * TC], ini, ALU.mult, ALU.add)))(),
                            waits=wl, sem="sV")
                    prev_scan[(i, qg)] = ev_sc
                    dA_user[it_par] = ev_sc
                    evcar = P.step("scalar", (lambda i=i, qg=qg, hbase=hbase: (lambda e: e.copy(
                        out=hcar[:, i * S + qg * QS:i * S + (qg + 1) * QS],
                        in_=h_q[:, hbase:hbase + QS * TC].rearrange("p (s t) -> p s t", s=QS)[:, :, TC - 1])))(),
                        waits=[ev_sc], sem="sS")
                    prev_carry[(i, qg)] = evcar
                    prev_carry[("hp", it_par)] = evcar
                    evm1 = P.step("vector", (lambda qpar=qpar, hbase=hbase: (lambda e: e.tensor_tensor(
                        u_q[:], h_q[:, hbase:hbase + QS * TC],
                        cqt[:, qpar * QS * TC:(qpar + 1) * QS * TC], ALU.mult)))(),
                        waits=[evcq, ev_sc], sem="sV")
                    hq_user[it_par] = evm1
                    P.step("vector", lambda e: e.tensor_tensor(
                        u_q[:, 0:2 * TC], u_q[:, 0:2 * TC], u_q[:, 2 * TC:4 * TC], ALU.add), sem="sV")
                    if qg == 0:
                        evy = P.step("vector", (lambda i=i: (lambda e: e.tensor_tensor(
                            blk(yp, i, TC), u_q[:, 0:TC], u_q[:, TC:2 * TC], ALU.add)))(), sem="sV")
                    else:
                        P.step("vector", lambda e: e.tensor_tensor(
                            u_q[:, 0:TC], u_q[:, 0:TC], u_q[:, TC:2 * TC], ALU.add), sem="sV")
                        evy = P.step("vector", (lambda i=i: (lambda e: e.tensor_tensor(
                            blk(yp, i, TC), blk(yp, i, TC), u_q[:, 0:TC], ALU.add)))(), sem="sV")
                    ev_y[i] = evy
                    uq_user = evy
                    last_u = evu
                    pump(1)
                    if qg in (1, 2) and i % 2 == 0 and bidx < IT and c < NCH - 1:
                        emit_b_pair(bidx, ev_ldbr)
                        bidx += 1
                bq_user[qpar] = last_u
                cq_user[qpar] = evm1
                if qg == 0:
                    if c > 0:
                        emit_tail(c - 1)
                    emit_z(c)
                    if c < NCH - 1:
                        emit_a(c + 1)
                        ev_ldbr = emit_wbr_load()

            for i in range(IT):
                P.step("vector", (lambda i=i: (lambda e: e.scalar_tensor_tensor(
                    u_q[:, 0:TC], blk(xcp, i, TC), dcol[:, i:i + 1],
                    blk(yp, i, TC), ALU.mult, ALU.add)))(),
                    waits=[ev_y[i], ev_small], sem="sV")
                ev_g_new[i] = P.step("vector", (lambda i=i: (lambda e: e.tensor_tensor(
                    blk(pool_b, i, TC), u_q[:, 0:TC], blk(szp, i, TC), ALU.mult)))(),
                    waits=[st_zev[i], st_a_stats], sem="sV")
            pump(2)

            if c < NCH - 1:
                emit_c(c + 1, False)
            pump(2)

            evlw = P.step("sync", lambda e: e.dma_start(
                out=wwin[:].rearrange("p (kt j) -> p kt j", kt=IT),
                in_=wout_b.rearrange("(kt p) j -> p kt j", p=128)),
                waits=[ev_w["wout"], wwin_guard], sem="sWW", k=16)
            ev_t3d = []
            for j in range(IT):
                bank = [p5, p6][j % 2]
                evm = None
                for kt in range(IT):
                    wl = [ev_g_new[kt]]
                    if kt == 0:
                        wl += [evlw, pg.get("p%d" % (5 + j % 2))]
                    evm = P.step("tensor", (lambda bank=bank, kt=kt, j=j: (lambda e: e.matmul(
                        bank[:, 0:TC], wwin[:, kt * DM + j * 128:kt * DM + (j + 1) * 128],
                        blk(pool_b, kt, TC), start=(kt == 0), stop=(kt == IT - 1))))(),
                        waits=wl, sem="sT" if kt == IT - 1 else None)
                wwin_guard = evm
                uqb = u_q[:, (j % 4) * TC:((j % 4) + 1) * TC]
                eve3 = P.step("scalar", (lambda bank=bank, uqb=uqb: (lambda e: e.copy(
                    out=uqb, in_=bank[:, 0:TC])))(),
                    waits=[evm, t3_uq_guard[j % 4]], sem="sS")
                evd = P.step("sync", (lambda c=c, j=j, uqb=uqb: (lambda e: e.dma_start(
                    out=wo_i[c][j * 128:(j + 1) * 128, :], in_=uqb)))(),
                    waits=[eve3], sem="sT3%d" % (j % 4), k=16)
                pg["p%d" % (5 + j % 2)] = eve3
                t3_uq_guard[j % 4] = evd
                ev_t3d.append(evd)
            st_t3_mm = wwin_guard
            ev_ar2 = P.step("gpsimd", (lambda c=c: (lambda e: e.collective_compute(
                "AllReduce", ALU.add, replica_groups=GROUPS,
                ins=[wo_i[c]], outs=[wo_o[c]])))(), waits=ev_t3d, sem="sCCW")
            pump(2)

            ev_xm = P.step("gpsimd", (lambda c=c: (lambda e: e.dma_start(
                out=pool_a[:].rearrange("p (kt t) -> p kt t", kt=KT),
                in_=wo_o[c].rearrange("(kt p) t -> p kt t", p=128))))(),
                waits=[ev_ar2, st_x1c], sem="sXM", k=16)
            ev_sq2 = []
            for kt in range(KT):
                ev_sq2.append(P.step("scalar", (lambda kt=kt: (lambda e: e.activation(
                    out=blk(pool_b, kt, TC), in_=blk(pool_a, kt, TC), func=AF.Square)))(),
                    waits=[ev_xm, st_t3_mm], sem="sS"))
            for kt in range(KT):
                P.step("tensor", mm_stats(p3, (lambda kt=kt: lambda: blk(pool_a, kt, TC))(), kt, kt == 0, kt == KT - 1),
                       waits=([ev_xm, p34_guard] if kt == 0 else []),
                       sem="sT" if kt == KT - 1 else None)
            evB2 = None
            for kt in range(KT):
                evB2 = P.step("tensor", mm_stats(p4, (lambda kt=kt: lambda: blk(pool_b, kt, TC))(), kt, kt == 0, kt == KT - 1),
                              waits=[ev_sq2[kt]], sem="sT" if kt == KT - 1 else None)
            eba2, ebb2 = ln_rows(NCH + c, TC, which=1)
            ev_xr = P.step("gpsimd", (lambda c=c: (lambda e: e.dma_start(
                out=szp[:].rearrange("p (kt t) -> p kt t", kt=KT), in_=xsrc_of(c))))(),
                waits=[ev_g_new[IT - 1]], sem="sXR", k=16)
            pump(2)

        emit_tail(NCH - 1)
        drain()
        P.step("sync", lambda e: e.nop(), waits=[("sST", P.sems["sST"][1])])

        with nc.Block() as b:
            @b.sync
            def _(e):
                P.run("sync", e)

            @b.gpsimd
            def _(e):
                P.run("gpsimd", e)

            @b.scalar
            def _(e):
                P.run("scalar", e)

            @b.vector
            def _(e):
                P.run("vector", e)

            @b.tensor
            def _(e):
                P.run("tensor", e)

    return nc


def kernel(L0_attn_x, L0_attn_W_in, L0_attn_conv_w, L0_attn_conv_b, L0_attn_W_x,
           L0_attn_A_log, L0_attn_D_param, L0_attn_dt_bias, L0_attn_W_out,
           L0_ffn_k0, L0_ffn_k1, L0_ffn_k2, L0_ffn_s0, L0_ffn_s1, L0_ffn_s2):
    x = np.asarray(L0_attn_x, dtype=np.float32)
    W_in = np.asarray(L0_attn_W_in, dtype=np.float32)
    conv_w = np.asarray(L0_attn_conv_w, dtype=np.float32)
    conv_b = np.asarray(L0_attn_conv_b, dtype=np.float32)
    W_x = np.asarray(L0_attn_W_x, dtype=np.float32)
    A_log = np.asarray(L0_attn_A_log, dtype=np.float32)
    D_param = np.asarray(L0_attn_D_param, dtype=np.float32)
    dt_bias = np.asarray(L0_attn_dt_bias, dtype=np.float32)
    W_out = np.asarray(L0_attn_W_out, dtype=np.float32)
    k0 = np.ascontiguousarray(np.asarray(L0_ffn_k0)).view(np.uint8)
    k1 = np.ascontiguousarray(np.asarray(L0_ffn_k1)).view(np.uint8)
    k2 = np.ascontiguousarray(np.asarray(L0_ffn_k2)).view(np.uint8)
    s0 = np.asarray(L0_ffn_s0, dtype=np.float32).reshape(1, 1)
    s1 = np.asarray(L0_ffn_s1, dtype=np.float32).reshape(1, 1)
    s2 = np.asarray(L0_ffn_s2, dtype=np.float32).reshape(1, 1)

    if "nc" not in _NC_CACHE:
        _NC_CACHE["nc"] = build_nc()
    nc = _NC_CACHE["nc"]
    in_maps = make_in_maps(x, W_in, conv_w, conv_b, W_x, A_log, D_param,
                           dt_bias, W_out, k0, k1, k2, s0, s1, s2)

    res = run_bass_kernel_spmd(nc, in_maps, core_ids=list(range(8)))

    out = np.empty((4, NT, DM), np.float32)
    for c in range(8):
        b, dh = c // 2, c % 2
        oT = res.results[c]["outT"]
        for k in range(4):
            ch = 2 * k + dh
            out[b, ch * TC:(ch + 1) * TC, :] = oT[:, k * TC:(k + 1) * TC].T
    return out


def make_in_maps(x, W_in, conv_w, conv_b, W_x, A_log, D_param, dt_bias, W_out,
                 k0, k1, k2, s0, s1, s2):
    in_maps = []
    for c in range(8):
        b, dh = c // 2, c % 2
        sl = slice(dh * DI, (dh + 1) * DI)
        im = dict(
            xT=np.ascontiguousarray(x[b].T),
            wbr=np.ascontiguousarray(W_in[:, dh * DI:(dh + 1) * DI]),
            wz=np.ascontiguousarray(W_in[:, 2048 + dh * DI:2048 + (dh + 1) * DI]),
            wx=np.ascontiguousarray(W_x[sl]),
            convw=np.ascontiguousarray(conv_w[sl, 0, :]),
            convb=np.ascontiguousarray(conv_b[sl].reshape(DI, 1)),
            alog=np.ascontiguousarray(A_log[sl]),
            dpar=np.ascontiguousarray(D_param[sl].reshape(DI, 1)),
            dtb=np.ascontiguousarray(dt_bias[sl].reshape(DI, 1)),
            wout=np.ascontiguousarray(W_out[sl]),
            k0t=np.ascontiguousarray(k0[:DM]),
            k0b=np.ascontiguousarray(k0[DM:2 * DM]),
            k1u=k1, k2u=k2,
            s0i=s0, s1i=s1, s2i=s2,
            m_e=np.full((128, 1), 1.0 if dh == 0 else 0.0, np.float32),
            minv_e=np.full((128, 1), 0.0 if dh == 0 else 1.0, np.float32),
            m_o=np.full((128, 1), 1.0 if dh == 1 else 0.0, np.float32),
            minv_o=np.full((128, 1), 0.0 if dh == 1 else 1.0, np.float32),
            ones_bf=np.ones((128, 1), dtype=BF),
            neghalf=np.full((1, 128), -0.5, dtype=BF),
        )
        in_maps.append(im)
    return in_maps


if __name__ == "__main__":
    build_nc()
    print("built ok")


# version 10
# speedup vs baseline: 1.2850x; 1.0275x over previous
import numpy as np
import ml_dtypes
from contextlib import ExitStack
import concourse.bass as bass
import concourse.mybir as mybir
from concourse.bass_utils import run_bass_kernel_spmd

F32 = mybir.dt.float32
BF16 = mybir.dt.bfloat16
U8 = mybir.dt.uint8
AF = mybir.ActivationFunctionType
ALU = mybir.AluOpType
BF = ml_dtypes.bfloat16

DM = 1024
DI = 1024
NT = 4096
TC = 512
NCH = NT // TC
S = 16
QS = 4
NQ = S // QS
KT = DM // 128
IT = DI // 128
HTOK = TC // 2
LNEPS = 1e-5
GROUPS = [[0, 1], [2, 3], [4, 5], [6, 7]]

_NC_CACHE = {}


class Plan:
    ENGS = ("sync", "gpsimd", "scalar", "vector", "tensor")

    def __init__(self):
        self.items = {e: [] for e in self.ENGS}
        self.sems = {}
        self.waited = {e: {} for e in self.ENGS}

    def add_sem(self, name, handle):
        self.sems[name] = [handle, 0]

    def ensure(self, name, nc, stack):
        if name not in self.sems:
            self.add_sem(name, stack.enter_context(nc.semaphore(name)))

    def step(self, eng, fn, waits=(), sem=None, k=1):
        ws = []
        for w in waits:
            if w is None:
                continue
            sn, val = w
            if val <= 0:
                continue
            if val > self.waited[eng].get(sn, 0):
                self.waited[eng][sn] = val
                ws.append((self.sems[sn][0], val))
        ev = None
        if sem is not None:
            self.sems[sem][1] += k
            ev = (sem, self.sems[sem][1])
        self.items[eng].append((fn, tuple(ws), None if sem is None else self.sems[sem][0], k))
        return ev

    def run(self, eng_name, eng):
        for fn, ws, semh, k in self.items[eng_name]:
            for (h, v) in ws:
                eng.wait_ge(h, v)
            ins = fn(eng)
            if semh is not None:
                ins.then_inc(semh, k)


def build_nc():
    nc = bass.Bass("TRN2", target_bir_lowering=False, detect_race_conditions=False)

    def inp(name, shape, dt):
        return nc.dram_tensor(name, list(shape), dt, kind="ExternalInput").ap()

    xT = inp("xT", (DM, NT), F32)
    wbr = inp("wbr", (DM, DI), F32)
    wz = inp("wz", (DM, DI), F32)
    wx = inp("wx", (DI, 33), F32)
    convw = inp("convw", (DI, 4), F32)
    convb = inp("convb", (DI, 1), F32)
    alog = inp("alog", (DI, S), F32)
    dpar = inp("dpar", (DI, 1), F32)
    dtb = inp("dtb", (DI, 1), F32)
    wout = inp("wout", (DI, DM), F32)
    k0t = inp("k0t", (DM, 4096), U8)
    k0b = inp("k0b", (DM, 4096), U8)
    k1u = inp("k1u", (4096, 4096), U8)
    k2u = inp("k2u", (4096, DM), U8)
    s0i = inp("s0i", (1, 1), F32)
    s1i = inp("s1i", (1, 1), F32)
    s2i = inp("s2i", (1, 1), F32)
    m_e = inp("m_e", (128, 1), F32)
    minv_e = inp("minv_e", (128, 1), F32)
    m_o = inp("m_o", (128, 1), F32)
    minv_o = inp("minv_o", (128, 1), F32)
    ones_bf = inp("ones_bf", (128, 1), BF16)
    neghalf = inp("neghalf", (1, 128), BF16)

    outT = nc.dram_tensor("outT", [DM, NT // 2], F32, kind="ExternalOutput").ap()

    def ihb(name, shape, dt):
        return nc.dram_tensor(name, list(shape), dt, kind="Internal").ap()

    w0e = ihb("w0e", (KT, 128, 4096), BF16)
    k1b = ihb("k1b", (4096, 4096), BF16)
    k2b = ihb("k2b", (4096, DM), BF16)
    wbr_b = ihb("wbr_b", (DM, DI), BF16)
    wz_b = ihb("wz_b", (DM, DI), BF16)
    wout_b = ihb("wout_b", (DI, DM), BF16)
    ssm_i = ihb("ssm_i", (NCH, 33, TC), BF16)
    ssm_o = ihb("ssm_o", (NCH, 33, TC), BF16)
    wo_i = ihb("wo_i", (NCH, DM, TC), BF16)
    wo_o = ihb("wo_o", (NCH, DM, TC), BF16)
    lnr = ihb("lnr", (4 * NCH, 2, TC), F32)

    st = ExitStack()
    sb = lambda n, shp, dt: st.enter_context(nc.sbuf_tensor(n, list(shp), dt))
    ps = lambda n: st.enter_context(nc.psum_tensor(n, [128, 512], F32))

    with st:
        wwin = sb("wwin", (128, KT * DI), BF16)
        wx_s = sb("wx_s", (128, IT * 33), BF16)
        acol = sb("acol", (128, IT * S), F32)
        cwcol = sb("cwcol", (128, IT * 4), F32)
        cbcol = sb("cbcol", (128, IT), F32)
        dtbcol = sb("dtbcol", (128, IT), F32)
        dcol = sb("dcol", (128, IT), F32)
        scols = sb("scols", (128, 4), F32)
        onec = sb("onec", (128, 1), BF16)
        nhrow = sb("nhrow", (1, 128), BF16)
        mcols = sb("mcols", (128, 4), F32)
        epsc = sb("epsc", (128, 1), F32)
        onef = sb("onef", (128, 1), F32)

        ln1x = sb("ln1x", (128, KT * TC), BF16)
        xcp = sb("xcp", (128, IT * TC), BF16)
        szp = sb("szp", (128, IT * TC), BF16)
        xbrg = sb("xbrg", (128, IT * (TC + 8)), BF16)
        dtp = sb("dtp", (128, IT * TC), BF16)
        wpl = sb("wpl", (128, IT * TC), BF16)
        yp = sb("yp", (128, IT * TC), BF16)
        x1b = sb("x1b", (128, KT * TC), BF16)
        pool_a = sb("pool_a", (128, KT * TC), BF16)
        pool_b = sb("pool_b", (128, KT * TC), BF16)

        u_q = sb("u_q", (128, QS * TC), BF16)
        cbc = u_q
        dA_q = sb("dA_q", (128, 2 * QS * TC), BF16)
        h_q = sb("h_q", (128, 2 * QS * TC), BF16)
        hcar = sb("hcar", (128, IT * S), F32)
        bqt = sb("bqt", (128, 2 * QS * TC), BF16)
        cqt = sb("cqt", (128, 2 * QS * TC), BF16)

        rows = sb("rows", (1, 3 * TC), F32)
        abc_a = sb("abc_a", (128, TC), BF16)
        abc_b = sb("abc_b", (128, TC), BF16)
        ssm_s = sb("ssm_s", (33, TC), BF16)

        f1 = sb("f1", (128, 32 * TC), BF16)
        f2h = sb("f2h", (128, 4 * TC), BF16)
        ln3 = sb("ln3", (128, KT * TC), BF16)
        f3 = sb("f3", (128, KT * TC), BF16)
        kwin = sb("kwin", (128, 3 * 1024), BF16)
        rshl = sb("rshl", (1, 4 * TC), BF16)
        x2t = sb("x2t", (128, TC), F32)

        p0, p1, p2, p3, p4, p5, p6, p7 = (ps(f"p{i}") for i in range(8))

        P = Plan()
        _orig_step = P.step
        def _step(eng, fn, waits=(), sem=None, k=1):
            if sem is not None:
                P.ensure(sem, nc, st)
            return _orig_step(eng, fn, waits, sem, k)
        P.step = _step
        for sname in ("sV", "sS", "sT"):
            P.ensure(sname, nc, st)

        def blk(t, i, w):
            return t[:, i * w:(i + 1) * w]

        ev_w = {}
        P.step("gpsimd", lambda e: e.dma_start(out=wbr_b, in_=wbr), sem="sWI", k=16)
        P.step("gpsimd", lambda e: e.dma_start(out=wz_b, in_=wz), sem="sWI", k=16)
        P.step("gpsimd", lambda e: e.dma_start(out=wout_b, in_=wout), sem="sWI", k=16)
        evwi = P.step("gpsimd", lambda e: e.dma_start(
            out=wx_s[:].rearrange("p (kt j) -> p kt j", kt=IT),
            in_=wx.rearrange("(kt p) j -> p kt j", p=128)), sem="sWI", k=16)
        ev_w["wbr"] = ev_w["wz"] = ev_w["wout"] = ev_w["wx"] = evwi

        def ld_cols(dst, src, n):
            return lambda e: e.dma_start(
                out=dst.rearrange("p (kt j) -> p kt j", kt=IT),
                in_=src.rearrange("(kt p) j -> p kt j", p=128),
                allow_slow_non_contiguous=True)

        P.step("sync", ld_cols(acol[:], alog, S), sem="sSM", k=16)
        P.step("sync", ld_cols(cwcol[:], convw, 4), sem="sSM", k=16)
        P.step("sync", ld_cols(cbcol[:], convb, 1), sem="sSM", k=16)
        P.step("sync", ld_cols(dtbcol[:], dtb, 1), sem="sSM", k=16)
        P.step("sync", ld_cols(dcol[:], dpar, 1), sem="sSM", k=16)
        P.step("sync", lambda e: e.dma_start(out=onec[:], in_=ones_bf), sem="sSM", k=16)
        P.step("sync", lambda e: e.dma_start(out=nhrow[:], in_=neghalf), sem="sSM", k=16)
        P.step("sync", lambda e: e.dma_start(out=mcols[:, 0:1], in_=m_e), sem="sSM", k=16)
        P.step("sync", lambda e: e.dma_start(out=mcols[:, 1:2], in_=minv_e), sem="sSM", k=16)
        P.step("sync", lambda e: e.dma_start(out=mcols[:, 2:3], in_=m_o), sem="sSM", k=16)
        ev_small = P.step("sync", lambda e: e.dma_start(out=mcols[:, 3:4], in_=minv_o), sem="sSM", k=16)
        ev_alog = ev_small

        P.step("gpsimd", lambda e: e.dma_start(
            out=scols[:, 0:1], in_=s0i[0:1, :].squeeze(0).partition_broadcast(128)), sem="sSC", k=16)
        P.step("gpsimd", lambda e: e.dma_start(
            out=scols[:, 1:2], in_=s1i[0:1, :].squeeze(0).partition_broadcast(128)), sem="sSC", k=16)
        ev_s2x = P.step("gpsimd", lambda e: e.dma_start(
            out=scols[:, 2:3], in_=s2i[0:1, :].squeeze(0).partition_broadcast(128)), sem="sSC", k=16)
        ev_s0 = ev_s1 = ev_s2 = ev_s2x
        P.step("vector", lambda e: e.tensor_scalar_mul(scols[:, 0:1], scols[:, 0:1], 2.0),
               waits=[ev_s0], sem="sV")
        P.step("vector", lambda e: e.tensor_scalar_mul(scols[:, 1:2], scols[:, 1:2], 4.0),
               waits=[ev_s1], sem="sV")
        ev_scl = P.step("vector", lambda e: e.tensor_scalar_mul(scols[:, 2:3], scols[:, 2:3], 4.0),
                        waits=[ev_s2], sem="sV")
        ev_ax = P.step("scalar", lambda e: e.activation(out=acol[:], in_=acol[:], func=AF.Exp),
                       waits=[ev_alog], sem="sS")
        ev_acol = P.step("vector", lambda e: e.tensor_scalar_mul(acol[:], acol[:], -1.0),
                         waits=[ev_ax], sem="sV")
        for i in range(IT):
            off = i * (TC + 8)
            P.step("vector", (lambda off=off: (lambda e: e.memset(xbrg[:, off:off + 8], 0.0)))(), sem="sV")
        P.step("vector", lambda e: e.memset(x1b[:], 0.0), sem="sV")
        ev_zero = P.step("vector", lambda e: e.memset(hcar[:], 0.0), sem="sV")
        ev_eps = P.step("vector", lambda e: e.memset(epsc[:], LNEPS), sem="sV")
        ev_onef = P.step("vector", lambda e: e.memset(onef[:], 1.0), sem="sV")


        pg = {}
        wwin_guard = None
        p2_guard = None
        p34_guard = None
        ln_dma_guard = None
        abc_guard = None
        abc2_guard = None
        prev_halo = [None] * IT
        halo_src_guard = [None] * IT
        prev_carry = {}
        prev_scan = {}
        hq_user = {}
        dA_user = {}
        uq_user = None
        bq_user = {}
        cq_user = {}
        ev_dt = [None] * IT
        ev_wpl = [None] * IT
        ev_ssm_out_prev = None
        t3_uq_guard = {0: None, 1: None, 2: None, 3: None}
        ev_dt_prev_last = None
        chunk_done = None
        ev_ffn_slotbuf = None
        f2h_guard = None
        x2t_guard = None
        rsf_guard = None
        kwin_user = {0: None, 1: None, 2: None}
        st_conv = [None] * IT
        st_zev = [None] * IT
        st_z_mm = None
        st_brev = [None] * IT
        st_ar1 = None
        st_a_stats = None
        st_ln1 = None
        st_t3_mm = None
        st_x1c = None
        ev_g_new = [None] * IT
        ev_y = [None] * IT
        ev_xr = None
        eba2 = ebb2 = None
        evB2 = None
        ev_xm = None

        def mm_stats(dst, getsrc, kt, first, last, N=TC):
            return lambda e: e.matmul(dst[0:1, 0:N], onec[:, 0:1], getsrc(),
                                      start=first, stop=last)

        def ln_rows(idx, w, which=0):
            nonlocal ln_dma_guard, p34_guard, abc_guard, abc2_guard
            if which == 0:
                dst_a, dst_b = abc_a[:, 0:w], abc_b[:, 0:w]
            else:
                dst_a, dst_b = cqt[:, 4 * TC:4 * TC + w], cqt[:, 5 * TC:5 * TC + w]
            r_mu = rows[0:1, 0:w]
            r_v = rows[0:1, TC:TC + w]
            r_a = rows[0:1, 2 * TC:2 * TC + w]
            r_b = r_v
            inv = 1.0 / DM
            wl = [("sT", P.sems["sT"][1])]
            if ln_dma_guard:
                wl.append(ln_dma_guard)
            e1 = P.step("vector", lambda e: e.tensor_scalar_mul(r_mu, p3[0:1, 0:w], inv),
                        waits=wl, sem="sV")
            e2 = P.step("vector", lambda e: e.tensor_scalar_mul(r_v, p4[0:1, 0:w], inv),
                        sem="sV")
            P.step("vector", lambda e: e.tensor_tensor(r_a, r_mu, r_mu, ALU.mult), sem="sV")
            ev_v = P.step("vector", lambda e: e.tensor_tensor(r_v, r_v, r_a, ALU.subtract), sem="sV")
            P.step("scalar", lambda e: e.activation(out=r_v, in_=r_v, func=AF.Ln, bias=epsc[0:1, 0:1]),
                   waits=[ev_v, ev_eps], sem="sS")
            es = P.step("scalar", lambda e: e.activation(out=r_a, in_=r_v, func=AF.Exp, scale=-0.5),
                        sem="sS")
            ea = es
            eb = P.step("vector", lambda e: e.tensor_tensor(r_b, r_mu, r_a, ALU.mult),
                        waits=[es], sem="sV")
            ed0 = P.step("sync", lambda e: e.dma_start(out=lnr[idx, 0:1, 0:w], in_=r_b),
                         waits=[ea, eb], sem="sRWB", k=16)
            ed1 = P.step("sync", lambda e: e.dma_start(out=lnr[idx, 1:2, 0:w], in_=r_a),
                         waits=[ea, eb], sem="sRWA", k=16)
            g = abc_guard if which == 0 else abc2_guard
            gl = [g] if g else []
            if which == 1:
                gl.append(cq_user.get(1))
            eba = P.step("gpsimd", (lambda dst_a=dst_a: (lambda e: e.dma_start(
                out=dst_a, in_=lnr[idx, 1:2, 0:w].squeeze(0).partition_broadcast(128))))(),
                waits=[ed1] + gl, sem="sABA", k=16)
            ebb = P.step("gpsimd", (lambda dst_b=dst_b: (lambda e: e.dma_start(
                out=dst_b, in_=lnr[idx, 0:1, 0:w].squeeze(0).partition_broadcast(128))))(),
                waits=[ed0] + gl, sem="sABB", k=16)
            ln_dma_guard = ed0
            p34_guard = e2 if e2 else e1
            return eba, ebb

        ev_out_final = None
        pend = [None]

        def pump(n=1):
            g = pend[0]
            if g is None:
                return
            for _ in range(n):
                try:
                    next(g)
                except StopIteration:
                    pend[0] = None
                    return

        def drain():
            g = pend[0]
            if g is not None:
                for _ in g:
                    pass
                pend[0] = None

        def emit_ffn(k, ev_blend):
            nonlocal rsf_guard, x2t_guard, abc_guard, f2h_guard, ev_ffn_slotbuf
            x1s = x1b
            bkey = ["p0", "p1", "p5", "p7"]
            banks4 = [p0, p1, p5, p7]
            ev_q3 = []
            for kt in range(KT):
                ev_q3.append(P.step("scalar", (lambda kt=kt: (lambda e: e.activation(
                    out=blk(f1, kt, TC), in_=blk(x1s, kt, TC), func=AF.Square)))(),
                    waits=[ev_blend, f2h_guard], sem="sS"))
            evA = evB = None
            for kt in range(KT):
                evA = P.step("tensor", mm_stats(p3, (lambda kt=kt: lambda: blk(x1s, kt, TC))(), kt, kt == 0, kt == KT - 1),
                             waits=([ev_blend, p34_guard] if kt == 0 else []),
                             sem="sT" if kt == KT - 1 else None)
            for kt in range(KT):
                evB = P.step("tensor", mm_stats(p4, (lambda kt=kt: lambda: blk(f1, kt, TC))(), kt, kt == 0, kt == KT - 1),
                             waits=[ev_q3[kt]], sem="sT" if kt == KT - 1 else None)
            eba3, ebb3 = ln_rows(2 * NCH + k, TC)
            ev_ln3 = None
            for kt in range(KT):
                P.step("vector", (lambda kt=kt: (lambda e: e.tensor_tensor(
                    blk(ln3, kt, TC), blk(x1s, kt, TC), abc_a[:, 0:TC], ALU.mult)))(),
                    waits=[eba3], sem="sV")
                ev_ln3 = P.step("vector", (lambda kt=kt: (lambda e: e.tensor_tensor(
                    blk(ln3, kt, TC), blk(ln3, kt, TC), abc_b[:, 0:TC], ALU.subtract)))(),
                    waits=[ebb3], sem="sV")
            abc_guard = ev_ln3
            f1_stats_done = evB
            yield

            ev_f1 = [None] * 32
            for mg in range(8):
                ev_l1mm = [None] * 4
                for kq in range(4):
                    wpar = (mg * 4 + kq) % 3
                    kw = kwin[:, wpar * 1024:(wpar + 1) * 1024]
                    evl_k = P.step("sync", (lambda mg=mg, kq=kq, kw=kw: (lambda e: e.dma_start(
                        out=kw.rearrange("p (kt j) -> p kt j", kt=2),
                        in_=w0e[kq * 2:(kq + 1) * 2, :, mg * 512:(mg + 1) * 512].rearrange("kt p j -> p kt j"))))(),
                        waits=[ev_stage_done, kwin_user[wpar]], sem="sKW%d" % wpar, k=16)
                    for mt in range(4):
                        m = mg * 4 + mt
                        bank = banks4[mt]
                        for kt in range(2):
                            wl = []
                            if kt == 0:
                                wl = [evl_k, ev_ln3, ev_f1_free]
                                if kq == 0:
                                    wl.append(pg.get(bkey[mt]))
                                    if mg == 0:
                                        wl.append(f1_stats_done)
                            evm = P.step("tensor", (lambda bank=bank, kw=kw, kt=kt, kq=kq, mt=mt: (lambda e: e.matmul(
                                bank[:, 0:TC],
                                kw[:, kt * 512 + mt * 128:kt * 512 + (mt + 1) * 128],
                                blk(ln3, kq * 2 + kt, TC),
                                start=(kq == 0 and kt == 0), stop=(kq == 3 and kt == 1))))(),
                                waits=wl, sem="sT" if kt == 1 else None)
                        if kq == 3:
                            ev_l1mm[mt] = evm
                    kwin_user[wpar] = evm
                for mt in range(4):
                    m = mg * 4 + mt
                    eve = P.step("scalar", (lambda bank=banks4[mt], m=m: (lambda e: e.activation(
                        out=blk(f1, m, TC), in_=bank[:, 0:TC], func=AF.Relu,
                        scale=scols[:, 0:1])))(),
                        waits=[ev_l1mm[mt], ev_scl, f1_stats_done], sem="sS")
                    ev_f1[m] = eve
                    pg[bkey[mt]] = eve
                yield
            evr = None
            for m in range(32):
                wl = [ev_f1[m]]
                if m == 0:
                    wl.append(pg.get("p6"))
                evr = P.step("tensor", (lambda m=m: (lambda e: e.matmul(
                    p6[0:1, 0:TC], onec[:, 0:1], blk(f1, m, TC),
                    start=(m == 0), stop=(m == 31))))(),
                    waits=wl, sem="sT" if m == 31 else None)
            ev_hi = P.step("scalar", lambda e: e.copy(
                out=rshl[0:1, 0:TC], in_=p6[0:1, 0:TC]), waits=[evr, rsf_guard], sem="sS")
            ev_rs1 = P.step("vector", lambda e: e.tensor_tensor(
                rshl[0:1, TC:2 * TC], p6[0:1, 0:TC], rshl[0:1, 0:TC], ALU.subtract),
                waits=[ev_hi], sem="sV")
            pg["p6"] = ev_rs1
            rsf_guard = ev_rs1
            yield

            ev_f3 = [None] * KT
            f2_prev = [f1_stats_done] * 4
            for mh in range(8):
                ev_f2 = [None] * 4
                evm_fin = [None] * 4
                for kq in range(8):
                    pass
                for kq in range(16):
                    wpar = (mh * 16 + kq) % 3
                    kw = kwin[:, wpar * 1024:(wpar + 1) * 1024]
                    evl = P.step("sync", (lambda kq=kq, mh=mh, kw=kw: (lambda e: e.dma_start(
                        out=kw.rearrange("p (kt j) -> p kt j", kt=2),
                        in_=k1b[kq * 256:(kq + 1) * 256, mh * 512:(mh + 1) * 512]
                        .rearrange("(kt p) j -> p kt j", p=128))))(),
                        waits=[ev_k1b, kwin_user[wpar]], sem="sKW%d" % wpar, k=16)
                    for mt in range(4):
                        bank = banks4[mt]
                        if kq == 0:
                            P.step("tensor", (lambda bank=bank: (lambda e: e.matmul(
                                bank[:, 0:TC], nhrow[0:1, :],
                                rshl[0:1, 0:TC], start=True, stop=False)))(),
                                waits=[ev_rs1, pg.get(bkey[mt]), f2_prev[mt]])
                            P.step("tensor", (lambda bank=bank: (lambda e: e.matmul(
                                bank[:, 0:TC], nhrow[0:1, :],
                                rshl[0:1, TC:2 * TC], start=False, stop=False)))())
                        evm = None
                        for kt8 in range(2):
                            evm = P.step("tensor", (lambda bank=bank, kw=kw, kt8=kt8, mt=mt, kq=kq: (lambda e: e.matmul(
                                bank[:, 0:TC],
                                kw[:, kt8 * 512 + mt * 128:kt8 * 512 + (mt + 1) * 128],
                                blk(f1, kq * 2 + kt8, TC),
                                start=False, stop=(kq == 15 and kt8 == 1))))(),
                                waits=([evl] if kt8 == 0 else []),
                                sem="sT" if kt8 == 1 else None)
                        evm_fin[mt] = evm
                    kwin_user[wpar] = evm_fin[3]
                for mt in range(4):
                    ev_f2[mt] = P.step("scalar", (lambda bank=banks4[mt], mt=mt: (lambda e: e.activation(
                        out=blk(f2h, mt, TC), in_=bank[:, 0:TC], func=AF.Relu,
                        scale=scols[:, 1:2])))(),
                        waits=[evm_fin[mt], f2_prev[mt]], sem="sS")
                    pg[bkey[mt]] = ev_f2[mt]
                evr = None
                for mt in range(4):
                    wl = [ev_f2[mt]]
                    if mt == 0:
                        wl.append(pg.get("p6"))
                    evr = P.step("tensor", (lambda mt=mt: (lambda e: e.matmul(
                        p6[0:1, 0:TC], onec[:, 0:1], blk(f2h, mt, TC),
                        start=(mt == 0), stop=(mt == 3))))(),
                        waits=wl, sem="sT" if mt == 3 else None)
                ev_hi2 = P.step("scalar", lambda e: e.copy(
                    out=rshl[0:1, 2 * TC:3 * TC], in_=p6[0:1, 0:TC]),
                    waits=[evr, rsf_guard], sem="sS")
                ev_rs2 = P.step("vector", lambda e: e.tensor_tensor(
                    rshl[0:1, 3 * TC:4 * TC], p6[0:1, 0:TC], rshl[0:1, 2 * TC:3 * TC], ALU.subtract),
                    waits=[ev_hi2], sem="sV")
                pg["p6"] = ev_rs2
                rsf_guard = ev_rs2
                yield
                for jg in range(2):
                    evm_fin = [None] * 4
                    for kq in range(2):
                        wpar = (mh * 4 + jg * 2 + kq) % 3
                        kw = kwin[:, wpar * 1024:(wpar + 1) * 1024]
                        krow = mh * 512 + kq * 256
                        evl = P.step("sync", (lambda krow=krow, jg=jg, kw=kw: (lambda e: e.dma_start(
                            out=kw.rearrange("p (kt j) -> p kt j", kt=2),
                            in_=k2b[krow:krow + 256, jg * 512:(jg + 1) * 512]
                            .rearrange("(kt p) j -> p kt j", p=128))))(),
                            waits=[ev_k2b, kwin_user[wpar]], sem="sKW%d" % wpar, k=16)
                        for jt in range(4):
                            j = jg * 4 + jt
                            bank = banks4[jt]
                            if kq == 0:
                                P.step("tensor", (lambda bank=bank: (lambda e: e.matmul(
                                    bank[:, 0:TC], nhrow[0:1, :],
                                    rshl[0:1, 2 * TC:3 * TC], start=True, stop=False)))(),
                                    waits=[ev_rs2, pg.get(bkey[jt])])
                                rsk1 = P.step("tensor", (lambda bank=bank: (lambda e: e.matmul(
                                    bank[:, 0:TC], nhrow[0:1, :],
                                    rshl[0:1, 3 * TC:4 * TC], start=False, stop=False)))(),
                                    sem="sT")
                                rsf_guard = rsk1
                            evm = None
                            for kt8 in range(2):
                                evm = P.step("tensor", (lambda bank=bank, kw=kw, kt8=kt8, jt=jt, kq=kq: (lambda e: e.matmul(
                                    bank[:, 0:TC],
                                    kw[:, kt8 * 512 + jt * 128:kt8 * 512 + (jt + 1) * 128],
                                    blk(f2h, kq * 2 + kt8, TC),
                                    start=False, stop=(kq == 1 and kt8 == 1))))(),
                                    waits=([evl] if kt8 == 0 else []),
                                    sem="sT" if kt8 == 1 else None)
                            evm_fin[jt] = evm
                        kwin_user[wpar] = evm_fin[3]
                    for jt in range(4):
                        j = jg * 4 + jt
                        if mh == 0:
                            ev_f3[j] = P.step("scalar", (lambda bank=banks4[jt], j=j: (lambda e: e.activation(
                                out=blk(f3, j, TC), in_=bank[:, 0:TC],
                                func=AF.Identity, scale=scols[:, 2:3])))(),
                                waits=[evm_fin[jt]], sem="sS")
                        else:
                            ev_f3[j] = P.step("vector", (lambda bank=banks4[jt], j=j: (lambda e: e.scalar_tensor_tensor(
                                blk(f3, j, TC), bank[:, 0:TC],
                                scols[:, 2:3], blk(f3, j, TC), ALU.mult, ALU.add)))(),
                                waits=[evm_fin[jt], ("sS", P.sems["sS"][1])], sem="sV")
                        pg[bkey[jt]] = ev_f3[j]
                    yield
                f2_prev = [evm_fin[jt] for jt in range(4)]
            f2h_guard = ev_f3[-1]
            ev_q4 = []
            for kt in range(KT):
                ev_q4.append(P.step("scalar", (lambda kt=kt: (lambda e: e.activation(
                    out=blk(f1, kt, TC), in_=blk(f3, kt, TC), func=AF.Square)))(),
                    waits=[ev_f3[kt], ("sT", P.sems["sT"][1])], sem="sS"))
            evA = evB = None
            for kt in range(KT):
                evA = P.step("tensor", mm_stats(p3, (lambda kt=kt: lambda: blk(f3, kt, TC))(), kt, kt == 0, kt == KT - 1),
                             waits=([ev_f3[kt], p34_guard] if kt == 0 else [ev_f3[kt]]),
                             sem="sT" if kt == KT - 1 else None)
            for kt in range(KT):
                evB = P.step("tensor", mm_stats(p4, (lambda kt=kt: lambda: blk(f1, kt, TC))(), kt, kt == 0, kt == KT - 1),
                             waits=[ev_q4[kt]], sem="sT" if kt == KT - 1 else None)
            eba4, ebb4 = ln_rows(3 * NCH + k, TC)
            last_norm = None
            yield
            for kt in range(KT):
                P.step("vector", (lambda kt=kt: (lambda e: e.tensor_tensor(
                    x2t[:, 0:TC], blk(f3, kt, TC), abc_a[:, 0:TC], ALU.mult)))(),
                    waits=[eba4, x2t_guard], sem="sV")
                P.step("vector", (lambda kt=kt: (lambda e: e.tensor_tensor(
                    x2t[:, 0:TC], x2t[:, 0:TC], abc_b[:, 0:TC], ALU.subtract)))(),
                    waits=[ebb4], sem="sV")
                evx2 = P.step("vector", (lambda kt=kt: (lambda e: e.tensor_tensor(
                    x2t[:, 0:TC], x2t[:, 0:TC], blk(x1s, kt, TC), ALU.add)))(), sem="sV")
                last_norm = evx2
                evst = P.step("sync", (lambda kt=kt, k=k: (lambda e: e.dma_start(
                    out=outT[kt * 128:(kt + 1) * 128, k * TC:(k + 1) * TC],
                    in_=x2t[:, 0:TC])))(),
                    waits=[evx2], sem="sST", k=16)
                ev_out_final = evst
                x2t_guard = evst
            abc_guard = last_norm
            ev_ffn_slotbuf = last_norm

        def xsrc_of(cn):
            return xT[:, cn * TC:(cn + 1) * TC].rearrange("(kt p) t -> p kt t", p=128)

        def emit_a(cn):
            nonlocal st_a_stats, st_ln1, abc_guard
            ev_xl = P.step("gpsimd", (lambda cn=cn: (lambda e: e.dma_start(
                out=ln1x[:].rearrange("p (kt t) -> p kt t", kt=KT), in_=xsrc_of(cn))))(),
                waits=[st_z_mm], sem="sXL", k=16)
            ev_sq = []
            for kt in range(KT):
                ev_sq.append(P.step("scalar", (lambda kt=kt: (lambda e: e.activation(
                    out=blk(pool_b, kt, TC), in_=blk(ln1x, kt, TC), func=AF.Square)))(),
                    waits=[ev_xl, chunk_done], sem="sS"))
            evA = evB = None
            for kt in range(KT):
                evA = P.step("tensor", mm_stats(p3, (lambda kt=kt: lambda: blk(ln1x, kt, TC))(), kt, kt == 0, kt == KT - 1),
                             waits=([ev_xl, ev_small, p34_guard] if kt == 0 else []),
                             sem="sT" if kt == KT - 1 else None)
            for kt in range(KT):
                evB = P.step("tensor", mm_stats(p4, (lambda kt=kt: lambda: blk(pool_b, kt, TC))(), kt, kt == 0, kt == KT - 1),
                             waits=[ev_sq[kt]], sem="sT" if kt == KT - 1 else None)
            st_a_stats = evB
            eba, ebb = ln_rows(cn, TC)
            ev_ln1 = None
            for kt in range(KT):
                P.step("vector", (lambda kt=kt: (lambda e: e.tensor_tensor(
                    blk(ln1x, kt, TC), blk(ln1x, kt, TC), abc_a[:, 0:TC], ALU.mult)))(),
                    waits=[eba, evA, ev_sq[kt]], sem="sV")
                ev_ln1 = P.step("vector", (lambda kt=kt: (lambda e: e.tensor_tensor(
                    blk(ln1x, kt, TC), blk(ln1x, kt, TC), abc_b[:, 0:TC], ALU.subtract)))(),
                    waits=[ebb], sem="sV")
            st_ln1 = ev_ln1
            abc_guard = ev_ln1

        def emit_wbr_load():
            return P.step("sync", lambda e: e.dma_start(
                out=wwin[:].rearrange("p (kt j) -> p kt j", kt=KT),
                in_=wbr_b.rearrange("(kt p) j -> p kt j", p=128)),
                waits=[ev_w["wbr"], wwin_guard], sem="sWW", k=16)

        def emit_b_pair(j, ev_ldbr):
            nonlocal p2_guard, wwin_guard
            evm = None
            for kt in range(KT):
                wl = [st_ln1, ev_ldbr, p2_guard] if kt == 0 else []
                evm = P.step("tensor", (lambda kt=kt, j=j: (lambda e: e.matmul(
                    p2[:, 0:TC], wwin[:, kt * DI + j * 128:kt * DI + (j + 1) * 128],
                    blk(ln1x, kt, TC), start=(kt == 0), stop=(kt == KT - 1))))(),
                    waits=wl, sem="sT" if kt == KT - 1 else None)
            wwin_guard = evm
            off = j * (TC + 8) + 3
            eve = P.step("scalar", (lambda off=off: (lambda e: e.copy(
                out=xbrg[:, off:off + TC], in_=p2[:, 0:TC])))(),
                waits=[evm, prev_halo[j], halo_src_guard[j], ev_zero], sem="sS")
            p2_guard = eve
            st_brev[j] = eve

        def emit_z(cn):
            nonlocal wwin_guard, st_z_mm
            evl = P.step("sync", lambda e: e.dma_start(
                out=wwin[:].rearrange("p (kt j) -> p kt j", kt=KT),
                in_=wz_b.rearrange("(kt p) j -> p kt j", p=128)),
                waits=[ev_w["wz"], wwin_guard], sem="sWW", k=16)
            evm = None
            for j in range(IT):
                bank = [p0, p1][j % 2]
                bn = "p%d" % (j % 2)
                for kt in range(KT):
                    wl = [st_ln1, evl, pg.get(bn)] if kt == 0 else []
                    evm = P.step("tensor", (lambda bank=bank, kt=kt, j=j: (lambda e: e.matmul(
                        bank[:, 0:TC], wwin[:, kt * DI + j * 128:kt * DI + (j + 1) * 128],
                        blk(ln1x, kt, TC), start=(kt == 0), stop=(kt == KT - 1))))(),
                        waits=wl, sem="sT" if kt == KT - 1 else None)
                eve = P.step("scalar", (lambda bank=bank, j=j: (lambda e: e.activation(
                    out=blk(szp, j, TC), in_=bank[:, 0:TC], func=AF.Silu)))(),
                    waits=[evm, st_x1c], sem="sS")
                pg[bn] = eve
                st_zev[j] = eve
                wwin_guard = evm
            st_z_mm = evm

        def emit_conv_i(i, first):
            off = i * (TC + 8)
            e0 = P.step("vector", (lambda i=i, off=off: (lambda e: e.tensor_scalar_mul(
                blk(xcp, i, TC), xbrg[:, off:off + TC], cwcol[:, i * 4:i * 4 + 1])))(),
                waits=[st_brev[i], (ev_zero if first else prev_halo[i])], sem="sV")
            for tau in range(1, 4):
                e0 = P.step("vector", (lambda i=i, off=off, tau=tau: (lambda e: e.scalar_tensor_tensor(
                    blk(xcp, i, TC), xbrg[:, off + tau:off + tau + TC],
                    cwcol[:, i * 4 + tau:i * 4 + tau + 1],
                    blk(xcp, i, TC), ALU.mult, ALU.add)))(), sem="sV")
            halo_src_guard[i] = e0
            evs = P.step("scalar", (lambda i=i: (lambda e: e.activation(
                out=blk(xcp, i, TC), in_=blk(xcp, i, TC), func=AF.Silu,
                bias=cbcol[:, i:i + 1])))(), waits=[e0], sem="sS")
            st_conv[i] = evs
            evh = P.step("scalar", (lambda off=off: (lambda e: e.copy(
                out=xbrg[:, off:off + 3], in_=xbrg[:, off + TC:off + TC + 3])))(),
                waits=[e0], sem="sS")
            prev_halo[i] = evh

        def emit_gc(i, c):
            P.step("vector", (lambda i=i: (lambda e: e.scalar_tensor_tensor(
                u_q[:, 0:TC], blk(xcp, i, TC), dcol[:, i:i + 1],
                blk(yp, i, TC), ALU.mult, ALU.add)))(),
                waits=[ev_y[i], ev_small], sem="sV")
            ev_g_new[i] = P.step("vector", (lambda i=i: (lambda e: e.tensor_tensor(
                blk(pool_b, i, TC), u_q[:, 0:TC], blk(szp, i, TC), ALU.mult)))(),
                waits=[st_zev[i], st_a_stats], sem="sV")
            if c < NCH - 1:
                emit_conv_i(i, False)

        def emit_ssm(cn):
            nonlocal p2_guard, ev_ssm_out_prev, st_ar1
            evm = None
            for kt in range(IT):
                wl = [st_conv[kt], ev_w["wx"]]
                if kt == 0:
                    wl.append(p2_guard)
                evm = P.step("tensor", (lambda kt=kt: (lambda e: e.matmul(
                    p2[0:33, 0:TC], wx_s[:, kt * 33:(kt + 1) * 33], blk(xcp, kt, TC),
                    start=(kt == 0), stop=(kt == IT - 1))))(),
                    waits=wl, sem="sT" if kt == IT - 1 else None)
            ev_sevac = P.step("vector", lambda e: e.tensor_copy(ssm_s[:, 0:TC], p2[0:33, 0:TC]),
                              waits=[evm, ev_ssm_out_prev], sem="sV")
            p2_guard = ev_sevac
            ev_sout = P.step("gpsimd", (lambda cn=cn: (lambda e: e.dma_start(
                out=ssm_i[cn], in_=ssm_s[:, 0:TC])))(), waits=[ev_sevac], sem="sSSM", k=16)
            ev_ssm_out_prev = ev_sout
            st_ar1 = P.step("gpsimd", (lambda cn=cn: (lambda e: e.collective_compute(
                "AllReduce", ALU.add, replica_groups=GROUPS,
                ins=[ssm_i[cn]], outs=[ssm_o[cn]])))(), waits=[ev_sout], sem="sCCS")

        def emit_tail(cp):
            nonlocal st_x1c, x2t_guard, chunk_done, abc2_guard
            drain()
            ev_x1c = [None] * KT
            for kt in range(KT):
                P.step("vector", (lambda kt=kt: (lambda e: e.tensor_tensor(
                    blk(pool_b, kt, TC), blk(pool_a, kt, TC), cqt[:, 4 * TC:5 * TC], ALU.mult)))(),
                    waits=[eba2, evB2, st_t3_mm], sem="sV")
                P.step("vector", (lambda kt=kt: (lambda e: e.tensor_tensor(
                    blk(pool_b, kt, TC), blk(pool_b, kt, TC), cqt[:, 5 * TC:6 * TC], ALU.subtract)))(),
                    waits=[ebb2], sem="sV")
                ev_x1c[kt] = P.step("vector", (lambda kt=kt: (lambda e: e.tensor_tensor(
                    blk(pool_b, kt, TC), blk(pool_b, kt, TC), blk(szp, kt, TC), ALU.add)))(),
                    waits=[ev_xr], sem="sV")
            st_x1c = ev_x1c[KT - 1]
            abc2_guard = ev_x1c[KT - 1]
            mc = 0 if cp % 2 == 0 else 2
            ev_blend = None
            for kt in range(KT):
                P.step("vector", (lambda kt=kt, mc=mc: (lambda e: e.tensor_scalar_mul(
                    x2t[:, 0:TC], blk(x1b, kt, TC), mcols[:, mc + 1:mc + 2])))(),
                    waits=[ev_x1c[kt], ev_ffn_slotbuf, x2t_guard], sem="sV")
                ev_blend = P.step("vector", (lambda kt=kt, mc=mc: (lambda e: e.scalar_tensor_tensor(
                    blk(x1b, kt, TC), blk(pool_b, kt, TC), mcols[:, mc:mc + 1],
                    x2t[:, 0:TC], ALU.mult, ALU.add)))(), sem="sV")
            x2t_guard = ev_blend
            chunk_done = ev_blend
            if cp % 2 == 1:
                pend[0] = emit_ffn(cp // 2, ev_blend)

        emit_a(0)
        ev_ldbr = emit_wbr_load()
        for j in range(IT):
            emit_b_pair(j, ev_ldbr)
        for i in range(IT):
            emit_conv_i(i, True)
        emit_ssm(0)

        ev_k1b = ev_k2b = None

        def _mk_stg(dst, srcu, q, rows, semn, which):
            def f():
                nonlocal ev_k1b, ev_k2b
                ev = P.step("gpsimd", (lambda: (lambda e: e.dma_start(
                    out=dst[q * rows:(q + 1) * rows, :],
                    in_=srcu[q * rows:(q + 1) * rows, :])))(),
                    sem=semn, k=16)
                if which == 1:
                    ev_k1b = ev
                else:
                    ev_k2b = ev
            return f

        stage_pieces = [_mk_stg(k1b, k1u, q, 256, "sK1", 1) for q in range(16)]
        stage_pieces += [_mk_stg(k2b, k2u, q, 1024, "sK2", 2) for q in range(4)]
        f1u8 = f1[:].bitcast(U8)
        prev_sub = None
        prev_store = None
        ev_stage_done = None
        for kt in range(KT):
            wl = [prev_sub] if prev_sub else []
            eva = P.step("sync", (lambda kt=kt: (lambda e: e.dma_start(
                out=f1u8[:, 0:4096], in_=k0t[kt * 128:(kt + 1) * 128, :])))(),
                waits=wl, sem="sK0A", k=16)
            evb = P.step("sync", (lambda kt=kt: (lambda e: e.dma_start(
                out=f1u8[:, 4096:8192], in_=k0b[kt * 128:(kt + 1) * 128, :])))(),
                waits=wl, sem="sK0B", k=16)
            evs = P.step("vector", lambda e: e.tensor_tensor(
                f1[:, 4096:8192], f1u8[:, 0:4096], f1u8[:, 4096:8192], ALU.subtract),
                waits=[eva, evb] + ([prev_store] if prev_store else []), sem="sV")
            evd = P.step("gpsimd", (lambda kt=kt: (lambda e: e.dma_start(
                out=w0e[kt], in_=f1[:, 4096:8192])))(),
                waits=[evs], sem="sSTG", k=16)
            prev_sub, prev_store = evs, evd
            ev_stage_done = evd
        ev_f1_free = ev_stage_done

        for c in range(NCH):
            ev_cbc = P.step("gpsimd", (lambda c=c: (lambda e: e.dma_start(
                out=cbc[:, 0:TC], in_=ssm_o[c][32:33, :].squeeze(0).partition_broadcast(128))))(),
                waits=[st_ar1, ev_dt_prev_last, uq_user,
                       t3_uq_guard[0], t3_uq_guard[1], t3_uq_guard[2], t3_uq_guard[3]], sem="sCB", k=16)
            for i in range(IT):
                P.step("scalar", (lambda i=i: (lambda e: e.activation(
                    out=blk(dtp, i, TC), in_=cbc[:, 0:TC], func=AF.Identity,
                    bias=dtbcol[:, i:i + 1])))(),
                    waits=[ev_cbc, ev_wpl[i], ev_g_new[i]], sem="sS")
                P.step("scalar", (lambda i=i: (lambda e: e.activation(
                    out=blk(yp, i, TC), in_=blk(dtp, i, TC), func=AF.Abs)))(), sem="sS")
                P.step("scalar", (lambda i=i: (lambda e: e.activation(
                    out=blk(yp, i, TC), in_=blk(yp, i, TC), func=AF.Exp, scale=-1.0)))(), sem="sS")
                P.step("scalar", (lambda i=i: (lambda e: e.activation(
                    out=blk(yp, i, TC), in_=blk(yp, i, TC), func=AF.Ln,
                    bias=onef[:, 0:1])))(), waits=[ev_onef], sem="sS")
                ev_relu = P.step("scalar", (lambda i=i: (lambda e: e.activation(
                    out=blk(dtp, i, TC), in_=blk(dtp, i, TC), func=AF.Relu)))(), sem="sS")
                ev_dt[i] = P.step("vector", (lambda i=i: (lambda e: e.tensor_tensor(
                    blk(dtp, i, TC), blk(dtp, i, TC), blk(yp, i, TC), ALU.add)))(),
                    waits=[ev_relu], sem="sV")
                ev_wpl[i] = P.step("vector", (lambda i=i: (lambda e: e.tensor_tensor(
                    blk(wpl, i, TC), blk(dtp, i, TC), blk(xcp, i, TC), ALU.mult)))(),
                    waits=[st_conv[i]], sem="sV")
            ev_dt_prev_last = ev_dt[IT - 1]

            bidx = 0
            for qg in range(NQ):
                qpar = qg % 2
                evb = P.step("gpsimd", (lambda c=c, qg=qg, qpar=qpar: (lambda e: e.dma_start(
                    out=bqt[:, qpar * QS * TC:(qpar + 1) * QS * TC].rearrange("p (s t) -> p s t", s=QS),
                    in_=ssm_o[c][qg * QS:(qg + 1) * QS, :].partition_broadcast(128))))(),
                    waits=[st_ar1, bq_user.get(qpar)], sem="sBQ%d" % qpar, k=16)
                evcq = P.step("gpsimd", (lambda c=c, qg=qg, qpar=qpar: (lambda e: e.dma_start(
                    out=cqt[:, qpar * QS * TC:(qpar + 1) * QS * TC].rearrange("p (s t) -> p s t", s=QS),
                    in_=ssm_o[c][S + qg * QS:S + (qg + 1) * QS, :].partition_broadcast(128))))(),
                    waits=[st_ar1, cq_user.get(qpar)] + ([abc2_guard] if qpar == 1 else []),
                    sem="sCQ%d" % qpar, k=16)
                last_u = None
                for i in range(IT):
                    it_par = (qg * IT + i) % 2
                    hbase = it_par * QS * TC
                    ev_exps = []
                    for s4 in range(QS):
                        sidx = qg * QS + s4
                        ev_exps.append(P.step("scalar",
                            (lambda i=i, s4=s4, sidx=sidx, it_par=it_par: (lambda e: e.activation(
                                out=dA_q[:, it_par * QS * TC + s4 * TC:it_par * QS * TC + (s4 + 1) * TC],
                                in_=blk(dtp, i, TC), func=AF.Exp,
                                scale=acol[:, i * S + sidx:i * S + sidx + 1])))(),
                            waits=[ev_dt[i], ev_acol, dA_user.get(it_par)], sem="sS"))
                    wrep = blk(wpl, i, TC).unsqueeze(1).broadcast_to((128, QS, TC))
                    evu = P.step("vector", (lambda qpar=qpar, wrep=wrep: (lambda e: e.tensor_tensor(
                        u_q[:].rearrange("p (s t) -> p s t", s=QS), wrep,
                        bqt[:, qpar * QS * TC:(qpar + 1) * QS * TC].rearrange("p (s t) -> p s t", s=QS),
                        ALU.mult)))(),
                        waits=[ev_wpl[i], evb, uq_user, ev_dt[IT - 1],
                               t3_uq_guard[0], t3_uq_guard[1], t3_uq_guard[2], t3_uq_guard[3]], sem="sV")
                    ev_sc = None
                    for s4 in range(QS):
                        sidx = qg * QS + s4
                        ini = 0.0 if c == 0 else hcar[:, i * S + sidx:i * S + sidx + 1]
                        wl = [evu, ev_exps[s4], hq_user.get(it_par), prev_carry.get(("hp", it_par))]
                        if c > 0:
                            wl.append(prev_carry.get((i, qg)))
                        ev_sc = P.step("vector",
                            (lambda s4=s4, hbase=hbase, ini=ini: (lambda e: e.tensor_tensor_scan(
                                h_q[:, hbase + s4 * TC:hbase + (s4 + 1) * TC],
                                dA_q[:, hbase + s4 * TC:hbase + (s4 + 1) * TC],
                                u_q[:, s4 * TC:(s4 + 1) * TC], ini, ALU.mult, ALU.add)))(),
                            waits=wl, sem="sV")
                    prev_scan[(i, qg)] = ev_sc
                    dA_user[it_par] = ev_sc
                    evcar = P.step("scalar", (lambda i=i, qg=qg, hbase=hbase: (lambda e: e.copy(
                        out=hcar[:, i * S + qg * QS:i * S + (qg + 1) * QS],
                        in_=h_q[:, hbase:hbase + QS * TC].rearrange("p (s t) -> p s t", s=QS)[:, :, TC - 1])))(),
                        waits=[ev_sc], sem="sS")
                    prev_carry[(i, qg)] = evcar
                    prev_carry[("hp", it_par)] = evcar
                    evm1 = P.step("vector", (lambda qpar=qpar, hbase=hbase: (lambda e: e.tensor_tensor(
                        u_q[:], h_q[:, hbase:hbase + QS * TC],
                        cqt[:, qpar * QS * TC:(qpar + 1) * QS * TC], ALU.mult)))(),
                        waits=[evcq, ev_sc], sem="sV")
                    hq_user[it_par] = evm1
                    P.step("vector", lambda e: e.tensor_tensor(
                        u_q[:, 0:2 * TC], u_q[:, 0:2 * TC], u_q[:, 2 * TC:4 * TC], ALU.add), sem="sV")
                    if qg == 0:
                        evy = P.step("vector", (lambda i=i: (lambda e: e.tensor_tensor(
                            blk(yp, i, TC), u_q[:, 0:TC], u_q[:, TC:2 * TC], ALU.add)))(), sem="sV")
                    else:
                        P.step("vector", lambda e: e.tensor_tensor(
                            u_q[:, 0:TC], u_q[:, 0:TC], u_q[:, TC:2 * TC], ALU.add), sem="sV")
                        evy = P.step("vector", (lambda i=i: (lambda e: e.tensor_tensor(
                            blk(yp, i, TC), blk(yp, i, TC), u_q[:, 0:TC], ALU.add)))(), sem="sV")
                    ev_y[i] = evy
                    uq_user = evy
                    last_u = evu
                    pump(1)
                    if qg == 0 and i == 2:
                        if c > 0:
                            emit_tail(c - 1)
                        emit_z(c)
                        if c < NCH - 1:
                            emit_a(c + 1)
                            ev_ldbr = emit_wbr_load()
                    if qg in (1, 2) and i % 2 == 0 and bidx < IT and c < NCH - 1:
                        emit_b_pair(bidx, ev_ldbr)
                        bidx += 1
                    if c == 0 and qg >= 1 and stage_pieces:
                        stage_pieces.pop(0)()
                    if qg == NQ - 1:
                        emit_gc(i, c)
                bq_user[qpar] = last_u
                cq_user[qpar] = evm1

            while stage_pieces:
                stage_pieces.pop(0)()
            if c < NCH - 1:
                emit_ssm(c + 1)
            pump(2)

            evlw = P.step("sync", lambda e: e.dma_start(
                out=wwin[:].rearrange("p (kt j) -> p kt j", kt=IT),
                in_=wout_b.rearrange("(kt p) j -> p kt j", p=128)),
                waits=[ev_w["wout"], wwin_guard], sem="sWW", k=16)
            ev_t3d = []
            for j in range(IT):
                bank = [p5, p6][j % 2]
                evm = None
                for kt in range(IT):
                    wl = [ev_g_new[kt]]
                    if kt == 0:
                        wl += [evlw, pg.get("p%d" % (5 + j % 2))]
                    evm = P.step("tensor", (lambda bank=bank, kt=kt, j=j: (lambda e: e.matmul(
                        bank[:, 0:TC], wwin[:, kt * DM + j * 128:kt * DM + (j + 1) * 128],
                        blk(pool_b, kt, TC), start=(kt == 0), stop=(kt == IT - 1))))(),
                        waits=wl, sem="sT" if kt == IT - 1 else None)
                wwin_guard = evm
                uqb = u_q[:, (j % 4) * TC:((j % 4) + 1) * TC]
                eve3 = P.step("scalar", (lambda bank=bank, uqb=uqb: (lambda e: e.copy(
                    out=uqb, in_=bank[:, 0:TC])))(),
                    waits=[evm, t3_uq_guard[j % 4]], sem="sS")
                evd = P.step("sync", (lambda c=c, j=j, uqb=uqb: (lambda e: e.dma_start(
                    out=wo_i[c][j * 128:(j + 1) * 128, :], in_=uqb)))(),
                    waits=[eve3], sem="sT3%d" % (j % 4), k=16)
                pg["p%d" % (5 + j % 2)] = eve3
                t3_uq_guard[j % 4] = evd
                ev_t3d.append(evd)
            st_t3_mm = wwin_guard
            ev_ar2 = P.step("gpsimd", (lambda c=c: (lambda e: e.collective_compute(
                "AllReduce", ALU.add, replica_groups=GROUPS,
                ins=[wo_i[c]], outs=[wo_o[c]])))(), waits=ev_t3d, sem="sCCW")
            pump(2)

            ev_xm = P.step("gpsimd", (lambda c=c: (lambda e: e.dma_start(
                out=pool_a[:].rearrange("p (kt t) -> p kt t", kt=KT),
                in_=wo_o[c].rearrange("(kt p) t -> p kt t", p=128))))(),
                waits=[ev_ar2, st_x1c], sem="sXM", k=16)
            ev_sq2 = []
            for kt in range(KT):
                ev_sq2.append(P.step("scalar", (lambda kt=kt: (lambda e: e.activation(
                    out=blk(pool_b, kt, TC), in_=blk(pool_a, kt, TC), func=AF.Square)))(),
                    waits=[ev_xm, st_t3_mm], sem="sS"))
            for kt in range(KT):
                P.step("tensor", mm_stats(p3, (lambda kt=kt: lambda: blk(pool_a, kt, TC))(), kt, kt == 0, kt == KT - 1),
                       waits=([ev_xm, p34_guard] if kt == 0 else []),
                       sem="sT" if kt == KT - 1 else None)
            evB2 = None
            for kt in range(KT):
                evB2 = P.step("tensor", mm_stats(p4, (lambda kt=kt: lambda: blk(pool_b, kt, TC))(), kt, kt == 0, kt == KT - 1),
                              waits=[ev_sq2[kt]], sem="sT" if kt == KT - 1 else None)
            eba2, ebb2 = ln_rows(NCH + c, TC, which=1)
            ev_xr = P.step("gpsimd", (lambda c=c: (lambda e: e.dma_start(
                out=szp[:].rearrange("p (kt t) -> p kt t", kt=KT), in_=xsrc_of(c))))(),
                waits=[ev_g_new[IT - 1]], sem="sXR", k=16)
            pump(2)

        emit_tail(NCH - 1)
        drain()
        P.step("sync", lambda e: e.nop(), waits=[("sST", P.sems["sST"][1])])

        with nc.Block() as b:
            @b.sync
            def _(e):
                P.run("sync", e)

            @b.gpsimd
            def _(e):
                P.run("gpsimd", e)

            @b.scalar
            def _(e):
                P.run("scalar", e)

            @b.vector
            def _(e):
                P.run("vector", e)

            @b.tensor
            def _(e):
                P.run("tensor", e)

    return nc


def kernel(L0_attn_x, L0_attn_W_in, L0_attn_conv_w, L0_attn_conv_b, L0_attn_W_x,
           L0_attn_A_log, L0_attn_D_param, L0_attn_dt_bias, L0_attn_W_out,
           L0_ffn_k0, L0_ffn_k1, L0_ffn_k2, L0_ffn_s0, L0_ffn_s1, L0_ffn_s2):
    x = np.asarray(L0_attn_x, dtype=np.float32)
    W_in = np.asarray(L0_attn_W_in, dtype=np.float32)
    conv_w = np.asarray(L0_attn_conv_w, dtype=np.float32)
    conv_b = np.asarray(L0_attn_conv_b, dtype=np.float32)
    W_x = np.asarray(L0_attn_W_x, dtype=np.float32)
    A_log = np.asarray(L0_attn_A_log, dtype=np.float32)
    D_param = np.asarray(L0_attn_D_param, dtype=np.float32)
    dt_bias = np.asarray(L0_attn_dt_bias, dtype=np.float32)
    W_out = np.asarray(L0_attn_W_out, dtype=np.float32)
    k0 = np.ascontiguousarray(np.asarray(L0_ffn_k0)).view(np.uint8)
    k1 = np.ascontiguousarray(np.asarray(L0_ffn_k1)).view(np.uint8)
    k2 = np.ascontiguousarray(np.asarray(L0_ffn_k2)).view(np.uint8)
    s0 = np.asarray(L0_ffn_s0, dtype=np.float32).reshape(1, 1)
    s1 = np.asarray(L0_ffn_s1, dtype=np.float32).reshape(1, 1)
    s2 = np.asarray(L0_ffn_s2, dtype=np.float32).reshape(1, 1)

    if "nc" not in _NC_CACHE:
        _NC_CACHE["nc"] = build_nc()
    nc = _NC_CACHE["nc"]
    in_maps = make_in_maps(x, W_in, conv_w, conv_b, W_x, A_log, D_param,
                           dt_bias, W_out, k0, k1, k2, s0, s1, s2)

    res = run_bass_kernel_spmd(nc, in_maps, core_ids=list(range(8)))

    out = np.empty((4, NT, DM), np.float32)
    for c in range(8):
        b, dh = c // 2, c % 2
        oT = res.results[c]["outT"]
        for k in range(4):
            ch = 2 * k + dh
            out[b, ch * TC:(ch + 1) * TC, :] = oT[:, k * TC:(k + 1) * TC].T
    return out


def make_in_maps(x, W_in, conv_w, conv_b, W_x, A_log, D_param, dt_bias, W_out,
                 k0, k1, k2, s0, s1, s2):
    in_maps = []
    for c in range(8):
        b, dh = c // 2, c % 2
        sl = slice(dh * DI, (dh + 1) * DI)
        im = dict(
            xT=np.ascontiguousarray(x[b].T),
            wbr=np.ascontiguousarray(W_in[:, dh * DI:(dh + 1) * DI]),
            wz=np.ascontiguousarray(W_in[:, 2048 + dh * DI:2048 + (dh + 1) * DI]),
            wx=np.ascontiguousarray(W_x[sl]),
            convw=np.ascontiguousarray(conv_w[sl, 0, :]),
            convb=np.ascontiguousarray(conv_b[sl].reshape(DI, 1)),
            alog=np.ascontiguousarray(A_log[sl]),
            dpar=np.ascontiguousarray(D_param[sl].reshape(DI, 1)),
            dtb=np.ascontiguousarray(dt_bias[sl].reshape(DI, 1)),
            wout=np.ascontiguousarray(W_out[sl]),
            k0t=np.ascontiguousarray(k0[:DM]),
            k0b=np.ascontiguousarray(k0[DM:2 * DM]),
            k1u=k1, k2u=k2,
            s0i=s0, s1i=s1, s2i=s2,
            m_e=np.full((128, 1), 1.0 if dh == 0 else 0.0, np.float32),
            minv_e=np.full((128, 1), 0.0 if dh == 0 else 1.0, np.float32),
            m_o=np.full((128, 1), 1.0 if dh == 1 else 0.0, np.float32),
            minv_o=np.full((128, 1), 0.0 if dh == 1 else 1.0, np.float32),
            ones_bf=np.ones((128, 1), dtype=BF),
            neghalf=np.full((1, 128), -0.5, dtype=BF),
        )
        in_maps.append(im)
    return in_maps


if __name__ == "__main__":
    build_nc()
    print("built ok")
